# version 02
# baseline (speedup 1.0000x reference)
import numpy as np
import ml_dtypes
from contextlib import ExitStack
import concourse.bass as bass
import concourse.mybir as mybir
from concourse.bass_utils import run_bass_kernel_spmd
from concourse.bass_core_rust import IndirectOffsetOnAxis

F32 = mybir.dt.float32
BF16 = mybir.dt.bfloat16
I32 = mybir.dt.int32
AF = mybir.ActivationFunctionType
ALU = mybir.AluOpType
AX = mybir.AxisListType.X

T = 4096
TH = 2048
D = 1024
NTB = TH // 128
NB = T // 128
NHEAD = 16
DH = 64


class Counter:
    def __init__(self, prog, name):
        master = prog
        while getattr(master, "parent", None) is not None:
            master = master.parent
        self.sem = master.es.enter_context(prog.nc.semaphore(name))
        self.n = 0

    def inc(self, instr, k=1):
        instr.then_inc(self.sem, k)
        self.n += k

    def wait(self, eng):
        if self.n > 0:
            eng.wait_ge(self.sem, self.n)

    def chain(self, eng, instr):
        self.inc(instr)
        eng.wait_ge(self.sem, self.n)


class Prog:
    def __init__(self, parent=None, io=None, tag=""):
        self.parent = parent
        self.io = io or {}
        self.tag = tag
        if parent is None:
            self.nc = bass.Bass("TRN2", target_bir_lowering=False, debug=False)
            self.es = ExitStack()
            self.blk = None
            self.ctrs = {}
        else:
            self.nc = parent.nc
            self.es = ExitStack()
            self.blk = parent.blk
            self.ctrs = parent.ctrs

    def inp(self, name, shape, dtype):
        if self.parent is not None:
            return self.io[name]
        return self.nc.dram_tensor(name, list(shape), dtype, kind="ExternalInput").ap()

    def out(self, name, shape, dtype):
        if self.parent is not None:
            return self.io[name]
        return self.nc.dram_tensor(name, list(shape), dtype, kind="ExternalOutput").ap()

    def dram(self, name, shape, dtype):
        return self.nc.dram_tensor(self.tag + name, list(shape), dtype, kind="Internal").ap()

    def sb(self, name, shape, dtype):
        return self.es.enter_context(self.nc.sbuf_tensor(self.tag + name, list(shape), dtype))

    def ps(self, name, shape, dtype=F32):
        return self.es.enter_context(self.nc.psum_tensor(self.tag + name, list(shape), dtype))

    def ctr(self, name):
        if name in ("cc1", "cc2", "ccg") or self.parent is None:
            if self.parent is None and name not in self.ctrs:
                self.ctrs[name] = Counter(self, name)
                return self.ctrs[name]
            return Counter(self, self.tag + name)
        if name not in self.ctrs:
            self.ctrs[name] = Counter(self, name)
        return self.ctrs[name]

    def start(self):
        if self.parent is None:
            self.blk = self.es.enter_context(self.nc.Block())

    def barrier(self):
        cs_ = [c for n, c in self.ctrs.items()]
        for fn in (self.V, self.S, self.Tn, self.G, self.Y):
            def w(e, cs_=cs_):
                for c in cs_:
                    c.wait(e)
            fn(w)

    def V(self, fn):
        self.blk.vector(fn)

    def S(self, fn):
        self.blk.scalar(fn)

    def Tn(self, fn):
        self.blk.tensor(fn)

    def G(self, fn):
        self.blk.gpsimd(fn)

    def Y(self, fn):
        self.blk.sync(fn)


def ln_emit(p, xin, xout, tmp, mc, rc, cv, cs, n=D, eps=1e-5):
    def s1(s):
        cv.wait(s)
        s.activation(out=tmp, in_=xin, func=AF.Identity, accum_out=mc)
        cs.inc(s.activation(out=tmp, in_=xin, func=AF.Square, accum_out=rc))
    p.S(s1)

    def v1(v):
        cs.wait(v)
        cv.chain(v, v.tensor_scalar(mc, mc, 1.0 / n, None, ALU.mult))
        cv.chain(v, v.tensor_scalar(rc, rc, 1.0 / n, None, ALU.mult))
        cv.chain(v, v.tensor_scalar(tmp[:, 0:1], mc, mc, None, ALU.mult))
        cv.chain(v, v.tensor_tensor(rc, rc, tmp[:, 0:1], ALU.subtract))
        cv.chain(v, v.tensor_scalar(rc, rc, 1.0, eps, ALU.mult, ALU.add))
    p.V(v1)

    def s2(s):
        cv.wait(s)
        cs.chain(s, s.activation(out=rc, in_=rc, func=AF.Ln))
        cs.inc(s.activation(out=rc, in_=rc, func=AF.Exp, scale=-0.5))
    p.S(s2)

    def v2(v):
        cs.wait(v)
        cv.chain(v, v.tensor_scalar(xout, xin, mc, None, ALU.subtract))
        cv.inc(v.tensor_scalar(xout, xout, rc, None, ALU.mult))
    p.V(v2)


def build_l0_attn(parent=None, io=None, tag="B_"):
    p = Prog(parent, io, tag)
    xb = p.inp("xb", (T, D), F32)
    WqT = p.inp("WqT", (128, DH), F32)
    WkT = p.inp("WkT", (128, DH), F32)
    WvT = p.inp("WvT", (128, DH), F32)
    WoT = p.inp("WoT", (D, D), F32)
    qposb = p.inp("qposb", (128, TH), F32)
    kposc = p.inp("kposc", (128, NB), F32)
    ident = p.inp("ident", (128, 128), F32)
    onesc = p.inp("onesc", (1, 128), F32)
    s_own = p.out("s_own", (TH, D), F32)
    xn_own = p.out("xn_own", (TH, D), F32)

    with p.es:
        xt = p.sb("xt", (128, D), F32)
        xnt = p.sb("xnt", (128, D), F32)
        tmp = p.sb("tmp", (128, D), F32)
        mc = p.sb("mc", (128, 1), F32)
        rc = p.sb("rc", (128, 1), F32)
        xnT = p.sb("xnT", (128, 8, T), BF16)
        wq_sb = p.sb("wq_sb", (128, DH), BF16)
        wk_sb = p.sb("wk_sb", (128, DH), BF16)
        wv_sb = p.sb("wv_sb", (128, DH), BF16)
        woT_sb = p.sb("woT_sb", (128, 8, D), BF16)
        id_sb = p.sb("id_sb", (128, 128), F32)
        ones_sb = p.sb("ones_sb", (1, 128), F32)
        qpb = p.sb("qpb", (128, TH), F32)
        kpc = p.sb("kpc", (128, NB), F32)
        qT = p.sb("qT", (DH, T), BF16)
        kT = p.sb("kT", (DH, T), BF16)
        vrow = p.sb("vrow", (128, NB, 66), BF16)
        red = p.sb("red", (DH, 2), F32)
        bsc = p.sb("bsc", (1, 4), F32)
        nbound = p.sb("nbound", (128, 1), F32)
        mkc = p.sb("mkc", (128, TH), BF16)
        pt = p.sb("pt", (128, TH), BF16)
        avn = p.sb("avn", (DH, TH), F32)
        sumrow = p.sb("sumrow", (1, TH), F32)
        avT = p.sb("avT", (128, 8, TH), BF16)
        srow = p.sb("srow", (128, D), F32)

        ps_a = p.ps("ps_a", (128, 128))
        ps_qk = p.ps("ps_qk", (DH, 512))
        ps_st = p.ps("ps_st", (128, 1024))
        ps_av = p.ps("ps_av", (66, TH))
        ps_c = ps_a

        cl = p.ctr("cl")
        cg = p.ctr("cg")
        cv = p.ctr("cv")
        cs = p.ctr("cs")
        ct = p.ctr("ct")
        cst = p.ctr("cst")
        p.start()

        def y0(y):
            cl.inc(y.dma_start(out=id_sb[:], in_=ident), 16)
            cl.inc(y.dma_start(out=ones_sb[:], in_=onesc), 16)
            cl.inc(y.dma_start(out=qpb[:], in_=qposb), 16)
            cl.inc(y.dma_start(out=kpc[:], in_=kposc), 16)
        p.Y(y0)

        def g0(g):
            cg.inc(g.dma_start(out=wq_sb[:], in_=WqT), 16)
            cg.inc(g.dma_start(out=wk_sb[:], in_=WkT), 16)
            cg.inc(g.dma_start(out=wv_sb[:], in_=WvT), 16)
            for c in range(8):
                cg.inc(g.dma_start(out=woT_sb[:, c, :], in_=WoT[c * 128:(c + 1) * 128, :]), 16)
        p.G(g0)

        def v_init(v):
            v.memset(vrow[:, :, 64:65], 1.0)
            cv.inc(v.memset(vrow[:, :, 65:66], 0.0))
        p.V(v_init)

        for b in range(NB):
            def yb(y, b=b):
                if b > 0:
                    ct.wait(y)
                    cst.wait(y)
                cl.inc(y.dma_start(out=xt[:], in_=xb[b * 128:(b + 1) * 128, :]), 16)
            p.Y(yb)

            def vw(v):
                cl.wait(v)
                cst.wait(v)
                cv.inc(v.nop())
            p.V(vw)

            ln_emit(p, xt[:], xnt[:], tmp[:], mc[:], rc[:], cv, cs)

            if b < NTB:
                def yo(y, b=b):
                    cv.wait(y)
                    cst.inc(y.dma_start(out=xn_own[b * 128:(b + 1) * 128, :], in_=xnt[:]), 16)
                p.Y(yo)

            for c in range(8):
                def tb_(t, b=b, c=c):
                    cv.wait(t)
                    cs.wait(t)
                    ct.inc(t.transpose(ps_a[:], xnt[:, c * 128:(c + 1) * 128], id_sb[:]))
                p.Tn(tb_)

                def se(s, b=b, c=c):
                    ct.wait(s)
                    cs.inc(s.copy(out=xnT[:, c, b * 128:(b + 1) * 128], in_=ps_a[:]))
                p.S(se)

        for h in range(NHEAD):
            hp = (h % 2) * 64
            hc = h // 2
            for wsb, dstT in ((wq_sb, qT), (wk_sb, kT)):
                for i in range(8):
                    def tm(t, wsb=wsb, hp=hp, hc=hc, i=i):
                        cs.wait(t)
                        cg.wait(t)
                        ct.inc(t.matmul(ps_qk[:, :], wsb[hp:hp + 64, :],
                                        xnT[hp:hp + 64, hc, i * 512:(i + 1) * 512],
                                        start=True, stop=True))
                    p.Tn(tm)

                    def sev(s, dstT=dstT, i=i):
                        ct.wait(s)
                        cs.inc(s.copy(out=dstT[:, i * 512:(i + 1) * 512], in_=ps_qk[:, :]))
                    p.S(sev)

            for b in range(NB):
                def tv(t, hp=hp, hc=hc, b=b):
                    cs.wait(t)
                    cv.wait(t)
                    ct.inc(t.matmul(ps_a[:, 0:64], xnT[hp:hp + 64, hc, b * 128:(b + 1) * 128],
                                    wv_sb[hp:hp + 64, :], start=True, stop=True))
                p.Tn(tv)

                def sv(s, b=b):
                    ct.wait(s)
                    cs.inc(s.copy(out=vrow[:, b, 0:64], in_=ps_a[:, 0:64]))
                p.S(sv)

            def vb(v):
                cs.wait(v)
                v.reduce_max(red[:, 0:1], qT[:], axis=AX, apply_absolute_value=True)
                cv.inc(v.reduce_max(red[:, 1:2], kT[:], axis=AX, apply_absolute_value=True))
            p.V(vb)

            def tb3(t):
                cv.wait(t)
                ct.inc(t.transpose(ps_c[0:1, 0:64], red[:, 0:1], id_sb[0:64, 0:64]))
                ct.inc(t.transpose(ps_c[0:1, 64:128], red[:, 1:2], id_sb[0:64, 0:64]))
            p.Tn(tb3)


            def vb4(v):
                if h == 0:
                    cg.wait(v)
                ct.wait(v)
                v.reduce_max(bsc[0:1, 0:1], ps_c[0:1, 0:64], axis=AX)
                cv.chain(v, v.reduce_max(bsc[0:1, 1:2], ps_c[0:1, 64:128], axis=AX))
                cv.chain(v, v.tensor_tensor(bsc[0:1, 2:3], bsc[0:1, 0:1], bsc[0:1, 1:2], ALU.mult))
                cv.inc(v.tensor_scalar(bsc[0:1, 2:3], bsc[0:1, 2:3], -8.0, None, ALU.mult))
            p.V(vb4)


            def tb5(t):
                cv.wait(t)
                cg.wait(t)
                ct.inc(t.matmul(ps_c[:, 0:1], ones_sb[:], bsc[0:1, 2:3], start=True, stop=True))
            p.Tn(tb5)

            def vb5(v):
                ct.wait(v)
                cv.inc(v.tensor_copy(nbound[:], ps_c[:, 0:1]))
            p.V(vb5)

            for kc in range(NB):
                def vm(v, kc=kc):
                    cv.wait(v)
                    ct.wait(v)
                    cv.inc(v.tensor_scalar(mkc[:], qpb[:], kpc[:, kc:kc + 1], None, ALU.is_ge))
                p.V(vm)

                for qh in range(2):
                    def ts_(t, kc=kc, qh=qh):
                        cs.wait(t)
                        for j in range(2):
                            ct.inc(t.matmul(
                                ps_st[:, j * 512:(j + 1) * 512],
                                kT[:, kc * 128:(kc + 1) * 128],
                                qT[:, qh * 1024 + j * 512:qh * 1024 + (j + 1) * 512],
                                start=True, stop=True))
                    p.Tn(ts_)

                    def sx(s, kc=kc, qh=qh):
                        ct.wait(s)
                        cv.wait(s)
                        cs.inc(s.activation(out=pt[:, qh * 1024:(qh + 1) * 1024],
                                            in_=ps_st[:, 0:1024], func=AF.Exp,
                                            bias=nbound[:, 0:1], scale=0.125))
                    p.S(sx)

                def vmask(v, kc=kc):
                    cs.wait(v)
                    cv.inc(v.tensor_tensor(pt[:], pt[:], mkc[:], ALU.mult))
                p.V(vmask)


                def tav(t, kc=kc):
                    cg.wait(t)
                    cv.wait(t)
                    for j in range(4):
                        ct.inc(t.matmul(ps_av[:, j * 512:(j + 1) * 512],
                                        vrow[:, kc, :], pt[:, j * 512:(j + 1) * 512],
                                        start=(kc == 0), stop=(kc == NB - 1)))
                p.Tn(tav)

            def sav(s, h=h):
                ct.wait(s)
                cv.wait(s)
                s.copy(out=avn[:, 0:1024], in_=ps_av[0:64, 0:1024])
                s.copy(out=avn[:, 1024:2048], in_=ps_av[0:64, 1024:2048])
                cs.inc(s.copy(out=sumrow[:], in_=ps_av[64:65, :]))
            p.S(sav)


            def vrs(v):
                cs.wait(v)
                cg.wait(v)
                cv.inc(v.reciprocal(sumrow[:], sumrow[:]))
            p.V(vrs)

            for j in range(4):
                def tbc(t, j=j):
                    cv.wait(t)
                    ct.inc(t.matmul(ps_st[0:64, 0:512], ones_sb[0:1, 0:64],
                                    sumrow[0:1, j * 512:(j + 1) * 512], start=True, stop=True))
                p.Tn(tbc)

                def vnrm(v, h=h, j=j):
                    ct.wait(v)
                    cv.inc(v.tensor_tensor(
                        avT[(h % 2) * 64:(h % 2) * 64 + 64, h // 2, j * 512:(j + 1) * 512],
                        avn[:, j * 512:(j + 1) * 512], ps_st[0:64, 0:512], ALU.mult))
                p.V(vnrm)

        for b in range(NTB):
            def tw(t, b=b):
                cv.wait(t)
                cs.wait(t)
                cg.wait(t)
                for c in range(8):
                    for j in range(2):
                        ct.inc(t.matmul(ps_st[:, j * 512:(j + 1) * 512],
                                        avT[:, c, b * 128:(b + 1) * 128],
                                        woT_sb[:, c, j * 512:(j + 1) * 512],
                                        start=(c == 0), stop=(c == 7)))
            p.Tn(tw)

            def sw(s):
                ct.wait(s)
                cs.inc(s.copy(out=srow[:], in_=ps_st[:, 0:1024]))
            p.S(sw)

            def yw(y, b=b):
                cs.wait(y)
                cst.inc(y.dma_start(out=s_own[b * 128:(b + 1) * 128, :], in_=srow[:]), 16)
            p.Y(yw)

        def yfin(y):
            cst.wait(y)
            cl.wait(y)
        p.Y(yfin)

    return p


def _core_slices():
    return [(ci // 2, ci % 2) for ci in range(8)]


def run_l0_attn(inputs):
    x = inputs["L0_attn_x"]
    p = build_l0_attn()
    ident = np.eye(128, dtype=np.float32)
    onesc = np.ones((1, 128), dtype=np.float32)
    in_maps = []
    for (b, hh) in _core_slices():
        own = x[b, hh * TH:(hh + 1) * TH, :]
        other = x[b, (1 - hh) * TH:(1 - hh) * TH + TH, :]
        xb = np.concatenate([own, other], axis=0)
        qpos = np.arange(hh * TH, (hh + 1) * TH, dtype=np.float32)
        qposb = np.tile(qpos[None, :], (128, 1))
        rot = np.concatenate([np.arange(hh * TH, (hh + 1) * TH),
                              np.arange((1 - hh) * TH, (1 - hh) * TH + TH)]).astype(np.float32)
        kposc = rot.reshape(NB, 128).T.copy()
        in_maps.append({
            "xb": np.ascontiguousarray(xb),
            "WqT": np.ascontiguousarray(np.vstack([inputs["L0_attn_Wq"].T] * 2)),
            "WkT": np.ascontiguousarray(np.vstack([inputs["L0_attn_Wk"].T] * 2)),
            "WvT": np.ascontiguousarray(np.vstack([inputs["L0_attn_Wv"].T] * 2)),
            "WoT": np.ascontiguousarray(inputs["L0_attn_Wo"].T),
            "qposb": qposb,
            "kposc": np.ascontiguousarray(kposc),
            "ident": ident,
            "onesc": onesc,
        })
    res = run_bass_kernel_spmd(p.nc, in_maps, core_ids=list(range(8)))
    s = np.zeros((4, T, D), np.float32)
    xn = np.zeros((4, T, D), np.float32)
    for ci, (b, hh) in enumerate(_core_slices()):
        s[b, hh * TH:(hh + 1) * TH] = res.results[ci]["s_own"]
        xn[b, hh * TH:(hh + 1) * TH] = res.results[ci]["xn_own"]
    return s, xn




def build_l0_ffn_sim(parent=None, io=None, tag="C_"):
    p = Prog(parent, io, tag)
    xn_own = p.inp("xn_own", (TH, D), F32)
    wqT = p.inp("wqT", (D, 8192), F32)
    gcol = p.inp("gcol", (128, 8), F32)
    keysT = p.inp("keysT", (2, 8, 4, 128, 256), F32)
    iota256 = p.inp("iota256", (128, 256), I32)
    iota16f = p.inp("iota16f", (128, 16), F32)
    ident = p.inp("ident", (128, 128), F32)
    idx_all = p.out("idx_all", (TH, 128), I32)
    sm_all = p.out("sm_all", (TH, 128), F32)
    xnf_own = p.out("xnf_own", (TH, D), F32)

    with p.es:
        xt = p.sb("xt", (128, D), F32)
        xnf = p.sb("xnf", (128, D), F32)
        gb = p.sb("gb", (128, D), F32)
        rn = p.sb("rn", (128, NTB), F32)
        ss = p.sb("ss", (128, 1), F32)
        xnT = p.sb("xnT", (128, 8, TH), BF16)
        id_sb = p.sb("id_sb", (128, 128), F32)
        wqc = p.sb("wqc", (128, 8, 128), BF16)
        gcol_sb = p.sb("gcol_sb", (128, 8), F32)
        qTg = p.sb("qTg", (128, 4, TH), BF16)
        keys_sb = p.sb("keys_sb", (128, 4, 256), BF16)
        simH = p.sb("simH", (128, NTB, 2, 256), F32)
        idxA = p.sb("idxA", (128, NTB, 128), I32)
        smA = p.sb("smA", (128, NTB, 128), F32)
        io256 = p.sb("io256", (128, 256), I32)
        io16f = p.sb("io16f", (128, 16), F32)
        wa = p.sb("wa", (128, 256), I32)
        wmr = p.sb("wmr", (128, 256), F32)
        sp = p.sb("sp", (128, 2, 16), F32)
        ipf = p.sb("ipf", (128, 2, 16), F32)
        cs16 = p.sb("cs16", (128, 16), F32)
        pk_i = p.sb("pk_i", (128, 16), I32)
        hv = p.sb("hv", (128, 2, 16), F32)
        mm1 = p.sb("mm1", (128, 256), F32)
        sel = p.sb("sel", (128, 2, 16), F32)
        eid = p.sb("eid", (128, 16), F32)
        nmx = p.sb("nmx", (128, 1), F32)
        ssum = p.sb("ssum", (128, 1), F32)

        ps_tr = p.ps("ps_tr", (128, 128))
        ps_q = p.ps("ps_q", (128, 2048))
        ps_sim = p.ps("ps_sim", (128, 256))

        cl = p.ctr("cl")
        cg = p.ctr("cg")
        cv = p.ctr("cv")
        cs = p.ctr("cs")
        ct = p.ctr("ct")
        cst = p.ctr("cst")
        p.start()

        def y0(y):
            cl.inc(y.dma_start(out=id_sb[:], in_=ident), 16)
            cl.inc(y.dma_start(out=io256[:], in_=iota256), 16)
            cl.inc(y.dma_start(out=io16f[:], in_=iota16f), 16)
            cl.inc(y.dma_start(out=gcol_sb[:], in_=gcol), 16)
        p.Y(y0)

        def g0(g):
            for half in range(2):
                for h in range(8):
                    cg.inc(g.dma_start(out=keys_sb[:, :, :] if False else keys_sb[:],
                                       in_=keysT[half, h]), 16)
        p.G(lambda g: None)

        gb_in = p.inp("gb_in", (128, D), F32)

        def y1(y):
            cl.inc(y.dma_start(out=gb[:], in_=gb_in), 16)
        p.Y(y1)

        for b in range(NTB):
            def yb(y, b=b):
                if b > 0:
                    ct.wait(y)
                    cst.wait(y)
                cl.inc(y.dma_start(out=xt[:], in_=xb_slice(xn_own, b)), 16)
            p.Y(yb)

            def sA(s, b=b):
                cl.wait(s)
                cv.wait(s)
                cst.wait(s)
                cs.inc(s.activation(out=xnf[:], in_=xt[:], func=AF.Square, accum_out=ss[:]))
            p.S(sA)

            def vA(v, b=b):
                cs.wait(v)
                cv.chain(v, v.tensor_scalar_max(ss[:], ss[:], 1e-24))
                cv.inc(v.nop())
            p.V(vA)

            def sB(s, b=b):
                cv.wait(s)
                cs.chain(s, s.activation(out=ss[:], in_=ss[:], func=AF.Ln))
                cs.inc(s.activation(out=rn[:, b:b + 1], in_=ss[:], func=AF.Exp, scale=-0.5))
            p.S(sB)

            def vB(v, b=b):
                cs.wait(v)
                cv.chain(v, v.tensor_tensor(xnf[:], xt[:], gb[:], ALU.mult))
                cv.chain(v, v.tensor_scalar(rn[:, b:b + 1], rn[:, b:b + 1], 32.0, None, ALU.mult))
                cv.inc(v.tensor_scalar(xnf[:], xnf[:], rn[:, b:b + 1], None, ALU.mult))
            p.V(vB)

            def yo(y, b=b):
                cv.wait(y)
                cst.inc(y.dma_start(out=xnf_own[b * 128:(b + 1) * 128, :], in_=xnf[:]), 16)
            p.Y(yo)

            for c in range(8):
                def tb_(t, b=b, c=c):
                    cv.wait(t)
                    cs.wait(t)
                    ct.inc(t.transpose(ps_tr[:], xt[:, c * 128:(c + 1) * 128], id_sb[:]))
                p.Tn(tb_)

                def se(s, b=b, c=c):
                    ct.wait(s)
                    cs.inc(s.copy(out=xnT[:, c, b * 128:(b + 1) * 128], in_=ps_tr[:]))
                p.S(se)

        for h in range(8):
            for half in range(2):
                def gk(g, half=half, h=h):
                    ct.wait(g)
                    cg.inc(g.dma_start(out=keys_sb[:], in_=keysT[half, h].rearrange("i p n -> p i n")), 16)
                p.G(gk)

                for i in range(4):
                    cc = half * 32 + h * 4 + i
                    def gw(g, cc=cc):
                        ct.wait(g)
                        cg.inc(g.dma_start(
                            out=wqc[:],
                            in_=wqT[:, cc * 128:(cc + 1) * 128].rearrange("(c p) n -> p c n", p=128)), 16)
                    p.G(gw)

                    def vg(v):
                        cg.wait(v)
                        for c in range(8):
                            v.tensor_scalar(wqc[:, c, :], wqc[:, c, :], gcol_sb[:, c:c + 1], None, ALU.mult)
                        cv.inc(v.nop())
                    p.V(vg)

                    def tq(t, i=i):
                        cv.wait(t)
                        cs.wait(t)
                        for c in range(8):
                            for j in range(4):
                                ct.inc(t.matmul(ps_q[:, j * 512:(j + 1) * 512],
                                                wqc[:, c, :], xnT[:, c, j * 512:(j + 1) * 512],
                                                start=(c == 0), stop=(c == 7)))
                    p.Tn(tq)

                    def sq(s, i=i):
                        ct.wait(s)
                        s.copy(out=qTg[:, i, 0:1024], in_=ps_q[:, 0:1024])
                        cs.inc(s.copy(out=qTg[:, i, 1024:2048], in_=ps_q[:, 1024:2048]))
                    p.S(sq)

                for b in range(NTB):
                    def tsim(t, b=b):
                        cs.wait(t)
                        cg.wait(t)
                        for i in range(4):
                            ct.inc(t.matmul(ps_sim[:], qTg[:, i, b * 128:(b + 1) * 128],
                                            keys_sb[:, i, :], start=(i == 0), stop=(i == 3)))
                    p.Tn(tsim)

                    def ssim(s, half=half, b=b):
                        ct.wait(s)
                        cv.wait(s)
                        cg.wait(s)
                        cs.inc(s.activation(out=simH[:, b, half, :], in_=ps_sim[:],
                                            func=AF.Identity, scale=rn[:, b:b + 1]))
                    p.S(ssim)

            for b in range(NTB):
                def vt(v, b=b, h=h):
                    cs.wait(v)
                    for half in range(2):
                        src = simH[:, b, half, :]
                        cv.chain(v, v.tensor_scalar(wa[:], src.bitcast(I32), -256, None, ALU.bitwise_and))
                        cv.chain(v, v.tensor_tensor(wa[:], wa[:], io256[:], ALU.bitwise_or))
                        cv.chain(v, v.max(out=sp[:, half, 0:8], in_=wa[:].bitcast(F32)))
                        cv.chain(v, v.match_replace(out=wmr[:], in_to_replace=sp[:, half, 0:8],
                                                    in_values=wa[:].bitcast(F32), imm_value=-3e38))
                        cv.chain(v, v.max(out=sp[:, half, 8:16], in_=wmr[:]))
                        cv.chain(v, v.tensor_scalar(ipf[:, half, :].bitcast(I32),
                                                    sp[:, half, :].bitcast(I32), 255, None, ALU.bitwise_and))
                        cv.chain(v, v.tensor_copy(ipf[:, half, :], ipf[:, half, :].bitcast(I32)))
                    cv.chain(v, v.tensor_tensor(
                        wmr[:].rearrange("p (a c) -> p a c", c=16),
                        sp[:, 0, :].unsqueeze(2).to_broadcast([128, 16, 16]),
                        sp[:, 1, :].unsqueeze(1).to_broadcast([128, 16, 16]),
                        ALU.add))
                    cv.chain(v, v.tensor_scalar(wa[:], wmr[:].bitcast(I32), -256, None, ALU.bitwise_and))
                    cv.chain(v, v.tensor_tensor(wa[:], wa[:], io256[:], ALU.bitwise_or))
                    cv.chain(v, v.max(out=cs16[:, 0:8], in_=wa[:].bitcast(F32)))
                    cv.chain(v, v.match_replace(out=wmr[:], in_to_replace=cs16[:, 0:8],
                                                in_values=wa[:].bitcast(F32), imm_value=-3e38))
                    cv.chain(v, v.max(out=cs16[:, 8:16], in_=wmr[:]))
                    cv.chain(v, v.tensor_scalar(pk_i[:], cs16[:].bitcast(I32), 255, None, ALU.bitwise_and))
                    cv.chain(v, v.tensor_scalar(hv[:, 0, :].bitcast(I32), pk_i[:], 4, None, ALU.arith_shift_right))
                    cv.chain(v, v.tensor_copy(hv[:, 0, :], hv[:, 0, :].bitcast(I32)))
                    cv.chain(v, v.tensor_scalar(hv[:, 1, :].bitcast(I32), pk_i[:], 15, None, ALU.bitwise_and))
                    cv.chain(v, v.tensor_copy(hv[:, 1, :], hv[:, 1, :].bitcast(I32)))
                    for g_ in range(2):
                        cv.chain(v, v.tensor_tensor(
                            mm1[:].rearrange("p (a c) -> p a c", c=16),
                            hv[:, g_, :].unsqueeze(2).to_broadcast([128, 16, 16]),
                            io16f[:].unsqueeze(1).to_broadcast([128, 16, 16]),
                            ALU.is_equal))
                        cv.chain(v, v.tensor_tensor(
                            mm1[:].rearrange("p (a c) -> p a c", c=16),
                            mm1[:].rearrange("p (a c) -> p a c", c=16),
                            ipf[:, g_, :].unsqueeze(1).to_broadcast([128, 16, 16]),
                            ALU.mult))
                        cv.chain(v, v.reduce_sum(sel[:, g_, :], mm1[:].rearrange("p (a c) -> p a c", c=16), axis=AX))
                    cv.chain(v, v.tensor_scalar(eid[:], sel[:, 0, :], 256.0, None, ALU.mult))
                    cv.chain(v, v.tensor_tensor(eid[:], eid[:], sel[:, 1, :], ALU.add))
                    cv.chain(v, v.tensor_copy(idxA[:, b, h * 16:(h + 1) * 16], eid[:]))
                    cv.inc(v.reduce_max(nmx[:], cs16[:], axis=AX, negate=True))
                p.V(vt)

                def st_(s, b=b, h=h):
                    cv.wait(s)
                    cs.inc(s.activation(out=smA[:, b, h * 16:(h + 1) * 16], in_=cs16[:],
                                        func=AF.Exp, bias=nmx[:], accum_out=ssum[:]))
                p.S(st_)

                def vt2(v, b=b, h=h):
                    cs.wait(v)
                    cv.chain(v, v.reciprocal(ssum[:], ssum[:]))
                    cv.inc(v.tensor_scalar(smA[:, b, h * 16:(h + 1) * 16], smA[:, b, h * 16:(h + 1) * 16],
                                           ssum[:], None, ALU.mult))
                p.V(vt2)

        for b in range(NTB):
            def yidx(y, b=b):
                cv.wait(y)
                cst.inc(y.dma_start(out=idx_all[b * 128:(b + 1) * 128, :], in_=idxA[:, b, :]), 16)
                cst.inc(y.dma_start(out=sm_all[b * 128:(b + 1) * 128, :], in_=smA[:, b, :]), 16)
            p.Y(yidx)

        def yfin(y):
            cst.wait(y)
            cl.wait(y)
            cg.wait(y)
        p.Y(yfin)

    return p


def xb_slice(ap, b):
    return ap[b * 128:(b + 1) * 128, :]


def run_l0_ffn_sim(inputs, xn):
    p = build_l0_ffn_sim()
    keys = inputs["L0_ffn_keys"]
    keysT = np.zeros((2, 8, 4, 128, 256), np.float32)
    for half in range(2):
        for h in range(8):
            for i in range(4):
                keysT[half, h, i] = keys[h, :, half, i * 128:(i + 1) * 128].T
    gamma = inputs["L0_ffn_gamma"]
    gcol = gamma.reshape(8, 128).T.copy()
    gb_in = np.tile(gamma[None, :], (128, 1)).astype(np.float32)
    iota256 = np.tile(np.arange(256, dtype=np.int32)[None, :], (128, 1))
    iota16f = np.tile(np.arange(16, dtype=np.float32)[None, :], (128, 1))
    ident = np.eye(128, dtype=np.float32)
    in_maps = []
    for (b, hh) in _core_slices():
        in_maps.append({
            "xn_own": np.ascontiguousarray(xn[b, hh * TH:(hh + 1) * TH]),
            "wqT": np.ascontiguousarray(inputs["L0_ffn_wq"].T),
            "gcol": np.ascontiguousarray(gcol),
            "gb_in": gb_in,
            "keysT": keysT,
            "iota256": iota256,
            "iota16f": iota16f,
            "ident": ident,
        })
    res = run_bass_kernel_spmd(p.nc, in_maps, core_ids=list(range(8)))
    idx = np.zeros((4, T, 128), np.int32)
    sm = np.zeros((4, T, 128), np.float32)
    xnf = np.zeros((4, T, D), np.float32)
    for ci, (b, hh) in enumerate(_core_slices()):
        idx[b, hh * TH:(hh + 1) * TH] = res.results[ci]["idx_all"]
        sm[b, hh * TH:(hh + 1) * TH] = res.results[ci]["sm_all"]
        xnf[b, hh * TH:(hh + 1) * TH] = res.results[ci]["xnf_own"]
    return idx, sm, xnf




def build_l0_ffn_gather(parent=None, io=None, tag="D_"):
    p = Prog(parent, io, tag)
    idx_in = p.inp("idx_in", (TH, 128), I32)
    sm_in = p.inp("sm_in", (TH, 128), F32)
    xnf_in = p.inp("xnf_in", (TH, D), F32)
    s_in = p.inp("s_in", (TH, D), F32)
    x_in = p.inp("x_in", (TH, D), F32)
    tdn_sl = p.inp("tdn_sl", (8192, D), F32)
    tup_sl = p.inp("tup_sl", (8192, D), F32)
    ident = p.inp("ident", (128, 128), F32)
    x1_own = p.out("x1_own", (TH, D), F32)

    tdn_bf = p.dram("tdn_bf", (8192, D), BF16)
    tup_bf = p.dram("tup_bf", (8192, D), BF16)
    tdn_all = p.dram("tdn_all", (65536, D), BF16)
    tup_all = p.dram("tup_all", (65536, D), BF16)

    NGB = 4
    with p.es:
        id_sb = p.sb("id_sb", (128, 128), F32)
        idxt = p.sb("idxt", (128, 128), I32)
        idxf = p.sb("idxf", (128, 128), F32)
        idxT = p.sb("idxT", (128, 128), I32)
        smt = p.sb("smt", (128, 128), F32)
        xnf_bf = p.sb("xnf_bf", (128, D), BF16)
        xrow = p.sb("xrow", (128, D), F32)
        srw = p.sb("srw", (128, D), F32)
        x1t = p.sb("x1t", (128, D), F32)
        gdb = [p.sb(f"gdb{k}", (128, D), BF16) for k in range(NGB)]
        gub = [p.sb(f"gub{k}", (128, D), BF16) for k in range(NGB)]
        junk = p.sb("junk", (128, D), F32)
        htile = p.sb("htile", (128, 128), F32)
        wg = p.sb("wg", (128, 128), F32)
        wgT = p.sb("wgT", (128, 128), BF16)
        zb = [p.sb(f"zb{k}", (128, 64), BF16) for k in range(64)]
        gout = p.sb("gout", (128, D), F32)

        ps_tr = p.ps("ps_tr", (128, 128))
        ps_u0 = p.ps("ps_u0", (64, 512))
        ps_u1 = p.ps("ps_u1", (64, 512))

        cl = p.ctr("cl")
        cg = p.ctr("cg")
        cv = p.ctr("cv")
        cs = p.ctr("cs")
        ct = p.ctr("ct")
        cst = p.ctr("cst")
        cc1 = p.ctr("cc1")
        cc2 = p.ctr("cc2")
        p.start()

        def gtab(g):
            cg.inc(g.dma_start(out=tdn_bf, in_=tdn_sl), 16)
            cg.inc(g.dma_start(out=tup_bf, in_=tup_sl), 16)
            cg.wait(g)
            g.collective_compute("AllGather", ALU.bypass,
                                 replica_groups=[list(range(8))],
                                 ins=[tdn_bf], outs=[tdn_all]).then_inc(cc1.sem, 1)
            g.collective_compute("AllGather", ALU.bypass,
                                 replica_groups=[list(range(8))],
                                 ins=[tup_bf], outs=[tup_all]).then_inc(cc2.sem, 1)
            g.wait_ge(cc1.sem, 1)
            g.wait_ge(cc2.sem, 1)
            cg.inc(g.nop())
        p.G(gtab)
        TABLES_READY = cg.n

        def y0(y):
            cl.inc(y.dma_start(out=id_sb[:], in_=ident), 16)
        p.Y(y0)

        def vz(v):
            for k in range(64):
                v.memset(zb[k][:], 0.0)
            cv.inc(v.nop())
        p.V(vz)

        for b in range(NTB):
            def yb(y, b=b):
                if b > 0:
                    cv.wait(y)
                    ct.wait(y)
                    cst.wait(y)
                    cg.wait(y)
                cl.inc(y.dma_start(out=idxt[:], in_=idx_in[b * 128:(b + 1) * 128, :]), 16)
                cl.inc(y.dma_start(out=smt[:], in_=sm_in[b * 128:(b + 1) * 128, :]), 16)
                cl.inc(y.dma_start(out=xrow[:], in_=x_in[b * 128:(b + 1) * 128, :]), 16)
                cl.inc(y.dma_start(out=srw[:], in_=s_in[b * 128:(b + 1) * 128, :]), 16)
            p.Y(yb)

            def gx(g, b=b):
                if b > 0:
                    cv.wait(g)
                cg.inc(g.dma_start(out=xnf_bf[:], in_=xnf_in[b * 128:(b + 1) * 128, :]), 16)
            p.G(gx)

            def vi(v, b=b):
                cl.wait(v)
                cv.inc(v.tensor_copy(idxf[:], idxt[:]))
            p.V(vi)

            def ti(t, b=b):
                cv.wait(t)
                cs.wait(t)
                ct.inc(t.transpose(ps_tr[:], idxf[:], id_sb[:]))
            p.Tn(ti)

            def si(s, b=b):
                ct.wait(s)
                cs.inc(s.copy(out=idxT[:].bitcast(F32) if False else idxT[:], in_=ps_tr[:]))
            p.S(si)

            base_g = cg.n
            for s_ in range(128):
                def gd(g, b=b, s_=s_, base_g=base_g):
                    cl.wait(g)
                    if s_ >= NGB:
                        g.wait_ge(cv.sem, VTRACK[0] - (NGB - 1))
                    cg.inc(g.indirect_dma_start(gdb[s_ % NGB][:], None, tdn_all,
                                                IndirectOffsetOnAxis(idxt[:, s_:s_ + 1], 0)), 16)
                p.G(gd)

                def vd_(v, b=b, s_=s_, base_g=base_g):
                    v.wait_ge(cg.sem, base_g + (s_ + 1) * 16)
                    if s_ == 0:
                        cg.wait(v)
                    cv.inc(v.scalar_tensor_tensor(junk[:], gdb[s_ % NGB][:], 1.0, xnf_bf[:],
                                                  ALU.mult, ALU.mult,
                                                  accum_out=htile[:, s_:s_ + 1]))
                    VTRACK[0] = cv.n
                p.V(vd_)

            def sg(s, b=b):
                cv.wait(s)
                cs.inc(s.activation(out=wg[:], in_=htile[:], func=AF.Gelu))
            p.S(sg)

            def vw_(v, b=b):
                cs.wait(v)
                cv.inc(v.tensor_tensor(wg[:], wg[:], smt[:], ALU.mult))
            p.V(vw_)

            def tw_(t, b=b):
                cv.wait(t)
                cs.wait(t)
                ct.inc(t.transpose(ps_tr[:], wg[:], id_sb[:]))
            p.Tn(tw_)

            def sw_(s, b=b):
                ct.wait(s)
                cs.inc(s.copy(out=wgT[:], in_=ps_tr[:]))
            p.S(sw_)

            base_u = cg.n
            for j in range(128):
                def gu_(g, b=b, j=j, base_u=base_u):
                    cs.wait(g) if j == 0 else None
                    if j >= NGB:
                        g.wait_ge(ct.sem, TTRACK[0] - 2 * (NGB - 1))
                    cg.inc(g.indirect_dma_start(gub[j % NGB][:], None, tup_all,
                                                IndirectOffsetOnAxis(idxT[:, j:j + 1], 0)), 16)
                p.G(gu_)

                def vzc(v, b=b, j=j):
                    if j == 0:
                        cs.wait(v)
                    if j >= 64:
                        v.wait_ge(ct.sem, TTRACK[0])
                    cv.inc(v.tensor_copy(zb[j % 64][:, (j % 64):(j % 64) + 1],
                                         wgT[:, j:j + 1]))
                p.V(vzc)

                def tu(t, b=b, j=j, base_u=base_u):
                    t.wait_ge(cg.sem, base_u + (j + 1) * 16)
                    cv.wait(t)
                    if j % 64 == 0:
                        cs.wait(t)
                    half_idx = j % 64
                    ct.inc(t.matmul(ps_u0[:], zb[j % 64][:], gub[j % NGB][:, 0:512],
                                    start=(half_idx == 0), stop=(half_idx == 63)))
                    ct.inc(t.matmul(ps_u1[:], zb[j % 64][:], gub[j % NGB][:, 512:1024],
                                    start=(half_idx == 0), stop=(half_idx == 63)))
                    TTRACK[0] = ct.n
                p.Tn(tu)

                if j % 64 == 63:
                    grp = j // 64
                    def su(s, b=b, grp=grp):
                        ct.wait(s)
                        if grp == 0:
                            cv.wait(s)
                        r0 = grp * 64
                        s.activation(out=gout[r0:r0 + 64, 0:512], in_=ps_u0[:], func=AF.Silu)
                        cs.inc(s.activation(out=gout[r0:r0 + 64, 512:1024], in_=ps_u1[:], func=AF.Silu))
                    p.S(su)

            def vx(v, b=b):
                cs.wait(v)
                cv.chain(v, v.tensor_tensor(x1t[:], srw[:], gout[:], ALU.mult))
                cv.inc(v.tensor_tensor(x1t[:], x1t[:], xrow[:], ALU.add))
            p.V(vx)

            def yx(y, b=b):
                cv.wait(y)
                cst.inc(y.dma_start(out=x1_own[b * 128:(b + 1) * 128, :], in_=x1t[:]), 16)
            p.Y(yx)

        def yfin(y):
            cst.wait(y)
            cl.wait(y)
            cg.wait(y)
        p.Y(yfin)

    return p


VTRACK = [0]
TTRACK = [0]


def run_l0_ffn_gather(inputs, idx, sm, xnf, s0, x):
    p = build_l0_ffn_gather()
    ident = np.eye(128, dtype=np.float32)
    in_maps = []
    for ci, (b, hh) in enumerate(_core_slices()):
        in_maps.append({
            "idx_in": np.ascontiguousarray(idx[b, hh * TH:(hh + 1) * TH]),
            "sm_in": np.ascontiguousarray(sm[b, hh * TH:(hh + 1) * TH]),
            "xnf_in": np.ascontiguousarray(xnf[b, hh * TH:(hh + 1) * TH]),
            "s_in": np.ascontiguousarray(s0[b, hh * TH:(hh + 1) * TH]),
            "x_in": np.ascontiguousarray(x[b, hh * TH:(hh + 1) * TH]),
            "tdn_sl": np.ascontiguousarray(inputs["L0_ffn_table_down"][ci * 8192:(ci + 1) * 8192]),
            "tup_sl": np.ascontiguousarray(inputs["L0_ffn_table_up"][ci * 8192:(ci + 1) * 8192]),
            "ident": ident,
        })
    res = run_bass_kernel_spmd(p.nc, in_maps, core_ids=list(range(8)))
    x1 = np.zeros((4, T, D), np.float32)
    for ci, (b, hh) in enumerate(_core_slices()):
        x1[b, hh * TH:(hh + 1) * TH] = res.results[ci]["x1_own"]
    return x1




def build_rwkv(parent=None, io=None, tag="E_"):
    p = Prog(parent, io, tag)
    x1_in = p.inp("x1_in", (TH, D), F32)
    x1p_in = p.inp("x1p_in", (TH, D), F32)
    WkTc = p.inp("WkTc", (D, D), F32)
    WvTc = p.inp("WvTc", (D, D), F32)
    WrTc = p.inp("WrTc", (D, D), F32)
    WoTc = p.inp("WoTc", (D, D), F32)
    mixk_b = p.inp("mixk_b", (128, D), F32)
    mixv_b = p.inp("mixv_b", (128, D), F32)
    mixr_b = p.inp("mixr_b", (128, D), F32)
    td_b = p.inp("td_b", (128, D), F32)
    tf_b = p.inp("tf_b", (128, D), F32)
    jcol = p.inp("jcol", (128, 1), F32)
    tprefix = p.inp("tprefix", (128, 128), F32)
    rolem = p.inp("rolem", (1, D), F32)
    ident = p.inp("ident", (128, 128), F32)
    s1_own = p.out("s1_own", (TH, D), F32)

    sp_n = p.dram("sp_n", (TH, D), F32)
    sp_d = p.dram("sp_d", (TH, D), F32)
    sp_a = p.dram("sp_a", (TH, D), F32)
    sp_b = p.dram("sp_b", (TH, D), F32)
    ag_in = p.dram("ag_in", (2, D), F32)
    ag_out = p.dram("ag_out", (4, D), F32)

    with p.es:
        id_sb = p.sb("id_sb", (128, 128), F32)
        tpre = p.sb("tpre", (128, 128), F32)
        xa = p.sb("xa", (128, D), F32)
        xb_ = p.sb("xb_", (128, D), F32)
        tmp = p.sb("tmp", (128, D), F32)
        tmp2 = p.sb("tmp2", (128, D), F32)
        mc = p.sb("mc", (128, 1), F32)
        rc = p.sb("rc", (128, 1), F32)
        mixk = p.sb("mixk", (128, D), F32)
        mixv = p.sb("mixv", (128, D), F32)
        mixr = p.sb("mixr", (128, D), F32)
        wtile = p.sb("wtile", (128, D), F32)
        utile = p.sb("utile", (128, D), F32)
        lj = p.sb("lj", (128, D), F32)
        ljm = p.sb("ljm", (128, D), F32)
        l128 = p.sb("l128", (1, D), F32)
        l127 = p.sb("l127", (1, D), F32)
        jc = p.sb("jc", (128, 1), F32)
        wk_sb = p.sb("wk_sb", (128, 8, D), BF16)
        wv_sb = p.sb("wv_sb", (128, 8, D), BF16)
        wr_sb = p.sb("wr_sb", (128, 8, D), BF16)
        wo_sb = p.sb("wo_sb", (128, 8, D), BF16)
        kinT = p.sb("kinT", (128, 8, 128), BF16)
        krow = p.sb("krow", (128, D), F32)
        vrow_ = p.sb("vrow_", (128, D), F32)
        rbuf = p.sb("rbuf", (128, NTB, D), BF16)
        alpha = p.sb("alpha", (128, D), F32)
        beta = p.sb("beta", (128, D), F32)
        sn = p.sb("sn", (128, D), F32)
        sd_ = p.sb("sd_", (128, D), F32)
        rrowst = p.sb("rrowst", (1, D), F32)
        rrowst2 = p.sb("rrowst2", (1, D), F32)
        initr = p.sb("initr", (1, D), F32)
        initr2 = p.sb("initr2", (1, D), F32)
        rolesb = p.sb("rolesb", (1, D), F32)
        ones_cst = p.sb("ones_cst", (1, 128), F32)
        rw = p.sb("rw", (128, D), F32)
        rwT = p.sb("rwT", (128, 8, 128), BF16)
        srow_ = p.sb("srow_", (128, D), F32)

        ps_tr = p.ps("ps_tr", (128, 128))
        ps_kv = p.ps("ps_kv", (128, 1024))
        ps_p1 = p.ps("ps_p1", (128, 1024))
        ps_p2 = p.ps("ps_p2", (128, 1024))

        cl = p.ctr("cl")
        cg = p.ctr("cg")
        cv = p.ctr("cv")
        cs = p.ctr("cs")
        ct = p.ctr("ct")
        cst = p.ctr("cst")
        ccg = p.ctr("ccg")
        p.start()

        def y0(y):
            cl.inc(y.dma_start(out=id_sb[:], in_=ident), 16)
            cl.inc(y.dma_start(out=tpre[:], in_=tprefix), 16)
            cl.inc(y.dma_start(out=mixk[:], in_=mixk_b), 16)
            cl.inc(y.dma_start(out=mixv[:], in_=mixv_b), 16)
            cl.inc(y.dma_start(out=mixr[:], in_=mixr_b), 16)
            cl.inc(y.dma_start(out=wtile[:], in_=td_b), 16)
            cl.inc(y.dma_start(out=utile[:], in_=tf_b), 16)
            cl.inc(y.dma_start(out=jc[:], in_=jcol), 16)
            cl.inc(y.dma_start(out=rolesb[:], in_=rolem), 16)
        p.Y(y0)

        def g0(g):
            for c in range(8):
                cg.inc(g.dma_start(out=wk_sb[:, c, :], in_=WkTc[c * 128:(c + 1) * 128, :]), 16)
                cg.inc(g.dma_start(out=wv_sb[:, c, :], in_=WvTc[c * 128:(c + 1) * 128, :]), 16)
                cg.inc(g.dma_start(out=wr_sb[:, c, :], in_=WrTc[c * 128:(c + 1) * 128, :]), 16)
                cg.inc(g.dma_start(out=wo_sb[:, c, :], in_=WoTc[c * 128:(c + 1) * 128, :]), 16)
        p.G(g0)

        def s0_(s):
            cl.wait(s)
            cs.inc(s.activation(out=wtile[:], in_=wtile[:], func=AF.Exp))
        p.S(s0_)

        def v0(v):
            cs.wait(v)
            cv.chain(v, v.tensor_scalar(wtile[:], wtile[:], -1.0, None, ALU.mult))
            cv.chain(v, v.tensor_scalar(lj[:], wtile[:], jc[:], None, ALU.mult))
            cv.chain(v, v.tensor_scalar(tmp[:, 0:1], jc[:], -1.0, 1.0, ALU.mult, ALU.add))
            cv.chain(v, v.tensor_scalar(tmp[:, 0:1], jc[:], 1.0, -1.0, ALU.mult, ALU.add))
            cv.inc(v.tensor_scalar(ljm[:], wtile[:], tmp[:, 0:1], None, ALU.mult))
        p.V(v0)

        def s1_(s):
            cv.wait(s)
            s.activation(out=lj[:], in_=lj[:], func=AF.Exp)
            cs.chain(s, s.activation(out=ljm[:], in_=ljm[:], func=AF.Exp))
            cs.chain(s, s.activation(out=l128[:], in_=wtile[0:1, :], func=AF.Exp, scale=128.0))
            cs.chain(s, s.activation(out=l127[:], in_=wtile[0:1, :], func=AF.Exp, scale=127.0))
            cs.inc(s.activation(out=utile[:], in_=utile[:], func=AF.Exp))
        p.S(s1_)

        def v1(v):
            cs.wait(v)
            v.memset(rrowst[:], 0.0)
            v.memset(rrowst2[:], 0.0)
            cv.inc(v.memset(ones_cst[:], 1.0))
        p.V(v1)

        for c in range(NTB):
            def yb(y, c=c):
                if c > 0:
                    cv.wait(y)
                    ct.wait(y)
                cl.inc(y.dma_start(out=xa[:], in_=x1_in[c * 128:(c + 1) * 128, :]), 16)
                cl.inc(y.dma_start(out=xb_[:], in_=x1p_in[c * 128:(c + 1) * 128, :]), 16)
            p.Y(yb)

            def vw_(v, c=c):
                cl.wait(v)
                cv.inc(v.nop())
            p.V(vw_)
            ln_emit(p, xa[:], xa[:], tmp[:], mc[:], rc[:], cv, cs)
            ln_emit(p, xb_[:], xb_[:], tmp[:], mc[:], rc[:], cv, cs)

            for which, mx, wsb in (("r", mixr, wr_sb), ("k", mixk, wk_sb), ("v", mixv, wv_sb)):
                def vmix(v, which=which, mx=mx, c=c):
                    cs.wait(v)
                    cv.chain(v, v.tensor_tensor(tmp[:], xa[:], xb_[:], ALU.subtract))
                    cv.chain(v, v.tensor_tensor(tmp[:], tmp[:], mx[:], ALU.mult))
                    cv.inc(v.tensor_tensor(tmp[:], tmp[:], xb_[:], ALU.add))
                p.V(vmix)

                for cc in range(8):
                    def ttr(t, cc=cc, which=which):
                        cv.wait(t)
                        cs.wait(t)
                        ct.inc(t.transpose(ps_tr[:], tmp[:, cc * 128:(cc + 1) * 128], id_sb[:]))
                    p.Tn(ttr)

                    def str_(s, cc=cc):
                        ct.wait(s)
                        cs.inc(s.copy(out=kinT[:, cc, :], in_=ps_tr[:]))
                    p.S(str_)

                def tmm(t, which=which, c=c):
                    cs.wait(t)
                    cg.wait(t)
                    wsb_ = {"r": wr_sb, "k": wk_sb, "v": wv_sb}[which]
                    for cc in range(8):
                        for j in range(2):
                            ct.inc(t.matmul(ps_kv[:, j * 512:(j + 1) * 512],
                                            kinT[:, cc, :], wsb_[:, cc, j * 512:(j + 1) * 512],
                                            start=(cc == 0), stop=(cc == 7)))
                p.Tn(tmm)

                if which == "r":
                    def sre(s, c=c):
                        ct.wait(s)
                        cs.inc(s.copy(out=rbuf[:, c, :], in_=ps_kv[:, 0:1024]))
                    p.S(sre)
                elif which == "k":
                    def ske(s, c=c):
                        ct.wait(s)
                        cs.inc(s.activation(out=krow[:], in_=ps_kv[:, 0:1024], func=AF.Exp))
                    p.S(ske)
                else:
                    def sve(s, c=c):
                        ct.wait(s)
                        cs.inc(s.copy(out=vrow_[:], in_=ps_kv[:, 0:1024]))
                    p.S(sve)

            def va(v, c=c):
                cs.wait(v)
                cv.chain(v, v.reciprocal(tmp2[:], lj[:]))
                cv.chain(v, v.tensor_tensor(beta[:], krow[:], tmp2[:], ALU.mult))
                cv.chain(v, v.tensor_tensor(alpha[:], beta[:], vrow_[:], ALU.mult))
                cv.inc(v.nop())
            p.V(va)

            def gsp(g, c=c):
                cv.wait(g)
                cg.inc(g.dma_start(out=sp_b[c * 128:(c + 1) * 128, :], in_=krow[:]), 16)
                cg.inc(g.dma_start(out=sp_a[c * 128:(c + 1) * 128, :], in_=tmp[:]) if False else
                       g.dma_start(out=sp_a[c * 128:(c + 1) * 128, :], in_=vrow_[:]), 16)
            p.G(gsp)

            def tpm(t, c=c):
                cv.wait(t)
                cs.wait(t)
                for j in range(2):
                    ct.inc(t.matmul(ps_p1[:, j * 512:(j + 1) * 512], tpre[:],
                                    alpha[:, j * 512:(j + 1) * 512], start=True, stop=True))
                    ct.inc(t.matmul(ps_p2[:, j * 512:(j + 1) * 512], tpre[:],
                                    beta[:, j * 512:(j + 1) * 512], start=True, stop=True))
            p.Tn(tpm)

            def vsl(v, c=c):
                ct.wait(v)
                cv.chain(v, v.tensor_tensor(sn[:], ps_p1[:, 0:1024], ljm[:], ALU.mult))
                cv.chain(v, v.tensor_tensor(sd_[:], ps_p2[:, 0:1024], ljm[:], ALU.mult))
                cv.chain(v, v.memset(sn[0:1, :], 0.0))
                cv.chain(v, v.memset(sd_[0:1, :], 0.0))
                cv.inc(v.nop())
            p.V(vsl)

            def tbc(t, c=c):
                cv.wait(t)
                for j in range(2):
                    ct.inc(t.matmul(ps_kv[:, j * 512:(j + 1) * 512], ones_cst[0:1, :],
                                    rrowst[0:1, j * 512:(j + 1) * 512], start=True, stop=True))
            p.Tn(tbc)

            def vsl2(v, c=c):
                ct.wait(v)
                cv.chain(v, v.tensor_tensor(tmp2[:], ps_kv[:, 0:1024], lj[:], ALU.mult))
                cv.chain(v, v.tensor_tensor(sn[:], sn[:], tmp2[:], ALU.add))
                cv.inc(v.nop())
            p.V(vsl2)

            def tbc2(t, c=c):
                cv.wait(t)
                for j in range(2):
                    ct.inc(t.matmul(ps_kv[:, j * 512:(j + 1) * 512], ones_cst[0:1, :],
                                    rrowst2[0:1, j * 512:(j + 1) * 512], start=True, stop=True))
            p.Tn(tbc2)

            def vsl3(v, c=c):
                ct.wait(v)
                cv.chain(v, v.tensor_tensor(tmp2[:], ps_kv[:, 0:1024], lj[:], ALU.mult))
                cv.chain(v, v.tensor_tensor(sd_[:], sd_[:], tmp2[:], ALU.add))
                cv.inc(v.nop())
            p.V(vsl3)

            def gsp2(g, c=c):
                cv.wait(g)
                cg.inc(g.dma_start(out=sp_n[c * 128:(c + 1) * 128, :], in_=sn[:]), 16)
                cg.inc(g.dma_start(out=sp_d[c * 128:(c + 1) * 128, :], in_=sd_[:]), 16)
            p.G(gsp2)

            def vru(v, c=c):
                ct.wait(v)
                cv.chain(v, v.tensor_tensor(rrowst[0:1, :], rrowst[0:1, :], l128[:], ALU.mult))
                cv.chain(v, v.tensor_tensor(tmp[0:1, :], ps_p1[0:1, 0:1024], l127[:], ALU.mult))
                cv.chain(v, v.tensor_tensor(rrowst[0:1, :], rrowst[0:1, :], tmp[0:1, :], ALU.add))
                cv.chain(v, v.tensor_tensor(rrowst2[0:1, :], rrowst2[0:1, :], l128[:], ALU.mult))
                cv.chain(v, v.tensor_tensor(tmp[0:1, :], ps_p2[0:1, 0:1024], l127[:], ALU.mult))
                cv.inc(v.tensor_tensor(rrowst2[0:1, :], rrowst2[0:1, :], tmp[0:1, :], ALU.add))
            p.V(vru)

        def gag(g):
            cv.wait(g)
            cg.inc(g.dma_start(out=ag_in[0:1, :], in_=rrowst[0:1, :]), 16)
            cg.inc(g.dma_start(out=ag_in[1:2, :], in_=rrowst2[0:1, :]), 16)
            cg.wait(g)
            g.collective_compute("AllGather", ALU.bypass,
                                 replica_groups=[[0, 1], [2, 3], [4, 5], [6, 7]],
                                 ins=[ag_in], outs=[ag_out]).then_inc(ccg.sem, 1)
            g.wait_ge(ccg.sem, 1)
            cg.inc(g.dma_start(out=initr[:], in_=ag_out[0:1, :]), 16)
            cg.inc(g.dma_start(out=initr2[:], in_=ag_out[1:2, :]), 16)
        p.G(gag)

        def vinit(v):
            cg.wait(v)
            cl.wait(v)
            cv.chain(v, v.tensor_tensor(initr[0:1, :], initr[0:1, :], rolesb[:], ALU.mult))
            cv.inc(v.tensor_tensor(initr2[0:1, :], initr2[0:1, :], rolesb[:], ALU.mult))
        p.V(vinit)

        for c in range(NTB):
            def yb2(y, c=c):
                cv.wait(y)
                ct.wait(y)
                cl.inc(y.dma_start(out=sn[:], in_=sp_n[c * 128:(c + 1) * 128, :]), 16)
                cl.inc(y.dma_start(out=sd_[:], in_=sp_d[c * 128:(c + 1) * 128, :]), 16)
                cl.inc(y.dma_start(out=krow[:], in_=sp_b[c * 128:(c + 1) * 128, :]), 16)
                cl.inc(y.dma_start(out=vrow_[:], in_=sp_a[c * 128:(c + 1) * 128, :]), 16)
            p.Y(yb2)

            if c > 0:
                def vic(v, c=c):
                    cv.wait(v)
                    cv.chain(v, v.tensor_tensor(initr[0:1, :], initr[0:1, :], l128[:], ALU.mult))
                    cv.inc(v.tensor_tensor(initr2[0:1, :], initr2[0:1, :], l128[:], ALU.mult))
                p.V(vic)

            def tbc3(t, c=c):
                cv.wait(t)
                cs.wait(t)
                for j in range(2):
                    ct.inc(t.matmul(ps_p1[:, j * 512:(j + 1) * 512], ones_cst[0:1, :],
                                    initr[0:1, j * 512:(j + 1) * 512], start=True, stop=True))
                    ct.inc(t.matmul(ps_p2[:, j * 512:(j + 1) * 512], ones_cst[0:1, :],
                                    initr2[0:1, j * 512:(j + 1) * 512], start=True, stop=True))
            p.Tn(tbc3)

            def vwk(v, c=c):
                cl.wait(v)
                ct.wait(v)
                cv.chain(v, v.tensor_tensor(tmp[:], ps_p1[:, 0:1024], lj[:], ALU.mult))
                cv.chain(v, v.tensor_tensor(sn[:], sn[:], tmp[:], ALU.add))
                cv.chain(v, v.tensor_tensor(tmp[:], ps_p2[:, 0:1024], lj[:], ALU.mult))
                cv.chain(v, v.tensor_tensor(sd_[:], sd_[:], tmp[:], ALU.add))
                cv.chain(v, v.tensor_tensor(tmp[:], krow[:], utile[:], ALU.mult))
                cv.chain(v, v.tensor_tensor(sd_[:], sd_[:], tmp[:], ALU.add))
                cv.chain(v, v.tensor_tensor(tmp[:], tmp[:], vrow_[:], ALU.mult))
                cv.chain(v, v.tensor_tensor(sn[:], sn[:], tmp[:], ALU.add))
                cv.chain(v, v.reciprocal(tmp2[:], sd_[:]))
                cv.inc(v.tensor_tensor(rw[:], sn[:], tmp2[:], ALU.mult))
            p.V(vwk)

            def ssg(s, c=c):
                cv.wait(s)
                cs.inc(s.activation(out=tmp[:], in_=rbuf[:, c, :], func=AF.Sigmoid))
            p.S(ssg)

            def vrw(v, c=c):
                cs.wait(v)
                cv.inc(v.tensor_tensor(rw[:], rw[:], tmp[:], ALU.mult))
            p.V(vrw)

            for cc in range(8):
                def ttr2(t, cc=cc, c=c):
                    cv.wait(t)
                    cs.wait(t)
                    ct.inc(t.transpose(ps_tr[:], rw[:, cc * 128:(cc + 1) * 128], id_sb[:]))
                p.Tn(ttr2)

                def str2(s, cc=cc):
                    ct.wait(s)
                    cs.inc(s.copy(out=rwT[:, cc, :], in_=ps_tr[:]))
                p.S(str2)

            def two(t, c=c):
                cs.wait(t)
                for cc in range(8):
                    for j in range(2):
                        ct.inc(t.matmul(ps_kv[:, j * 512:(j + 1) * 512],
                                        rwT[:, cc, :], wo_sb[:, cc, j * 512:(j + 1) * 512],
                                        start=(cc == 0), stop=(cc == 7)))
            p.Tn(two)

            def sso(s, c=c):
                ct.wait(s)
                cs.inc(s.copy(out=srow_[:], in_=ps_kv[:, 0:1024]))
            p.S(sso)

            def yso(y, c=c):
                cs.wait(y)
                cst.inc(y.dma_start(out=s1_own[c * 128:(c + 1) * 128, :], in_=srow_[:]), 16)
            p.Y(yso)

        def yfin(y):
            cst.wait(y)
            cl.wait(y)
            cg.wait(y)
        p.Y(yfin)

    return p


def run_rwkv(inputs, x1):
    p = build_rwkv()
    ident = np.eye(128, dtype=np.float32)
    jcol = np.arange(128, dtype=np.float32).reshape(128, 1)
    tpre = np.zeros((128, 128), np.float32)
    tpre[:, 0] = 1.0
    for j in range(1, 128):
        tpre[:j, j] = 1.0
    tile = lambda vv: np.tile(np.asarray(vv, np.float32).reshape(1, -1), (128, 1))
    in_maps = []
    for (b, hh) in _core_slices():
        own = x1[b, hh * TH:(hh + 1) * TH]
        if hh == 0:
            prev = np.concatenate([np.zeros((1, D), np.float32), own[:-1]], axis=0)
        else:
            prev = x1[b, hh * TH - 1:(hh + 1) * TH - 1]
        role = np.full((1, D), 1.0 if hh == 1 else 0.0, np.float32)
        in_maps.append({
            "x1_in": np.ascontiguousarray(own),
            "x1p_in": np.ascontiguousarray(prev),
            "WkTc": np.ascontiguousarray(inputs["L1_attn_Wk"].T),
            "WvTc": np.ascontiguousarray(inputs["L1_attn_Wv"].T),
            "WrTc": np.ascontiguousarray(inputs["L1_attn_Wr"].T),
            "WoTc": np.ascontiguousarray(inputs["L1_attn_Wo"].T),
            "mixk_b": tile(inputs["L1_attn_mix_k"]),
            "mixv_b": tile(inputs["L1_attn_mix_v"]),
            "mixr_b": tile(inputs["L1_attn_mix_r"]),
            "td_b": tile(inputs["L1_attn_time_decay"]),
            "tf_b": tile(inputs["L1_attn_time_first"]),
            "jcol": jcol,
            "tprefix": tpre,
            "rolem": role,
            "ident": ident,
        })
    res = run_bass_kernel_spmd(p.nc, in_maps, core_ids=list(range(8)))
    s1 = np.zeros((4, T, D), np.float32)
    for ci, (b, hh) in enumerate(_core_slices()):
        s1[b, hh * TH:(hh + 1) * TH] = res.results[ci]["s1_own"]
    return s1




NKB = 18
NPAD = 2432

def build_l1_attn(parent=None, io=None, tag="Fa_"):
    p = Prog(parent, io, tag)
    st_in = p.inp("st_in", (NPAD, D), F32)
    Wqkv_in = p.inp("Wqkv_in", (D, 3072), F32)
    Wo_in = p.inp("Wo_in", (D, D), F32)
    qs_t = p.inp("qs_t", (128, D), F32)
    ks_t = p.inp("ks_t", (128, D), F32)
    cq_t = p.inp("cq_t", (NKB * 128, DH), F32)
    sq_t = p.inp("sq_t", (NKB * 128, DH), F32)
    ck_t = p.inp("ck_t", (NKB * 128, DH), F32)
    sk_t = p.inp("sk_t", (NKB * 128, DH), F32)
    m_first = p.inp("m_first", (128, 2, 512), F32)
    m_rest = p.inp("m_rest", (128, 2, 512), F32)
    ident = p.inp("ident", (128, 128), F32)
    st_out = p.out("st_out", (TH, D), F32)

    with p.es:
        id_sb = p.sb("id_sb", (128, 128), F32)
        xs = p.sb("xs", (128, D), F32)
        xsT = p.sb("xsT", (128, 8, 128), BF16)
        qk = p.sb("qk", (128, 2048), F32)
        ss2 = p.sb("ss2", (128, 32), F32)
        rn2 = p.sb("rn2", (128, 32), F32)
        qs_sb = p.sb("qs_sb", (128, D), F32)
        ks_sb = p.sb("ks_sb", (128, D), F32)
        rotb = p.sb("rotb", (128, 4, DH), F32)
        tmpa = p.sb("tmpa", (128, D), F32)
        tmpb = p.sb("tmpb", (128, D), F32)
        qT2 = p.sb("qT2", (128, 8, TH), BF16)
        kT2 = p.sb("kT2", (128, 8, NKB * 128), BF16)
        vr = p.sb("vr", (128, NKB, 1024), BF16)
        wq_sb = p.sb("wq_sb", (128, 8, 2048), BF16)
        wo_sb = p.sb("wo_sb", (128, 8, D), BF16)
        mf_sb = p.sb("mf_sb", (128, 2, 512), F32)
        mr_sb = p.sb("mr_sb", (128, 2, 512), F32)
        prow = p.sb("prow", (128, 512), F32)
        ptb = p.sb("ptb", (128, 4, 128), BF16)
        nmx2 = p.sb("nmx2", (128, 1), F32)
        ssum2 = p.sb("ssum2", (128, 1), F32)
        avf = p.sb("avf", (128, DH), F32)
        avT = p.sb("avT", (128, 8, 256), BF16)
        orow = p.sb("orow", (128, D), F32)
        strow = p.sb("strow", (128, D), F32)

        ps_b = p.ps("ps_b", (128, 2048))
        ps_tr = p.ps("ps_tr", (128, 128))
        ps_s = p.ps("ps_s", (128, 512))
        ps_av = p.ps("ps_av", (128, 64))

        cl = p.ctr("cl")
        cg = p.ctr("cg")
        cv = p.ctr("cv")
        cs = p.ctr("cs")
        ct = p.ctr("ct")
        cst = p.ctr("cst")
        p.start()

        def y0(y):
            cl.inc(y.dma_start(out=id_sb[:], in_=ident), 16)
            cl.inc(y.dma_start(out=qs_sb[:], in_=qs_t), 16)
            cl.inc(y.dma_start(out=ks_sb[:], in_=ks_t), 16)
            cl.inc(y.dma_start(out=mf_sb[:], in_=m_first), 16)
            cl.inc(y.dma_start(out=mr_sb[:], in_=m_rest), 16)
        p.Y(y0)

        def g0(g):
            for c in range(8):
                cg.inc(g.dma_start(out=wo_sb[:, c, :], in_=Wo_in[c * 128:(c + 1) * 128, :]), 16)
        p.G(g0)

        for which in ("qk", "v"):
            def gw(g, which=which):
                ct.wait(g)
                for c in range(8):
                    if which == "qk":
                        cg.inc(g.dma_start(out=wq_sb[:, c, 0:2048],
                                           in_=Wqkv_in[c * 128:(c + 1) * 128, 0:2048]), 16)
                    else:
                        cg.inc(g.dma_start(out=wq_sb[:, c, 0:1024],
                                           in_=Wqkv_in[c * 128:(c + 1) * 128, 2048:3072]), 16)
            p.G(gw)

            for mb in range(NKB):
                def yx(y, mb=mb, which=which):
                    cv.wait(y)
                    ct.wait(y)
                    cl.inc(y.dma_start(out=xs[:, 0:512],
                                       in_=st_in[mb * 128 + 1:mb * 128 + 129, 0:512]), 16)
                    cl.inc(y.dma_start(out=xs[:, 512:1024],
                                       in_=st_in[mb * 128:mb * 128 + 128, 512:1024]), 16)
                    if which == "qk":
                        cl.inc(y.dma_start(out=rotb[:, 0, :], in_=cq_t[mb * 128:(mb + 1) * 128, :]), 16)
                        cl.inc(y.dma_start(out=rotb[:, 1, :], in_=sq_t[mb * 128:(mb + 1) * 128, :]), 16)
                        cl.inc(y.dma_start(out=rotb[:, 2, :], in_=ck_t[mb * 128:(mb + 1) * 128, :]), 16)
                        cl.inc(y.dma_start(out=rotb[:, 3, :], in_=sk_t[mb * 128:(mb + 1) * 128, :]), 16)
                p.Y(yx)

                for c in range(8):
                    def ttr(t, c=c):
                        cl.wait(t)
                        cs.wait(t)
                        ct.inc(t.transpose(ps_tr[:], xs[:, c * 128:(c + 1) * 128], id_sb[:]))
                    p.Tn(ttr)

                    def sev(s, c=c):
                        ct.wait(s)
                        cs.inc(s.copy(out=xsT[:, c, :], in_=ps_tr[:]))
                    p.S(sev)

                def tmm(t, which=which):
                    cs.wait(t)
                    cg.wait(t)
                    nj = 4 if which == "qk" else 2
                    for c in range(8):
                        for j in range(nj):
                            ct.inc(t.matmul(ps_b[:, j * 512:(j + 1) * 512],
                                            xsT[:, c, :], wq_sb[:, c, j * 512:(j + 1) * 512],
                                            start=(c == 0), stop=(c == 7)))
                p.Tn(tmm)

                if which == "v":
                    def sv(s, mb=mb):
                        ct.wait(s)
                        cs.inc(s.copy(out=vr[:, mb, :], in_=ps_b[:, 0:1024]))
                    p.S(sv)
                    continue

                def sqk(s):
                    ct.wait(s)
                    cv.wait(s)
                    s.copy(out=qk[:, 0:1024], in_=ps_b[:, 0:1024])
                    cs.inc(s.copy(out=qk[:, 1024:2048], in_=ps_b[:, 1024:2048]))
                p.S(sqk)

                def sl2(s):
                    cs.wait(s)
                    s.activation(out=tmpa[:], in_=qk[:, 0:1024], func=AF.Square)
                    cs.inc(s.activation(out=tmpb[:], in_=qk[:, 1024:2048], func=AF.Square))
                p.S(sl2)

                def vl2(v):
                    cs.wait(v)
                    cv.chain(v, v.reduce_sum(ss2[:, 0:16], tmpa[:].rearrange("p (h e) -> p h e", e=64), axis=AX))
                    cv.chain(v, v.reduce_sum(ss2[:, 16:32], tmpb[:].rearrange("p (h e) -> p h e", e=64), axis=AX))
                    cv.inc(v.tensor_scalar_max(ss2[:], ss2[:], 1e-24))
                p.V(vl2)

                def sl3(s):
                    cv.wait(s)
                    s.activation(out=ss2[:], in_=ss2[:], func=AF.Ln)
                    cs.inc(s.activation(out=rn2[:], in_=ss2[:], func=AF.Exp, scale=-0.5))
                p.S(sl3)

                def vrot(v, mb=mb):
                    cs.wait(v)
                    cv.chain(v, v.tensor_tensor(
                        qk[:, 0:1024].rearrange("p (h e) -> p h e", e=64),
                        qk[:, 0:1024].rearrange("p (h e) -> p h e", e=64),
                        rn2[:, 0:16].unsqueeze(2).to_broadcast([128, 16, 64]), ALU.mult))
                    cv.chain(v, v.tensor_tensor(qk[:, 0:1024], qk[:, 0:1024], qs_sb[:], ALU.mult))
                    cv.chain(v, v.tensor_tensor(
                        qk[:, 1024:2048].rearrange("p (h e) -> p h e", e=64),
                        qk[:, 1024:2048].rearrange("p (h e) -> p h e", e=64),
                        rn2[:, 16:32].unsqueeze(2).to_broadcast([128, 16, 64]), ALU.mult))
                    cv.chain(v, v.tensor_tensor(qk[:, 1024:2048], qk[:, 1024:2048], ks_sb[:], ALU.mult))
                    for base, ci, si_ in ((0, 0, 1), (1024, 2, 3)):
                        qv = qk[:, base:base + 1024].rearrange("p (h e) -> p h e", e=64)
                        ta = tmpa[:].rearrange("p (h e) -> p h e", e=64)
                        tb_ = tmpb[:].rearrange("p (h e) -> p h e", e=64)
                        cv.chain(v, v.tensor_tensor(
                            ta, qv, rotb[:, ci, :].unsqueeze(1).to_broadcast([128, 16, 64]), ALU.mult))
                        cv.chain(v, v.tensor_tensor(
                            tb_[:, :, 0:32], qv[:, :, 32:64],
                            rotb[:, si_, 0:32].unsqueeze(1).to_broadcast([128, 16, 32]), ALU.mult))
                        cv.chain(v, v.tensor_tensor(
                            tb_[:, :, 32:64], qv[:, :, 0:32],
                            rotb[:, si_, 32:64].unsqueeze(1).to_broadcast([128, 16, 32]), ALU.mult))
                        cv.chain(v, v.tensor_tensor(ta[:, :, 0:32], ta[:, :, 0:32], tb_[:, :, 0:32], ALU.subtract))
                        cv.chain(v, v.tensor_tensor(ta[:, :, 32:64], ta[:, :, 32:64], tb_[:, :, 32:64], ALU.add))
                        cv.chain(v, v.tensor_copy(qk[:, base:base + 1024], tmpa[:]))
                p.V(vrot)

                for h in range(16):
                    hp = (h % 2) * 64
                    def tq_(t, h=h):
                        cv.wait(t)
                        cs.wait(t)
                        ct.inc(t.transpose(ps_tr[0:64, :], qk[:, h * 64:(h + 1) * 64], id_sb[:]))
                    p.Tn(tq_)

                    def sq_(s, mb=mb, h=h, hp=hp):
                        ct.wait(s)
                        if mb >= 2:
                            cs.inc(s.copy(out=qT2[hp:hp + 64, h // 2, (mb - 2) * 128:(mb - 1) * 128],
                                          in_=ps_tr[0:64, :]))
                        else:
                            cs.inc(s.nop())
                    p.S(sq_)

                    def tk_(t, h=h):
                        cs.wait(t)
                        ct.inc(t.transpose(ps_tr[0:64, :], qk[:, 1024 + h * 64:1024 + (h + 1) * 64], id_sb[:]))
                    p.Tn(tk_)

                    def sk_(s, mb=mb, h=h, hp=hp):
                        ct.wait(s)
                        cs.inc(s.copy(out=kT2[hp:hp + 64, h // 2, mb * 128:(mb + 1) * 128],
                                      in_=ps_tr[0:64, :]))
                    p.S(sk_)

        for w in range(8):
            for h in range(16):
                hp = (h % 2) * 64
                hc = h // 2
                for qt in range(2):
                    qq0 = w * 256 + qt * 128
                    def ts_(t, w=w, qt=qt, hp=hp, hc=hc, qq0=qq0):
                        cs.wait(t)
                        cv.wait(t)
                        ct.inc(t.matmul(ps_s[:],
                                        qT2[hp:hp + 64, hc, qq0:qq0 + 128],
                                        kT2[hp:hp + 64, hc, w * 256:w * 256 + 512],
                                        start=True, stop=True))
                    p.Tn(ts_)

                    def vms(v, w=w, qt=qt):
                        ct.wait(v)
                        msk = mf_sb if w == 0 else mr_sb
                        cv.chain(v, v.tensor_scalar(prow[:], ps_s[:], 8.0, None, ALU.mult))
                        cv.chain(v, v.tensor_tensor(prow[:], prow[:], msk[:, qt, :], ALU.add))
                        cv.inc(v.reduce_max(nmx2[:], prow[:], axis=AX, negate=True))
                    p.V(vms)

                    def sxp(s):
                        cv.wait(s)
                        cs.inc(s.activation(out=prow[:], in_=prow[:], func=AF.Exp,
                                            bias=nmx2[:], accum_out=ssum2[:]))
                    p.S(sxp)

                    def vnp(v):
                        cs.wait(v)
                        cv.chain(v, v.reciprocal(ssum2[:], ssum2[:]))
                        cv.inc(v.tensor_scalar(prow[:], prow[:], ssum2[:], None, ALU.mult))
                    p.V(vnp)

                    for kb in range(4):
                        def tpt(t, kb=kb):
                            cv.wait(t)
                            cs.wait(t)
                            ct.inc(t.transpose(ps_tr[:], prow[:, kb * 128:(kb + 1) * 128], id_sb[:]))
                        p.Tn(tpt)

                        def spt(s, kb=kb):
                            ct.wait(s)
                            cs.inc(s.copy(out=ptb[:, kb, :], in_=ps_tr[:]))
                        p.S(spt)

                    def tav(t, w=w, h=h):
                        cs.wait(t)
                        for kb in range(4):
                            ct.inc(t.matmul(ps_av[:], ptb[:, kb, :],
                                            vr[:, w * 2 + kb, h * 64:(h + 1) * 64],
                                            start=(kb == 0), stop=(kb == 3)))
                    p.Tn(tav)

                    def sav(s):
                        ct.wait(s)
                        cs.inc(s.copy(out=avf[:], in_=ps_av[:]))
                    p.S(sav)

                    def tavT(t):
                        cs.wait(t)
                        ct.inc(t.transpose(ps_tr[0:64, :], avf[:], id_sb[:]))
                    p.Tn(tavT)

                    def savT(s, h=h, hp=hp, hc=hc, qt=qt):
                        ct.wait(s)
                        cs.inc(s.copy(out=avT[hp:hp + 64, hc, qt * 128:(qt + 1) * 128],
                                      in_=ps_tr[0:64, :]))
                    p.S(savT)

            for qt in range(2):
                def yst(y, w=w, qt=qt):
                    cv.wait(y)
                    cst.wait(y)
                    r0 = 257 + w * 256 + qt * 128
                    cl.inc(y.dma_start(out=strow[:], in_=st_in[r0:r0 + 128, :]), 16)
                p.Y(yst)

                def two(t, w=w, qt=qt):
                    cs.wait(t)
                    cg.wait(t)
                    for c in range(8):
                        for j in range(2):
                            ct.inc(t.matmul(ps_b[:, j * 512:(j + 1) * 512],
                                            avT[:, c, qt * 128:(qt + 1) * 128],
                                            wo_sb[:, c, j * 512:(j + 1) * 512],
                                            start=(c == 0), stop=(c == 7)))
                p.Tn(two)

                def sor(s):
                    ct.wait(s)
                    cs.inc(s.copy(out=orow[:], in_=ps_b[:, 0:1024]))
                p.S(sor)

                def vor(v):
                    cs.wait(v)
                    cl.wait(v)
                    cv.inc(v.tensor_tensor(orow[:], orow[:], strow[:], ALU.add))
                p.V(vor)

                def yo2(y, w=w, qt=qt):
                    cv.wait(y)
                    cst.inc(y.dma_start(out=st_out[(w * 256 + qt * 128):(w * 256 + qt * 128 + 128), :],
                                        in_=orow[:]), 16)
                p.Y(yo2)

        def yfin(y):
            cst.wait(y)
            cl.wait(y)
            cg.wait(y)
        p.Y(yfin)
    return p


def build_l1_ffn(parent=None, io=None, tag="Ff_"):
    p = Prog(parent, io, tag)
    st_in = p.inp("st_in", (TH + 1, D), F32)
    W1_in = p.inp("W1_in", (D, 5460), F32)
    W2_in = p.inp("W2_in", (2816, D), F32)
    b1_t = p.inp("b1_t", (128, 5460), F32)
    b2_t = p.inp("b2_t", (128, D), F32)
    g_t = p.inp("g_t", (128, D), F32)
    bl_t = p.inp("bl_t", (128, D), F32)
    ident = p.inp("ident", (128, 128), F32)
    st_out = p.out("st_out", (TH, D), F32)

    with p.es:
        id_sb = p.sb("id_sb", (128, 128), F32)
        xs = p.sb("xs", (128, D), F32)
        xr = p.sb("xr", (128, D), F32)
        tmp = p.sb("tmp", (128, D), F32)
        mc = p.sb("mc", (128, 1), F32)
        rc = p.sb("rc", (128, 1), F32)
        xnT = p.sb("xnT", (128, 8, 128), BF16)
        w1a = p.sb("w1a", (128, 8, 2730), BF16)
        w1g = p.sb("w1g", (128, 8, 2730), BF16)
        w2_sb = p.sb("w2_sb", (128, 22, D), BF16)
        b1_sb = p.sb("b1_sb", (128, 5460), BF16)
        b2_sb = p.sb("b2_sb", (128, D), F32)
        g_sb = p.sb("g_sb", (128, D), F32)
        bl_sb = p.sb("bl_sb", (128, D), F32)
        h2 = p.sb("h2", (128, 2816), F32)
        apad = p.sb("apad", (128, 1365), F32)
        gat = p.sb("gat", (128, 1365), F32)
        h2T = p.sb("h2T", (128, 22, 128), BF16)
        orow = p.sb("orow", (128, D), F32)

        ps_a = p.ps("ps_a", (128, 1408))
        ps_g = p.ps("ps_g", (128, 1408))
        ps_tr = p.ps("ps_tr", (128, 128))

        cl = p.ctr("cl")
        cg = p.ctr("cg")
        cv = p.ctr("cv")
        cs = p.ctr("cs")
        ct = p.ctr("ct")
        cst = p.ctr("cst")
        p.start()

        def y0(y):
            cl.inc(y.dma_start(out=id_sb[:], in_=ident), 16)
            cl.inc(y.dma_start(out=b2_sb[:], in_=b2_t), 16)
            cl.inc(y.dma_start(out=g_sb[:], in_=g_t), 16)
            cl.inc(y.dma_start(out=bl_sb[:], in_=bl_t), 16)
        p.Y(y0)

        def g0(g):
            cg.inc(g.dma_start(out=b1_sb[:], in_=b1_t), 16)
            for c in range(22):
                cg.inc(g.dma_start(out=w2_sb[:, c, :], in_=W2_in[c * 128:(c + 1) * 128, :]), 16)
            for c in range(8):
                cg.inc(g.dma_start(out=w1a[:, c, :], in_=W1_in[c * 128:(c + 1) * 128, 0:2730]), 16)
                cg.inc(g.dma_start(out=w1g[:, c, :], in_=W1_in[c * 128:(c + 1) * 128, 2730:5460]), 16)
        p.G(g0)

        for mb in range(NTB):
            def yx(y, mb=mb):
                cv.wait(y)
                ct.wait(y)
                cst.wait(y)
                cl.inc(y.dma_start(out=xs[:, 0:512],
                                   in_=st_in[mb * 128 + 1:mb * 128 + 129, 0:512]), 16)
                cl.inc(y.dma_start(out=xs[:, 512:1024],
                                   in_=st_in[mb * 128:mb * 128 + 128, 512:1024]), 16)
                cl.inc(y.dma_start(out=xr[:], in_=st_in[mb * 128 + 1:mb * 128 + 129, :]), 16)
            p.Y(yx)

            def vw0(v):
                cl.wait(v)
                cv.inc(v.nop())
            p.V(vw0)
            ln_emit(p, xs[:], xs[:], tmp[:], mc[:], rc[:], cv, cs)

            def vgb(v):
                cs.wait(v)
                cv.chain(v, v.tensor_tensor(xs[:], xs[:], g_sb[:], ALU.mult))
                cv.inc(v.tensor_tensor(xs[:], xs[:], bl_sb[:], ALU.add))
            p.V(vgb)

            for c in range(8):
                def ttr(t, c=c):
                    cv.wait(t)
                    cs.wait(t)
                    ct.inc(t.transpose(ps_tr[:], xs[:, c * 128:(c + 1) * 128], id_sb[:]))
                p.Tn(ttr)

                def sev(s, c=c):
                    ct.wait(s)
                    cs.inc(s.copy(out=xnT[:, c, :], in_=ps_tr[:]))
                p.S(sev)

            for half in range(2):
                def tw1(t, half=half):
                    cs.wait(t)
                    cg.wait(t)
                    cv.wait(t)
                    for c in range(8):
                        for j in range(3):
                            n0, n1 = j * 512, min((j + 1) * 512, 1365)
                            ct.inc(t.matmul(ps_a[:, n0:n1], xnT[:, c, :],
                                            w1a[:, c, half * 1365 + n0:half * 1365 + n1],
                                            start=(c == 0), stop=(c == 7)))
                            ct.inc(t.matmul(ps_g[:, n0:n1], xnT[:, c, :],
                                            w1g[:, c, half * 1365 + n0:half * 1365 + n1],
                                            start=(c == 0), stop=(c == 7)))
                p.Tn(tw1)

                def vba(v, half=half):
                    ct.wait(v)
                    cs.wait(v)
                    cg.wait(v)
                    cv.chain(v, v.tensor_tensor(apad[:], ps_a[:, 0:1365],
                                                b1_sb[:, half * 1365:(half + 1) * 1365], ALU.add))
                    cv.inc(v.tensor_tensor(gat[:], ps_g[:, 0:1365],
                                           b1_sb[:, 2730 + half * 1365:2730 + (half + 1) * 1365], ALU.add))
                p.V(vba)

                def sge(s, half=half):
                    cv.wait(s)
                    cs.inc(s.activation(out=gat[:], in_=gat[:], func=AF.Gelu))
                p.S(sge)

                def vh2(v, half=half):
                    cs.wait(v)
                    cv.inc(v.tensor_tensor(h2[:, half * 1365:(half + 1) * 1365],
                                           apad[:], gat[:], ALU.mult))
                p.V(vh2)

            def vpad(v):
                cv.inc(v.memset(h2[:, 2730:2816], 0.0))
            p.V(vpad)

            for c in range(22):
                def tht(t, c=c):
                    cv.wait(t)
                    cs.wait(t)
                    ct.inc(t.transpose(ps_tr[:], h2[:, c * 128:(c + 1) * 128], id_sb[:]))
                p.Tn(tht)

                def sht(s, c=c):
                    ct.wait(s)
                    cs.inc(s.copy(out=h2T[:, c, :], in_=ps_tr[:]))
                p.S(sht)

            def tw2(t):
                cs.wait(t)
                for c in range(22):
                    for j in range(2):
                        ct.inc(t.matmul(ps_a[:, j * 512:(j + 1) * 512],
                                        h2T[:, c, :], w2_sb[:, c, j * 512:(j + 1) * 512],
                                        start=(c == 0), stop=(c == 21)))
            p.Tn(tw2)

            def so2(s):
                ct.wait(s)
                cs.inc(s.copy(out=orow[:], in_=ps_a[:, 0:1024]))
            p.S(so2)

            def vo2(v):
                cs.wait(v)
                cv.chain(v, v.tensor_tensor(orow[:], orow[:], b2_sb[:], ALU.add))
                cv.inc(v.tensor_tensor(orow[:], orow[:], xr[:], ALU.add))
            p.V(vo2)

            def yo2(y, mb=mb):
                cv.wait(y)
                cst.inc(y.dma_start(out=st_out[mb * 128:(mb + 1) * 128, :], in_=orow[:]), 16)
            p.Y(yo2)

        def yfin(y):
            cst.wait(y)
            cl.wait(y)
            cg.wait(y)
        p.Y(yfin)
    return p


def _rotary_tables():
    half = np.arange(0, DH, 2, dtype=np.float64)
    inv_freq = 1.0 / 10000.0 ** (half / DH)
    t = np.arange(T, dtype=np.float64)
    freqs = t[:, None] * inv_freq[None, :]
    freqs = np.concatenate([freqs, freqs], -1)
    base_scale = (half + 0.4 * DH) / (1.4 * DH)
    power = (t - T // 2) / 128.0
    scale = base_scale[None, :] ** power[:, None]
    scale = np.concatenate([scale, scale], -1)
    cos, sin = np.cos(freqs), np.sin(freqs)
    cq = (cos * scale).astype(np.float32)
    sq = (sin * scale).astype(np.float32)
    ck = (cos / scale).astype(np.float32)
    sk = (sin / scale).astype(np.float32)
    return cq, sq, ck, sk


def _attn_masks(first_window_blocked):
    m = np.zeros((128, 2, 512), np.float32)
    for qt in range(2):
        for pi in range(128):
            i = qt * 128 + pi
            for j in range(512):
                if not (i <= j <= i + 256):
                    m[pi, qt, j] = -3e38
    mf = m.copy()
    if first_window_blocked:
        mf[:, :, 0:256] = -3e38
    return mf, m


_MASK_CACHE = {}


def run_l1_attn(inputs, st, layer):
    p = build_l1_attn()
    ident = np.eye(128, dtype=np.float32)
    cq, sq, ck, sk = _rotary_tables()
    if "m" not in _MASK_CACHE:
        _MASK_CACHE["m0"] = _attn_masks(True)
        _MASK_CACHE["m1"] = _attn_masks(False)
        _MASK_CACHE["m"] = 1
    qs = np.tile(inputs["L1_ffn_q_scale"][layer], 16)
    ks = np.tile(inputs["L1_ffn_k_scale"][layer], 16)
    in_maps = []
    for (b, hh) in _core_slices():
        own0 = hh * TH
        sti = np.zeros((NPAD, D), np.float32)
        lo = own0 - 257
        src_lo = max(lo, 0)
        sti[src_lo - lo:257 + TH] = st[b, src_lo:own0 + TH]
        pos = np.arange(own0 - 256, own0 + 2048)
        posc = np.clip(pos, 0, T - 1)
        mf, mr = _MASK_CACHE["m0" if hh == 0 else "m1"]
        in_maps.append({
            "st_in": sti,
            "Wqkv_in": np.ascontiguousarray(inputs["L1_ffn_Wqkv"][layer]),
            "Wo_in": np.ascontiguousarray(inputs["L1_ffn_Wo"][layer]),
            "qs_t": np.tile(qs[None, :], (128, 1)).astype(np.float32),
            "ks_t": np.tile(ks[None, :], (128, 1)).astype(np.float32),
            "cq_t": np.ascontiguousarray(cq[posc]),
            "sq_t": np.ascontiguousarray(sq[posc]),
            "ck_t": np.ascontiguousarray(ck[posc]),
            "sk_t": np.ascontiguousarray(sk[posc]),
            "m_first": mf, "m_rest": mr,
            "ident": ident,
        })
    res = run_bass_kernel_spmd(p.nc, in_maps, core_ids=list(range(8)))
    out = np.zeros((4, T, D), np.float32)
    for ci, (b, hh) in enumerate(_core_slices()):
        out[b, hh * TH:(hh + 1) * TH] = res.results[ci]["st_out"]
    return out


def run_l1_ffn(inputs, st, layer):
    p = build_l1_ffn()
    ident = np.eye(128, dtype=np.float32)
    W2p = np.zeros((2816, D), np.float32)
    W2p[0:2730] = inputs["L1_ffn_W2"][layer]
    tile = lambda vv: np.tile(np.asarray(vv, np.float32)[None, :], (128, 1))
    in_maps = []
    for (b, hh) in _core_slices():
        own0 = hh * TH
        sti = np.zeros((TH + 1, D), np.float32)
        lo = own0 - 1
        src_lo = max(lo, 0)
        sti[src_lo - lo:] = st[b, src_lo:own0 + TH]
        in_maps.append({
            "st_in": sti,
            "W1_in": np.ascontiguousarray(inputs["L1_ffn_W1"][layer]),
            "W2_in": W2p,
            "b1_t": tile(inputs["L1_ffn_b1"][layer]),
            "b2_t": tile(inputs["L1_ffn_b2"][layer]),
            "g_t": tile(inputs["L1_ffn_ln_g"][layer]),
            "bl_t": tile(inputs["L1_ffn_ln_b"][layer]),
            "ident": ident,
        })
    res = run_bass_kernel_spmd(p.nc, in_maps, core_ids=list(range(8)))
    out = np.zeros((4, T, D), np.float32)
    for ci, (b, hh) in enumerate(_core_slices()):
        out[b, hh * TH:(hh + 1) * TH] = res.results[ci]["st_out"]
    return out




def build_combine(parent=None, io=None, tag="G_"):
    p = Prog(parent, io, tag)
    x1_in = p.inp("x1_in", (TH, D), F32)
    s1_in = p.inp("s1_in", (TH, D), F32)
    g1_in = p.inp("g1_in", (TH, D), F32)
    x2_out = p.out("x2_out", (TH, D), F32)
    with p.es:
        a = p.sb("a", (128, D), F32)
        b_ = p.sb("b_", (128, D), F32)
        c_ = p.sb("c_", (128, D), F32)
        cl = p.ctr("cl")
        cv = p.ctr("cv")
        cs = p.ctr("cs")
        cst = p.ctr("cst")
        p.start()
        for mb in range(NTB):
            def yb(y, mb=mb):
                if mb > 0:
                    cst.wait(y)
                    cv.wait(y)
                    cs.wait(y)
                cl.inc(y.dma_start(out=a[:], in_=g1_in[mb * 128:(mb + 1) * 128, :]), 16)
                cl.inc(y.dma_start(out=b_[:], in_=s1_in[mb * 128:(mb + 1) * 128, :]), 16)
                cl.inc(y.dma_start(out=c_[:], in_=x1_in[mb * 128:(mb + 1) * 128, :]), 16)
            p.Y(yb)

            def sg(s, mb=mb):
                cl.wait(s)
                cv.wait(s)
                cs.inc(s.activation(out=a[:], in_=a[:], func=AF.Silu))
            p.S(sg)

            def vc(v, mb=mb):
                cs.wait(v)
                cv.chain(v, v.tensor_tensor(a[:], a[:], b_[:], ALU.mult))
                cv.inc(v.tensor_tensor(a[:], a[:], c_[:], ALU.add))
            p.V(vc)

            def yo(y, mb=mb):
                cv.wait(y)
                cst.inc(y.dma_start(out=x2_out[mb * 128:(mb + 1) * 128, :], in_=a[:]), 16)
            p.Y(yo)
        def yfin(y):
            cst.wait(y)
        p.Y(yfin)
    return p


def run_combine(x1, s1, g1):
    p = build_combine()
    in_maps = []
    for (b, hh) in _core_slices():
        in_maps.append({
            "x1_in": np.ascontiguousarray(x1[b, hh * TH:(hh + 1) * TH]),
            "s1_in": np.ascontiguousarray(s1[b, hh * TH:(hh + 1) * TH]),
            "g1_in": np.ascontiguousarray(g1[b, hh * TH:(hh + 1) * TH]),
        })
    res = run_bass_kernel_spmd(p.nc, in_maps, core_ids=list(range(8)))
    x2 = np.zeros((4, T, D), np.float32)
    for ci, (b, hh) in enumerate(_core_slices()):
        x2[b, hh * TH:(hh + 1) * TH] = res.results[ci]["x2_out"]
    return x2


def build_ln_only(parent=None, io=None, tag="L_"):
    p = Prog(parent, io, tag)
    x_in = p.inp("x_in", (TH, D), F32)
    xn_out = p.out("xn_out", (TH, D), F32)
    with p.es:
        xt = p.sb("xt", (128, D), F32)
        xo = p.sb("xo", (128, D), F32)
        tmp = p.sb("tmp", (128, D), F32)
        mc = p.sb("mc", (128, 1), F32)
        rc = p.sb("rc", (128, 1), F32)
        cl = p.ctr("cl")
        cv = p.ctr("cv")
        cs = p.ctr("cs")
        cst = p.ctr("cst")
        p.start()
        for mb in range(NTB):
            def yb(y, mb=mb):
                if mb > 0:
                    cst.wait(y)
                    cv.wait(y)
                    cs.wait(y)
                cl.inc(y.dma_start(out=xt[:], in_=x_in[mb * 128:(mb + 1) * 128, :]), 16)
            p.Y(yb)
            def vv(v):
                cl.wait(v)
                cv.inc(v.nop())
            p.V(vv)
            ln_emit(p, xt[:], xo[:], tmp[:], mc[:], rc[:], cv, cs)
            def yo(y, mb=mb):
                cv.wait(y)
                cst.inc(y.dma_start(out=xn_out[mb * 128:(mb + 1) * 128, :], in_=xo[:]), 16)
            p.Y(yo)
        def yfin(y):
            cst.wait(y)
        p.Y(yfin)
    return p


def run_ln(x1):
    p = build_ln_only()
    in_maps = []
    for (b, hh) in _core_slices():
        in_maps.append({"x_in": np.ascontiguousarray(x1[b, hh * TH:(hh + 1) * TH])})
    res = run_bass_kernel_spmd(p.nc, in_maps, core_ids=list(range(8)))
    out = np.zeros((4, T, D), np.float32)
    for ci, (b, hh) in enumerate(_core_slices()):
        out[b, hh * TH:(hh + 1) * TH] = res.results[ci]["xn_out"]
    return out



def emit_x1prev(master, x1_d, x1p_d, role_col, tag="XP_"):
    p = Prog(master, {}, tag)
    agi = p.dram("agi", (1, D), F32)
    ago = p.dram("ago", (2, D), F32)
    with p.es:
        row = p.sb("row", (1, D), F32)
        rcol = p.sb("rcol", (1, 1), F32)
        cg = p.ctr("cg")
        cl = p.ctr("cl")
        cv = p.ctr("cv")
        ccx = Counter(p, tag + "cc")

        def g(g_):
            cg.inc(g_.dma_start(out=x1p_d[1:TH, :], in_=x1_d[0:TH - 1, :]), 16)
            cg.inc(g_.dma_start(out=agi, in_=x1_d[TH - 1:TH, :]), 16)
            cg.wait(g_)
            g_.collective_compute("AllGather", ALU.bypass,
                                  replica_groups=[[0, 1], [2, 3], [4, 5], [6, 7]],
                                  ins=[agi], outs=[ago]).then_inc(ccx.sem, 1)
            g_.wait_ge(ccx.sem, 1)
            cg.inc(g_.dma_start(out=row[:], in_=ago[0:1, :]), 16)
        p.G(g)

        def y(y_):
            cl.inc(y_.dma_start(out=rcol[:], in_=role_col[0:1, 0:1]), 16)
        p.Y(y)

        def v(v_):
            cg.wait(v_)
            cl.wait(v_)
            cv.inc(v_.tensor_scalar(row[:], row[:], rcol[:], None, ALU.mult))
        p.V(v)

        def g2(g_):
            cv.wait(g_)
            cg.inc(g_.dma_start(out=x1p_d[0:1, :], in_=row[:]), 16)
        p.G(g2)


def emit_halo(master, st_d, role_col, tag):
    p = Prog(master, {}, tag)
    agi = p.dram("agi", (257, D), F32)
    ago = p.dram("ago", (514, D), F32)
    with p.es:
        tile = p.sb("tile", (128, D), F32)
        rcol = p.sb("rcol", (128, 1), F32)
        cg = p.ctr("cg")
        cl = p.ctr("cl")
        cv = p.ctr("cv")
        cs = p.ctr("cs")
        ccx = Counter(p, tag + "cc")

        def y(y_):
            cl.inc(y_.dma_start(out=rcol[:], in_=role_col), 16)
        p.Y(y)

        def g(g_):
            cg.inc(g_.dma_start(out=agi, in_=st_d[2048:2305, :]), 16)
            cg.wait(g_)
            g_.collective_compute("AllGather", ALU.bypass,
                                  replica_groups=[[0, 1], [2, 3], [4, 5], [6, 7]],
                                  ins=[agi], outs=[ago]).then_inc(ccx.sem, 1)
            g_.wait_ge(ccx.sem, 1)
            cg.inc(g_.nop())
        p.G(g)

        for i, (r0, r1) in enumerate(((0, 128), (128, 256), (256, 257))):
            n = r1 - r0
            def gl(g_, r0=r0, n=n):
                cg.wait(g_)
                cv.wait(g_)
                cg.inc(g_.dma_start(out=tile[0:n, :], in_=ago[r0:r0 + n, :]), 16)
            p.G(gl)

            def v(v_, n=n):
                cg.wait(v_)
                cl.wait(v_)
                cv.inc(v_.tensor_scalar(tile[0:n, :], tile[0:n, :], rcol[0:n, :], None, ALU.mult))
            p.V(v)

            def g2(g_, r0=r0, n=n):
                cv.wait(g_)
                cg.inc(g_.dma_start(out=st_d[r0:r0 + n, :], in_=tile[0:n, :]), 16)
            p.G(g2)


def kernel(**inputs):
    x = np.asarray(inputs["L0_attn_x"])
    master = Prog()
    nc = master.nc

    def mi(name, shape):
        return master.inp(name, shape, F32)
    xb = mi("xb", (T, D))
    WqT = mi("WqT", (128, DH)); WkT = mi("WkT", (128, DH)); WvT = mi("WvT", (128, DH))
    WoT = mi("WoT", (D, D))
    qposb = mi("qposb", (128, TH)); kposc = mi("kposc", (128, NB))
    ident = mi("ident", (128, 128)); onesc = mi("onesc", (1, 128))
    wqT = mi("wqT", (D, 8192)); gcol = mi("gcol", (128, 8)); gb_in = mi("gb_in", (128, D))
    keysT = mi("keysT", (2, 8, 4, 128, 256))
    iota256 = master.inp("iota256", (128, 256), I32)
    iota16f = mi("iota16f", (128, 16))
    tdn_sl = mi("tdn_sl", (8192, D)); tup_sl = mi("tup_sl", (8192, D))
    WkTc = mi("WkTc", (D, D)); WvTc = mi("WvTc", (D, D)); WrTc = mi("WrTc", (D, D)); WoTc = mi("WoTc", (D, D))
    mixk_b = mi("mixk_b", (128, D)); mixv_b = mi("mixv_b", (128, D)); mixr_b = mi("mixr_b", (128, D))
    td_b = mi("td_b", (128, D)); tf_b = mi("tf_b", (128, D))
    jcol = mi("jcol", (128, 1)); tprefix = mi("tprefix", (128, 128)); rolem = mi("rolem", (1, D))
    role_col = mi("role_col", (128, 1))
    qs_t = [mi(f"qs_t{i}", (128, D)) for i in range(6)]
    ks_t = [mi(f"ks_t{i}", (128, D)) for i in range(6)]
    cq_t = mi("cq_t", (NKB * 128, DH)); sq_t = mi("sq_t", (NKB * 128, DH))
    ck_t = mi("ck_t", (NKB * 128, DH)); sk_t = mi("sk_t", (NKB * 128, DH))
    m_first = mi("m_first", (128, 2, 512)); m_rest = mi("m_rest", (128, 2, 512))
    Wqkv_i = [mi(f"Wqkv_{i}", (D, 3072)) for i in range(6)]
    Wo_i = [mi(f"Wol_{i}", (D, D)) for i in range(6)]
    W1_i = [mi(f"W1_{i}", (D, 5460)) for i in range(6)]
    W2_i = [mi(f"W2_{i}", (2816, D)) for i in range(6)]
    b1_i = [mi(f"b1_{i}", (128, 5460)) for i in range(6)]
    b2_i = [mi(f"b2_{i}", (128, D)) for i in range(6)]
    g_i = [mi(f"g_{i}", (128, D)) for i in range(6)]
    bl_i = [mi(f"bl_{i}", (128, D)) for i in range(6)]
    x2_out = master.out("x2_out", (TH, D), F32)

    s0_d = master.dram("s0_d", (TH, D), F32)
    xn_d = master.dram("xn_d", (TH, D), F32)
    idx_d = nc.dram_tensor("idx_d", [TH, 128], I32, kind="Internal").ap()
    sm_d = master.dram("sm_d", (TH, 128), F32)
    xnf_d = master.dram("xnf_d", (TH, D), F32)
    x1_d = master.dram("x1_d", (TH, D), F32)
    x1p_d = master.dram("x1p_d", (TH, D), F32)
    s1_d = master.dram("s1_d", (TH, D), F32)
    stA = master.dram("stA", (NPAD, D), F32)
    stB = master.dram("stB", (NPAD, D), F32)

    with master.es:
        master.start()
        build_l0_attn(master, {
            "xb": xb, "WqT": WqT, "WkT": WkT, "WvT": WvT, "WoT": WoT,
            "qposb": qposb, "kposc": kposc, "ident": ident, "onesc": onesc,
            "s_own": s0_d, "xn_own": xn_d})
        master.barrier()
        build_l0_ffn_sim(master, {
            "xn_own": xn_d, "wqT": wqT, "gcol": gcol, "gb_in": gb_in, "keysT": keysT,
            "iota256": iota256, "iota16f": iota16f, "ident": ident,
            "idx_all": idx_d, "sm_all": sm_d, "xnf_own": xnf_d})
        master.barrier()
        build_l0_ffn_gather(master, {
            "idx_in": idx_d, "sm_in": sm_d, "xnf_in": xnf_d, "s_in": s0_d,
            "x_in": xb[0:TH, :], "tdn_sl": tdn_sl, "tup_sl": tup_sl, "ident": ident,
            "x1_own": x1_d})
        master.barrier()
        emit_x1prev(master, x1_d, x1p_d, role_col)
        master.barrier()
        build_rwkv(master, {
            "x1_in": x1_d, "x1p_in": x1p_d, "WkTc": WkTc, "WvTc": WvTc, "WrTc": WrTc,
            "WoTc": WoTc, "mixk_b": mixk_b, "mixv_b": mixv_b, "mixr_b": mixr_b,
            "td_b": td_b, "tf_b": tf_b, "jcol": jcol, "tprefix": tprefix,
            "rolem": rolem, "ident": ident, "s1_own": s1_d})
        master.barrier()
        build_ln_only(master, {"x_in": x1_d, "xn_out": stA[257:2305, :]})
        master.barrier()
        emit_halo(master, stA, role_col, "H0_")
        master.barrier()
        cur, nxt = stA, stB
        for layer in range(6):
            build_l1_attn(master, {
                "st_in": cur, "Wqkv_in": Wqkv_i[layer], "Wo_in": Wo_i[layer],
                "qs_t": qs_t[layer], "ks_t": ks_t[layer],
                "cq_t": cq_t, "sq_t": sq_t, "ck_t": ck_t, "sk_t": sk_t,
                "m_first": m_first, "m_rest": m_rest, "ident": ident,
                "st_out": nxt[257:2305, :]}, tag=f"F{layer}a_")
            master.barrier()
            emit_halo(master, nxt, role_col, f"Ha{layer}_")
            master.barrier()
            cur, nxt = nxt, cur
            build_l1_ffn(master, {
                "st_in": cur[256:2305, :], "W1_in": W1_i[layer], "W2_in": W2_i[layer],
                "b1_t": b1_i[layer], "b2_t": b2_i[layer], "g_t": g_i[layer],
                "bl_t": bl_i[layer], "ident": ident,
                "st_out": nxt[257:2305, :]}, tag=f"F{layer}f_")
            master.barrier()
            if layer < 5:
                emit_halo(master, nxt, role_col, f"Hf{layer}_")
                master.barrier()
            cur, nxt = nxt, cur
        build_combine(master, {
            "x1_in": x1_d, "s1_in": s1_d, "g1_in": cur[257:2305, :], "x2_out": x2_out})

    identH = np.eye(128, dtype=np.float32)
    onescH = np.ones((1, 128), dtype=np.float32)
    keys = inputs["L0_ffn_keys"]
    keysTH = np.zeros((2, 8, 4, 128, 256), np.float32)
    for half in range(2):
        for h in range(8):
            for i in range(4):
                keysTH[half, h, i] = keys[h, :, half, i * 128:(i + 1) * 128].T
    gamma = inputs["L0_ffn_gamma"]
    cq, sq, ck, sk = _rotary_tables()
    if "m" not in _MASK_CACHE:
        _MASK_CACHE["m0"] = _attn_masks(True)
        _MASK_CACHE["m1"] = _attn_masks(False)
        _MASK_CACHE["m"] = 1
    jcolH = np.arange(128, dtype=np.float32).reshape(128, 1)
    tpreH = np.zeros((128, 128), np.float32)
    tpreH[:, 0] = 1.0
    for j in range(1, 128):
        tpreH[:j, j] = 1.0
    tile = lambda vv: np.tile(np.asarray(vv, np.float32).reshape(1, -1), (128, 1))
    W2pH = [None] * 6
    for i in range(6):
        W2pH[i] = np.zeros((2816, D), np.float32)
        W2pH[i][0:2730] = inputs["L1_ffn_W2"][i]

    in_maps = []
    for ci, (b, hh) in enumerate(_core_slices()):
        own = x[b, hh * TH:(hh + 1) * TH, :]
        other = x[b, (1 - hh) * TH:(1 - hh) * TH + TH, :]
        xbH = np.ascontiguousarray(np.concatenate([own, other], axis=0))
        qpos = np.arange(hh * TH, (hh + 1) * TH, dtype=np.float32)
        rot = np.concatenate([np.arange(hh * TH, (hh + 1) * TH),
                              np.arange((1 - hh) * TH, (1 - hh) * TH + TH)]).astype(np.float32)
        pos = np.arange(hh * TH - 256, hh * TH + 2048)
        posc = np.clip(pos, 0, T - 1)
        mf, mr = _MASK_CACHE["m0" if hh == 0 else "m1"]
        im = {
            "xb": xbH,
            "WqT": np.ascontiguousarray(np.vstack([inputs["L0_attn_Wq"].T] * 2)),
            "WkT": np.ascontiguousarray(np.vstack([inputs["L0_attn_Wk"].T] * 2)),
            "WvT": np.ascontiguousarray(np.vstack([inputs["L0_attn_Wv"].T] * 2)),
            "WoT": np.ascontiguousarray(inputs["L0_attn_Wo"].T),
            "qposb": np.tile(qpos[None, :], (128, 1)),
            "kposc": np.ascontiguousarray(rot.reshape(NB, 128).T),
            "ident": identH, "onesc": onescH,
            "wqT": np.ascontiguousarray(inputs["L0_ffn_wq"].T),
            "gcol": np.ascontiguousarray(gamma.reshape(8, 128).T),
            "gb_in": tile(gamma),
            "keysT": keysTH,
            "iota256": np.tile(np.arange(256, dtype=np.int32)[None, :], (128, 1)),
            "iota16f": np.tile(np.arange(16, dtype=np.float32)[None, :], (128, 1)),
            "tdn_sl": np.ascontiguousarray(inputs["L0_ffn_table_down"][ci * 8192:(ci + 1) * 8192]),
            "tup_sl": np.ascontiguousarray(inputs["L0_ffn_table_up"][ci * 8192:(ci + 1) * 8192]),
            "WkTc": np.ascontiguousarray(inputs["L1_attn_Wk"].T),
            "WvTc": np.ascontiguousarray(inputs["L1_attn_Wv"].T),
            "WrTc": np.ascontiguousarray(inputs["L1_attn_Wr"].T),
            "WoTc": np.ascontiguousarray(inputs["L1_attn_Wo"].T),
            "mixk_b": tile(inputs["L1_attn_mix_k"]),
            "mixv_b": tile(inputs["L1_attn_mix_v"]),
            "mixr_b": tile(inputs["L1_attn_mix_r"]),
            "td_b": tile(inputs["L1_attn_time_decay"]),
            "tf_b": tile(inputs["L1_attn_time_first"]),
            "jcol": jcolH, "tprefix": tpreH,
            "rolem": np.full((1, D), 1.0 if hh == 1 else 0.0, np.float32),
            "role_col": np.full((128, 1), 1.0 if hh == 1 else 0.0, np.float32),
            "cq_t": np.ascontiguousarray(cq[posc]),
            "sq_t": np.ascontiguousarray(sq[posc]),
            "ck_t": np.ascontiguousarray(ck[posc]),
            "sk_t": np.ascontiguousarray(sk[posc]),
            "m_first": mf, "m_rest": mr,
        }
        for i in range(6):
            im[f"qs_t{i}"] = tile(np.tile(inputs["L1_ffn_q_scale"][i], 16))
            im[f"ks_t{i}"] = tile(np.tile(inputs["L1_ffn_k_scale"][i], 16))
            im[f"Wqkv_{i}"] = np.ascontiguousarray(inputs["L1_ffn_Wqkv"][i])
            im[f"Wol_{i}"] = np.ascontiguousarray(inputs["L1_ffn_Wo"][i])
            im[f"W1_{i}"] = np.ascontiguousarray(inputs["L1_ffn_W1"][i])
            im[f"W2_{i}"] = W2pH[i]
            im[f"b1_{i}"] = tile(inputs["L1_ffn_b1"][i])
            im[f"b2_{i}"] = tile(inputs["L1_ffn_b2"][i])
            im[f"g_{i}"] = tile(inputs["L1_ffn_ln_g"][i])
            im[f"bl_{i}"] = tile(inputs["L1_ffn_ln_b"][i])
        in_maps.append(im)

    res = run_bass_kernel_spmd(master.nc, in_maps, core_ids=list(range(8)))
    x2 = np.zeros((4, T, D), np.float32)
    for ci, (b, hh) in enumerate(_core_slices()):
        x2[b, hh * TH:(hh + 1) * TH] = res.results[ci]["x2_out"]
    return x2


# version 03 (final)
# speedup vs baseline: 1.3913x; 1.3913x over previous
import numpy as np
import ml_dtypes
from contextlib import ExitStack
import concourse.bass as bass
import concourse.mybir as mybir
from concourse.bass_utils import run_bass_kernel_spmd
from concourse.bass_core_rust import IndirectOffsetOnAxis

F32 = mybir.dt.float32
BF16 = mybir.dt.bfloat16
I32 = mybir.dt.int32
AF = mybir.ActivationFunctionType
ALU = mybir.AluOpType
AX = mybir.AxisListType.X

T = 4096
TH = 2048
D = 1024
NTB = TH // 128
NB = T // 128
NHEAD = 16
DH = 64


class Counter:
    def __init__(self, prog, name):
        master = prog
        while getattr(master, "parent", None) is not None:
            master = master.parent
        self.sem = master.es.enter_context(prog.nc.semaphore(name))
        self.n = 0

    def inc(self, instr, k=1):
        instr.then_inc(self.sem, k)
        self.n += k

    def wait(self, eng):
        if self.n > 0:
            eng.wait_ge(self.sem, self.n)

    def chain(self, eng, instr):
        self.inc(instr)
        eng.wait_ge(self.sem, self.n)


class Prog:
    def __init__(self, parent=None, io=None, tag=""):
        self.parent = parent
        self.io = io or {}
        self.tag = tag
        if parent is None:
            self.nc = bass.Bass("TRN2", target_bir_lowering=False, debug=False)
            self.es = ExitStack()
            self.blk = None
            self.ctrs = {}
        else:
            self.nc = parent.nc
            self.es = ExitStack()
            self.blk = parent.blk
            self.ctrs = parent.ctrs

    def inp(self, name, shape, dtype):
        if self.parent is not None:
            return self.io[name]
        return self.nc.dram_tensor(name, list(shape), dtype, kind="ExternalInput").ap()

    def out(self, name, shape, dtype):
        if self.parent is not None:
            return self.io[name]
        return self.nc.dram_tensor(name, list(shape), dtype, kind="ExternalOutput").ap()

    def dram(self, name, shape, dtype):
        return self.nc.dram_tensor(self.tag + name, list(shape), dtype, kind="Internal").ap()

    def sb(self, name, shape, dtype):
        return self.es.enter_context(self.nc.sbuf_tensor(self.tag + name, list(shape), dtype))

    def ps(self, name, shape, dtype=F32):
        return self.es.enter_context(self.nc.psum_tensor(self.tag + name, list(shape), dtype))

    def ctr(self, name):
        if name in ("cc1", "cc2", "ccg") or self.parent is None:
            if self.parent is None and name not in self.ctrs:
                self.ctrs[name] = Counter(self, name)
                return self.ctrs[name]
            return Counter(self, self.tag + name)
        if name not in self.ctrs:
            self.ctrs[name] = Counter(self, name)
        return self.ctrs[name]

    def start(self):
        if self.parent is None:
            self.blk = self.es.enter_context(self.nc.Block())

    def barrier(self):
        cs_ = [c for n, c in self.ctrs.items()]
        for fn in (self.V, self.S, self.Tn, self.G, self.Y):
            def w(e, cs_=cs_):
                for c in cs_:
                    c.wait(e)
            fn(w)

    def V(self, fn):
        self.blk.vector(fn)

    def S(self, fn):
        self.blk.scalar(fn)

    def Tn(self, fn):
        self.blk.tensor(fn)

    def G(self, fn):
        self.blk.gpsimd(fn)

    def Y(self, fn):
        self.blk.sync(fn)


def ln_emit(p, xin, xout, tmp, mc, rc, cv, cs, n=D, eps=1e-5):
    def s1(s):
        cv.wait(s)
        s.activation(out=tmp, in_=xin, func=AF.Identity, accum_out=mc)
        cs.inc(s.activation(out=tmp, in_=xin, func=AF.Square, accum_out=rc))
    p.S(s1)

    def v1(v):
        cs.wait(v)
        cv.chain(v, v.tensor_scalar(mc, mc, 1.0 / n, None, ALU.mult))
        cv.chain(v, v.tensor_scalar(rc, rc, 1.0 / n, None, ALU.mult))
        cv.chain(v, v.tensor_scalar(tmp[:, 0:1], mc, mc, None, ALU.mult))
        cv.chain(v, v.tensor_tensor(rc, rc, tmp[:, 0:1], ALU.subtract))
        cv.chain(v, v.tensor_scalar(rc, rc, 1.0, eps, ALU.mult, ALU.add))
    p.V(v1)

    def s2(s):
        cv.wait(s)
        cs.chain(s, s.activation(out=rc, in_=rc, func=AF.Ln))
        cs.inc(s.activation(out=rc, in_=rc, func=AF.Exp, scale=-0.5))
    p.S(s2)

    def v2(v):
        cs.wait(v)
        cv.chain(v, v.tensor_scalar(xout, xin, mc, None, ALU.subtract))
        cv.inc(v.tensor_scalar(xout, xout, rc, None, ALU.mult))
    p.V(v2)


def build_l0_attn(parent=None, io=None, tag="B_"):
    p = Prog(parent, io, tag)
    xb = p.inp("xb", (T, D), F32)
    WqT = p.inp("WqT", (128, DH), F32)
    WkT = p.inp("WkT", (128, DH), F32)
    WvT = p.inp("WvT", (128, DH), F32)
    WoT = p.inp("WoT", (D, D), F32)
    qposb = p.inp("qposb", (128, TH), F32)
    kposc = p.inp("kposc", (128, NB), F32)
    ident = p.inp("ident", (128, 128), F32)
    onesc = p.inp("onesc", (1, 128), F32)
    s_own = p.out("s_own", (TH, D), F32)
    xn_own = p.out("xn_own", (TH, D), F32)

    with p.es:
        xt = p.sb("xt", (128, D), F32)
        xnt = p.sb("xnt", (128, D), F32)
        tmp = p.sb("tmp", (128, D), F32)
        mc = p.sb("mc", (128, 1), F32)
        rc = p.sb("rc", (128, 1), F32)
        xnT = p.sb("xnT", (128, 8, T), BF16)
        wq_sb = p.sb("wq_sb", (128, DH), BF16)
        wk_sb = p.sb("wk_sb", (128, DH), BF16)
        wv_sb = p.sb("wv_sb", (128, DH), BF16)
        woT_sb = p.sb("woT_sb", (128, 8, D), BF16)
        id_sb = p.sb("id_sb", (128, 128), F32)
        ones_sb = p.sb("ones_sb", (1, 128), F32)
        qpb = p.sb("qpb", (128, TH), F32)
        kpc = p.sb("kpc", (128, NB), F32)
        qT = p.sb("qT", (DH, T), BF16)
        kT = p.sb("kT", (DH, T), BF16)
        vrow = p.sb("vrow", (128, NB, 66), BF16)
        red = p.sb("red", (DH, 2), F32)
        bsc = p.sb("bsc", (1, 4), F32)
        nbound = p.sb("nbound", (128, 1), F32)
        mkc = p.sb("mkc", (128, TH), BF16)
        pt = p.sb("pt", (128, TH), BF16)
        avn = p.sb("avn", (DH, TH), F32)
        sumrow = p.sb("sumrow", (1, TH), F32)
        avT = p.sb("avT", (128, 8, TH), BF16)
        srow = p.sb("srow", (128, D), F32)

        ps_a = p.ps("ps_a", (128, 128))
        ps_qk = p.ps("ps_qk", (DH, 512))
        ps_st = p.ps("ps_st", (128, 1024))
        ps_av = p.ps("ps_av", (66, TH))
        ps_c = ps_a

        cl = p.ctr("cl")
        cg = p.ctr("cg")
        cv = p.ctr("cv")
        cs = p.ctr("cs")
        ct = p.ctr("ct")
        cst = p.ctr("cst")
        p.start()

        def y0(y):
            cl.inc(y.dma_start(out=id_sb[:], in_=ident), 16)
            cl.inc(y.dma_start(out=ones_sb[:], in_=onesc), 16)
            cl.inc(y.dma_start(out=qpb[:], in_=qposb), 16)
            cl.inc(y.dma_start(out=kpc[:], in_=kposc), 16)
        p.Y(y0)

        def g0(g):
            cg.inc(g.dma_start(out=wq_sb[:], in_=WqT), 16)
            cg.inc(g.dma_start(out=wk_sb[:], in_=WkT), 16)
            cg.inc(g.dma_start(out=wv_sb[:], in_=WvT), 16)
            for c in range(8):
                cg.inc(g.dma_start(out=woT_sb[:, c, :], in_=WoT[c * 128:(c + 1) * 128, :]), 16)
        p.G(g0)

        def v_init(v):
            v.memset(vrow[:, :, 64:65], 1.0)
            cv.inc(v.memset(vrow[:, :, 65:66], 0.0))
        p.V(v_init)

        for b in range(NB):
            def yb(y, b=b):
                if b > 0:
                    ct.wait(y)
                    cst.wait(y)
                cl.inc(y.dma_start(out=xt[:], in_=xb[b * 128:(b + 1) * 128, :]), 16)
            p.Y(yb)

            def vw(v):
                cl.wait(v)
                cst.wait(v)
                cv.inc(v.nop())
            p.V(vw)

            ln_emit(p, xt[:], xnt[:], tmp[:], mc[:], rc[:], cv, cs)

            if b < NTB:
                def yo(y, b=b):
                    cv.wait(y)
                    cst.inc(y.dma_start(out=xn_own[b * 128:(b + 1) * 128, :], in_=xnt[:]), 16)
                p.Y(yo)

            for c in range(8):
                def tb_(t, b=b, c=c):
                    cv.wait(t)
                    cs.wait(t)
                    ct.inc(t.transpose(ps_a[:], xnt[:, c * 128:(c + 1) * 128], id_sb[:]))
                p.Tn(tb_)

                def se(s, b=b, c=c):
                    ct.wait(s)
                    cs.inc(s.copy(out=xnT[:, c, b * 128:(b + 1) * 128], in_=ps_a[:]))
                p.S(se)

        for h in range(NHEAD):
            hp = (h % 2) * 64
            hc = h // 2
            for wsb, dstT in ((wq_sb, qT), (wk_sb, kT)):
                for i in range(8):
                    def tm(t, wsb=wsb, hp=hp, hc=hc, i=i):
                        cs.wait(t)
                        cg.wait(t)
                        ct.inc(t.matmul(ps_qk[:, :], wsb[hp:hp + 64, :],
                                        xnT[hp:hp + 64, hc, i * 512:(i + 1) * 512],
                                        start=True, stop=True))
                    p.Tn(tm)

                    def sev(s, dstT=dstT, i=i):
                        ct.wait(s)
                        cs.inc(s.copy(out=dstT[:, i * 512:(i + 1) * 512], in_=ps_qk[:, :]))
                    p.S(sev)

            for b in range(NB):
                def tv(t, hp=hp, hc=hc, b=b):
                    cs.wait(t)
                    cv.wait(t)
                    ct.inc(t.matmul(ps_a[:, 0:64], xnT[hp:hp + 64, hc, b * 128:(b + 1) * 128],
                                    wv_sb[hp:hp + 64, :], start=True, stop=True))
                p.Tn(tv)

                def sv(s, b=b):
                    ct.wait(s)
                    cs.inc(s.copy(out=vrow[:, b, 0:64], in_=ps_a[:, 0:64]))
                p.S(sv)

            def vb(v):
                cs.wait(v)
                v.reduce_max(red[:, 0:1], qT[:], axis=AX, apply_absolute_value=True)
                cv.inc(v.reduce_max(red[:, 1:2], kT[:], axis=AX, apply_absolute_value=True))
            p.V(vb)

            def tb3(t):
                cv.wait(t)
                ct.inc(t.transpose(ps_c[0:1, 0:64], red[:, 0:1], id_sb[0:64, 0:64]))
                ct.inc(t.transpose(ps_c[0:1, 64:128], red[:, 1:2], id_sb[0:64, 0:64]))
            p.Tn(tb3)


            def vb4(v):
                if h == 0:
                    cg.wait(v)
                ct.wait(v)
                v.reduce_max(bsc[0:1, 0:1], ps_c[0:1, 0:64], axis=AX)
                cv.chain(v, v.reduce_max(bsc[0:1, 1:2], ps_c[0:1, 64:128], axis=AX))
                cv.chain(v, v.tensor_tensor(bsc[0:1, 2:3], bsc[0:1, 0:1], bsc[0:1, 1:2], ALU.mult))
                cv.inc(v.tensor_scalar(bsc[0:1, 2:3], bsc[0:1, 2:3], -8.0, None, ALU.mult))
            p.V(vb4)


            def tb5(t):
                cv.wait(t)
                cg.wait(t)
                ct.inc(t.matmul(ps_c[:, 0:1], ones_sb[:], bsc[0:1, 2:3], start=True, stop=True))
            p.Tn(tb5)

            def vb5(v):
                ct.wait(v)
                cv.inc(v.tensor_copy(nbound[:], ps_c[:, 0:1]))
            p.V(vb5)

            for kc in range(NB):
                def vm(v, kc=kc):
                    cv.wait(v)
                    ct.wait(v)
                    cv.inc(v.tensor_scalar(mkc[:], qpb[:], kpc[:, kc:kc + 1], None, ALU.is_ge))
                p.V(vm)

                for qh in range(2):
                    def ts_(t, kc=kc, qh=qh):
                        cs.wait(t)
                        for j in range(2):
                            ct.inc(t.matmul(
                                ps_st[:, j * 512:(j + 1) * 512],
                                kT[:, kc * 128:(kc + 1) * 128],
                                qT[:, qh * 1024 + j * 512:qh * 1024 + (j + 1) * 512],
                                start=True, stop=True))
                    p.Tn(ts_)

                    def sx(s, kc=kc, qh=qh):
                        ct.wait(s)
                        cv.wait(s)
                        cs.inc(s.activation(out=pt[:, qh * 1024:(qh + 1) * 1024],
                                            in_=ps_st[:, 0:1024], func=AF.Exp,
                                            bias=nbound[:, 0:1], scale=0.125))
                    p.S(sx)

                def vmask(v, kc=kc):
                    cs.wait(v)
                    cv.inc(v.tensor_tensor(pt[:], pt[:], mkc[:], ALU.mult))
                p.V(vmask)


                def tav(t, kc=kc):
                    cg.wait(t)
                    cv.wait(t)
                    for j in range(4):
                        ct.inc(t.matmul(ps_av[:, j * 512:(j + 1) * 512],
                                        vrow[:, kc, :], pt[:, j * 512:(j + 1) * 512],
                                        start=(kc == 0), stop=(kc == NB - 1)))
                p.Tn(tav)

            def sav(s, h=h):
                ct.wait(s)
                cv.wait(s)
                s.copy(out=avn[:, 0:1024], in_=ps_av[0:64, 0:1024])
                s.copy(out=avn[:, 1024:2048], in_=ps_av[0:64, 1024:2048])
                cs.inc(s.copy(out=sumrow[:], in_=ps_av[64:65, :]))
            p.S(sav)


            def vrs(v):
                cs.wait(v)
                cg.wait(v)
                cv.inc(v.reciprocal(sumrow[:], sumrow[:]))
            p.V(vrs)

            for j in range(4):
                def tbc(t, j=j):
                    cv.wait(t)
                    ct.inc(t.matmul(ps_st[0:64, 0:512], ones_sb[0:1, 0:64],
                                    sumrow[0:1, j * 512:(j + 1) * 512], start=True, stop=True))
                p.Tn(tbc)

                def vnrm(v, h=h, j=j):
                    ct.wait(v)
                    cv.inc(v.tensor_tensor(
                        avT[(h % 2) * 64:(h % 2) * 64 + 64, h // 2, j * 512:(j + 1) * 512],
                        avn[:, j * 512:(j + 1) * 512], ps_st[0:64, 0:512], ALU.mult))
                p.V(vnrm)

        for b in range(NTB):
            def tw(t, b=b):
                cv.wait(t)
                cs.wait(t)
                cg.wait(t)
                for c in range(8):
                    for j in range(2):
                        ct.inc(t.matmul(ps_st[:, j * 512:(j + 1) * 512],
                                        avT[:, c, b * 128:(b + 1) * 128],
                                        woT_sb[:, c, j * 512:(j + 1) * 512],
                                        start=(c == 0), stop=(c == 7)))
            p.Tn(tw)

            def sw(s):
                ct.wait(s)
                cs.inc(s.copy(out=srow[:], in_=ps_st[:, 0:1024]))
            p.S(sw)

            def yw(y, b=b):
                cs.wait(y)
                cst.inc(y.dma_start(out=s_own[b * 128:(b + 1) * 128, :], in_=srow[:]), 16)
            p.Y(yw)

        def yfin(y):
            cst.wait(y)
            cl.wait(y)
        p.Y(yfin)

    return p


def _core_slices():
    return [(ci // 2, ci % 2) for ci in range(8)]


def run_l0_attn(inputs):
    x = inputs["L0_attn_x"]
    p = build_l0_attn()
    ident = np.eye(128, dtype=np.float32)
    onesc = np.ones((1, 128), dtype=np.float32)
    in_maps = []
    for (b, hh) in _core_slices():
        own = x[b, hh * TH:(hh + 1) * TH, :]
        other = x[b, (1 - hh) * TH:(1 - hh) * TH + TH, :]
        xb = np.concatenate([own, other], axis=0)
        qpos = np.arange(hh * TH, (hh + 1) * TH, dtype=np.float32)
        qposb = np.tile(qpos[None, :], (128, 1))
        rot = np.concatenate([np.arange(hh * TH, (hh + 1) * TH),
                              np.arange((1 - hh) * TH, (1 - hh) * TH + TH)]).astype(np.float32)
        kposc = rot.reshape(NB, 128).T.copy()
        in_maps.append({
            "xb": np.ascontiguousarray(xb),
            "WqT": np.ascontiguousarray(np.vstack([inputs["L0_attn_Wq"].T] * 2)),
            "WkT": np.ascontiguousarray(np.vstack([inputs["L0_attn_Wk"].T] * 2)),
            "WvT": np.ascontiguousarray(np.vstack([inputs["L0_attn_Wv"].T] * 2)),
            "WoT": np.ascontiguousarray(inputs["L0_attn_Wo"].T),
            "qposb": qposb,
            "kposc": np.ascontiguousarray(kposc),
            "ident": ident,
            "onesc": onesc,
        })
    res = run_bass_kernel_spmd(p.nc, in_maps, core_ids=list(range(8)))
    s = np.zeros((4, T, D), np.float32)
    xn = np.zeros((4, T, D), np.float32)
    for ci, (b, hh) in enumerate(_core_slices()):
        s[b, hh * TH:(hh + 1) * TH] = res.results[ci]["s_own"]
        xn[b, hh * TH:(hh + 1) * TH] = res.results[ci]["xn_own"]
    return s, xn




def build_l0_ffn_sim(parent=None, io=None, tag="C_"):
    p = Prog(parent, io, tag)
    xn_own = p.inp("xn_own", (TH, D), F32)
    wqT = p.inp("wqT", (D, 8192), F32)
    gcol = p.inp("gcol", (128, 8), F32)
    keysT = p.inp("keysT", (2, 8, 4, 128, 256), F32)
    iota256 = p.inp("iota256", (128, 256), I32)
    iota16f = p.inp("iota16f", (128, 16), F32)
    ident = p.inp("ident", (128, 128), F32)
    idx_all = p.out("idx_all", (TH, 128), I32)
    sm_all = p.out("sm_all", (TH, 128), F32)
    xnf_own = p.out("xnf_own", (TH, D), F32)

    with p.es:
        xt = p.sb("xt", (128, D), F32)
        xnf = p.sb("xnf", (128, D), F32)
        gb = p.sb("gb", (128, D), F32)
        rn = p.sb("rn", (128, NTB), F32)
        ss = p.sb("ss", (128, 1), F32)
        xnT = p.sb("xnT", (128, 8, TH), BF16)
        id_sb = p.sb("id_sb", (128, 128), F32)
        wqc = p.sb("wqc", (128, 8, 128), BF16)
        gcol_sb = p.sb("gcol_sb", (128, 8), F32)
        qTg = p.sb("qTg", (128, 4, TH), BF16)
        keys_sb = p.sb("keys_sb", (128, 4, 256), BF16)
        simH = p.sb("simH", (128, NTB, 2, 256), F32)
        idxA = p.sb("idxA", (128, NTB, 128), I32)
        smA = p.sb("smA", (128, NTB, 128), F32)
        io256 = p.sb("io256", (128, 256), I32)
        io16f = p.sb("io16f", (128, 16), F32)
        wa = p.sb("wa", (128, 256), I32)
        wmr = p.sb("wmr", (128, 256), F32)
        sp = p.sb("sp", (128, 2, 16), F32)
        ipf = p.sb("ipf", (128, 2, 16), F32)
        cs16 = p.sb("cs16", (128, 16), F32)
        pk_i = p.sb("pk_i", (128, 16), I32)
        hv = p.sb("hv", (128, 2, 16), F32)
        mm1 = p.sb("mm1", (128, 256), F32)
        sel = p.sb("sel", (128, 2, 16), F32)
        eid = p.sb("eid", (128, 16), F32)
        nmx = p.sb("nmx", (128, 1), F32)
        ssum = p.sb("ssum", (128, 1), F32)

        ps_tr = p.ps("ps_tr", (128, 128))
        ps_q = p.ps("ps_q", (128, 2048))
        ps_sim = p.ps("ps_sim", (128, 256))

        cl = p.ctr("cl")
        cg = p.ctr("cg")
        cv = p.ctr("cv")
        cs = p.ctr("cs")
        ct = p.ctr("ct")
        cst = p.ctr("cst")
        p.start()

        def y0(y):
            cl.inc(y.dma_start(out=id_sb[:], in_=ident), 16)
            cl.inc(y.dma_start(out=io256[:], in_=iota256), 16)
            cl.inc(y.dma_start(out=io16f[:], in_=iota16f), 16)
            cl.inc(y.dma_start(out=gcol_sb[:], in_=gcol), 16)
        p.Y(y0)

        def g0(g):
            for half in range(2):
                for h in range(8):
                    cg.inc(g.dma_start(out=keys_sb[:, :, :] if False else keys_sb[:],
                                       in_=keysT[half, h]), 16)
        p.G(lambda g: None)

        gb_in = p.inp("gb_in", (128, D), F32)

        def y1(y):
            cl.inc(y.dma_start(out=gb[:], in_=gb_in), 16)
        p.Y(y1)

        for b in range(NTB):
            def yb(y, b=b):
                if b > 0:
                    ct.wait(y)
                    cst.wait(y)
                cl.inc(y.dma_start(out=xt[:], in_=xb_slice(xn_own, b)), 16)
            p.Y(yb)

            def sA(s, b=b):
                cl.wait(s)
                cv.wait(s)
                cst.wait(s)
                cs.inc(s.activation(out=xnf[:], in_=xt[:], func=AF.Square, accum_out=ss[:]))
            p.S(sA)

            def vA(v, b=b):
                cs.wait(v)
                cv.chain(v, v.tensor_scalar_max(ss[:], ss[:], 1e-24))
                cv.inc(v.nop())
            p.V(vA)

            def sB(s, b=b):
                cv.wait(s)
                cs.chain(s, s.activation(out=ss[:], in_=ss[:], func=AF.Ln))
                cs.inc(s.activation(out=rn[:, b:b + 1], in_=ss[:], func=AF.Exp, scale=-0.5))
            p.S(sB)

            def vB(v, b=b):
                cs.wait(v)
                cv.chain(v, v.tensor_tensor(xnf[:], xt[:], gb[:], ALU.mult))
                cv.chain(v, v.tensor_scalar(rn[:, b:b + 1], rn[:, b:b + 1], 32.0, None, ALU.mult))
                cv.inc(v.tensor_scalar(xnf[:], xnf[:], rn[:, b:b + 1], None, ALU.mult))
            p.V(vB)

            def yo(y, b=b):
                cv.wait(y)
                cst.inc(y.dma_start(out=xnf_own[b * 128:(b + 1) * 128, :], in_=xnf[:]), 16)
            p.Y(yo)

            for c in range(8):
                def tb_(t, b=b, c=c):
                    cv.wait(t)
                    cs.wait(t)
                    ct.inc(t.transpose(ps_tr[:], xt[:, c * 128:(c + 1) * 128], id_sb[:]))
                p.Tn(tb_)

                def se(s, b=b, c=c):
                    ct.wait(s)
                    cs.inc(s.copy(out=xnT[:, c, b * 128:(b + 1) * 128], in_=ps_tr[:]))
                p.S(se)

        for h in range(8):
            for half in range(2):
                def gk(g, half=half, h=h):
                    ct.wait(g)
                    cg.inc(g.dma_start(out=keys_sb[:], in_=keysT[half, h].rearrange("i p n -> p i n")), 16)
                p.G(gk)

                for i in range(4):
                    cc = half * 32 + h * 4 + i
                    def gw(g, cc=cc):
                        ct.wait(g)
                        cg.inc(g.dma_start(
                            out=wqc[:],
                            in_=wqT[:, cc * 128:(cc + 1) * 128].rearrange("(c p) n -> p c n", p=128)), 16)
                    p.G(gw)

                    def vg(v):
                        cg.wait(v)
                        for c in range(8):
                            v.tensor_scalar(wqc[:, c, :], wqc[:, c, :], gcol_sb[:, c:c + 1], None, ALU.mult)
                        cv.inc(v.nop())
                    p.V(vg)

                    def tq(t, i=i):
                        cv.wait(t)
                        cs.wait(t)
                        for c in range(8):
                            for j in range(4):
                                ct.inc(t.matmul(ps_q[:, j * 512:(j + 1) * 512],
                                                wqc[:, c, :], xnT[:, c, j * 512:(j + 1) * 512],
                                                start=(c == 0), stop=(c == 7)))
                    p.Tn(tq)

                    def sq(s, i=i):
                        ct.wait(s)
                        s.copy(out=qTg[:, i, 0:1024], in_=ps_q[:, 0:1024])
                        cs.inc(s.copy(out=qTg[:, i, 1024:2048], in_=ps_q[:, 1024:2048]))
                    p.S(sq)

                for b in range(NTB):
                    def tsim(t, b=b):
                        cs.wait(t)
                        cg.wait(t)
                        for i in range(4):
                            ct.inc(t.matmul(ps_sim[:], qTg[:, i, b * 128:(b + 1) * 128],
                                            keys_sb[:, i, :], start=(i == 0), stop=(i == 3)))
                    p.Tn(tsim)

                    def ssim(s, half=half, b=b):
                        ct.wait(s)
                        cv.wait(s)
                        cg.wait(s)
                        cs.inc(s.activation(out=simH[:, b, half, :], in_=ps_sim[:],
                                            func=AF.Identity, scale=rn[:, b:b + 1]))
                    p.S(ssim)

            for b in range(NTB):
                def vt(v, b=b, h=h):
                    cs.wait(v)
                    for half in range(2):
                        src = simH[:, b, half, :]
                        cv.chain(v, v.tensor_scalar(wa[:], src.bitcast(I32), -256, None, ALU.bitwise_and))
                        cv.chain(v, v.tensor_tensor(wa[:], wa[:], io256[:], ALU.bitwise_or))
                        cv.chain(v, v.max(out=sp[:, half, 0:8], in_=wa[:].bitcast(F32)))
                        cv.chain(v, v.match_replace(out=wmr[:], in_to_replace=sp[:, half, 0:8],
                                                    in_values=wa[:].bitcast(F32), imm_value=-3e38))
                        cv.chain(v, v.max(out=sp[:, half, 8:16], in_=wmr[:]))
                        cv.chain(v, v.tensor_scalar(ipf[:, half, :].bitcast(I32),
                                                    sp[:, half, :].bitcast(I32), 255, None, ALU.bitwise_and))
                        cv.chain(v, v.tensor_copy(ipf[:, half, :], ipf[:, half, :].bitcast(I32)))
                    cv.chain(v, v.tensor_tensor(
                        wmr[:].rearrange("p (a c) -> p a c", c=16),
                        sp[:, 0, :].unsqueeze(2).to_broadcast([128, 16, 16]),
                        sp[:, 1, :].unsqueeze(1).to_broadcast([128, 16, 16]),
                        ALU.add))
                    cv.chain(v, v.tensor_scalar(wa[:], wmr[:].bitcast(I32), -256, None, ALU.bitwise_and))
                    cv.chain(v, v.tensor_tensor(wa[:], wa[:], io256[:], ALU.bitwise_or))
                    cv.chain(v, v.max(out=cs16[:, 0:8], in_=wa[:].bitcast(F32)))
                    cv.chain(v, v.match_replace(out=wmr[:], in_to_replace=cs16[:, 0:8],
                                                in_values=wa[:].bitcast(F32), imm_value=-3e38))
                    cv.chain(v, v.max(out=cs16[:, 8:16], in_=wmr[:]))
                    cv.chain(v, v.tensor_scalar(pk_i[:], cs16[:].bitcast(I32), 255, None, ALU.bitwise_and))
                    cv.chain(v, v.tensor_scalar(hv[:, 0, :].bitcast(I32), pk_i[:], 4, None, ALU.arith_shift_right))
                    cv.chain(v, v.tensor_copy(hv[:, 0, :], hv[:, 0, :].bitcast(I32)))
                    cv.chain(v, v.tensor_scalar(hv[:, 1, :].bitcast(I32), pk_i[:], 15, None, ALU.bitwise_and))
                    cv.chain(v, v.tensor_copy(hv[:, 1, :], hv[:, 1, :].bitcast(I32)))
                    for g_ in range(2):
                        cv.chain(v, v.tensor_tensor(
                            mm1[:].rearrange("p (a c) -> p a c", c=16),
                            hv[:, g_, :].unsqueeze(2).to_broadcast([128, 16, 16]),
                            io16f[:].unsqueeze(1).to_broadcast([128, 16, 16]),
                            ALU.is_equal))
                        cv.chain(v, v.tensor_tensor(
                            mm1[:].rearrange("p (a c) -> p a c", c=16),
                            mm1[:].rearrange("p (a c) -> p a c", c=16),
                            ipf[:, g_, :].unsqueeze(1).to_broadcast([128, 16, 16]),
                            ALU.mult))
                        cv.chain(v, v.reduce_sum(sel[:, g_, :], mm1[:].rearrange("p (a c) -> p a c", c=16), axis=AX))
                    cv.chain(v, v.tensor_scalar(eid[:], sel[:, 0, :], 256.0, None, ALU.mult))
                    cv.chain(v, v.tensor_tensor(eid[:], eid[:], sel[:, 1, :], ALU.add))
                    cv.chain(v, v.tensor_copy(idxA[:, b, h * 16:(h + 1) * 16], eid[:]))
                    cv.inc(v.reduce_max(nmx[:], cs16[:], axis=AX, negate=True))
                p.V(vt)

                def st_(s, b=b, h=h):
                    cv.wait(s)
                    cs.inc(s.activation(out=smA[:, b, h * 16:(h + 1) * 16], in_=cs16[:],
                                        func=AF.Exp, bias=nmx[:], accum_out=ssum[:]))
                p.S(st_)

                def vt2(v, b=b, h=h):
                    cs.wait(v)
                    cv.chain(v, v.reciprocal(ssum[:], ssum[:]))
                    cv.inc(v.tensor_scalar(smA[:, b, h * 16:(h + 1) * 16], smA[:, b, h * 16:(h + 1) * 16],
                                           ssum[:], None, ALU.mult))
                p.V(vt2)

        for b in range(NTB):
            def yidx(y, b=b):
                cv.wait(y)
                cst.inc(y.dma_start(out=idx_all[b * 128:(b + 1) * 128, :], in_=idxA[:, b, :]), 16)
                cst.inc(y.dma_start(out=sm_all[b * 128:(b + 1) * 128, :], in_=smA[:, b, :]), 16)
            p.Y(yidx)

        def yfin(y):
            cst.wait(y)
            cl.wait(y)
            cg.wait(y)
        p.Y(yfin)

    return p


def xb_slice(ap, b):
    return ap[b * 128:(b + 1) * 128, :]


def run_l0_ffn_sim(inputs, xn):
    p = build_l0_ffn_sim()
    keys = inputs["L0_ffn_keys"]
    keysT = np.zeros((2, 8, 4, 128, 256), np.float32)
    for half in range(2):
        for h in range(8):
            for i in range(4):
                keysT[half, h, i] = keys[h, :, half, i * 128:(i + 1) * 128].T
    gamma = inputs["L0_ffn_gamma"]
    gcol = gamma.reshape(8, 128).T.copy()
    gb_in = np.tile(gamma[None, :], (128, 1)).astype(np.float32)
    iota256 = np.tile(np.arange(256, dtype=np.int32)[None, :], (128, 1))
    iota16f = np.tile(np.arange(16, dtype=np.float32)[None, :], (128, 1))
    ident = np.eye(128, dtype=np.float32)
    in_maps = []
    for (b, hh) in _core_slices():
        in_maps.append({
            "xn_own": np.ascontiguousarray(xn[b, hh * TH:(hh + 1) * TH]),
            "wqT": np.ascontiguousarray(inputs["L0_ffn_wq"].T),
            "gcol": np.ascontiguousarray(gcol),
            "gb_in": gb_in,
            "keysT": keysT,
            "iota256": iota256,
            "iota16f": iota16f,
            "ident": ident,
        })
    res = run_bass_kernel_spmd(p.nc, in_maps, core_ids=list(range(8)))
    idx = np.zeros((4, T, 128), np.int32)
    sm = np.zeros((4, T, 128), np.float32)
    xnf = np.zeros((4, T, D), np.float32)
    for ci, (b, hh) in enumerate(_core_slices()):
        idx[b, hh * TH:(hh + 1) * TH] = res.results[ci]["idx_all"]
        sm[b, hh * TH:(hh + 1) * TH] = res.results[ci]["sm_all"]
        xnf[b, hh * TH:(hh + 1) * TH] = res.results[ci]["xnf_own"]
    return idx, sm, xnf




def build_l0_ffn_gather(parent=None, io=None, tag="D_"):
    p = Prog(parent, io, tag)
    idx_in = p.inp("idx_in", (TH, 128), I32)
    sm_in = p.inp("sm_in", (TH, 128), F32)
    xnf_in = p.inp("xnf_in", (TH, D), F32)
    s_in = p.inp("s_in", (TH, D), F32)
    x_in = p.inp("x_in", (TH, D), F32)
    tdn_sl = p.inp("tdn_sl", (8192, D), F32)
    tup_sl = p.inp("tup_sl", (8192, D), F32)
    ident = p.inp("ident", (128, 128), F32)
    x1_own = p.out("x1_own", (TH, D), F32)

    tdn_bf = p.dram("tdn_bf", (8192, D), BF16)
    tup_bf = p.dram("tup_bf", (8192, D), BF16)
    tdn_all = p.dram("tdn_all", (65536, D), BF16)
    tup_all = p.dram("tup_all", (65536, D), BF16)

    NGB = 4
    with p.es:
        id_sb = p.sb("id_sb", (128, 128), F32)
        idxt = p.sb("idxt", (128, 128), I32)
        idxf = p.sb("idxf", (128, 128), F32)
        idxT = p.sb("idxT", (128, 128), I32)
        smt = p.sb("smt", (128, 128), F32)
        xnf_bf = p.sb("xnf_bf", (128, D), BF16)
        xrow = p.sb("xrow", (128, D), F32)
        srw = p.sb("srw", (128, D), F32)
        x1t = p.sb("x1t", (128, D), F32)
        gdb = [p.sb(f"gdb{k}", (128, D), BF16) for k in range(NGB)]
        gub = [p.sb(f"gub{k}", (128, D), BF16) for k in range(NGB)]
        junk = p.sb("junk", (128, D), F32)
        htile = p.sb("htile", (128, 128), F32)
        wg = p.sb("wg", (128, 128), F32)
        wgT = p.sb("wgT", (128, 128), BF16)
        zb = [p.sb(f"zb{k}", (128, 64), BF16) for k in range(64)]
        gout = p.sb("gout", (128, D), F32)

        ps_tr = p.ps("ps_tr", (128, 128))
        ps_u0 = p.ps("ps_u0", (64, 512))
        ps_u1 = p.ps("ps_u1", (64, 512))

        cl = p.ctr("cl")
        cg = p.ctr("cg")
        cv = p.ctr("cv")
        cs = p.ctr("cs")
        ct = p.ctr("ct")
        cst = p.ctr("cst")
        cc1 = p.ctr("cc1")
        cc2 = p.ctr("cc2")
        p.start()

        def gtab(g):
            cg.inc(g.dma_start(out=tdn_bf, in_=tdn_sl), 16)
            cg.inc(g.dma_start(out=tup_bf, in_=tup_sl), 16)
            cg.wait(g)
            g.collective_compute("AllGather", ALU.bypass,
                                 replica_groups=[list(range(8))],
                                 ins=[tdn_bf], outs=[tdn_all]).then_inc(cc1.sem, 1)
            g.collective_compute("AllGather", ALU.bypass,
                                 replica_groups=[list(range(8))],
                                 ins=[tup_bf], outs=[tup_all]).then_inc(cc2.sem, 1)
            g.wait_ge(cc1.sem, 1)
            g.wait_ge(cc2.sem, 1)
            cg.inc(g.nop())
        p.G(gtab)
        TABLES_READY = cg.n

        def y0(y):
            cl.inc(y.dma_start(out=id_sb[:], in_=ident), 16)
        p.Y(y0)

        def vz(v):
            for k in range(64):
                v.memset(zb[k][:], 0.0)
            cv.inc(v.nop())
        p.V(vz)

        for b in range(NTB):
            def yb(y, b=b):
                if b > 0:
                    cv.wait(y)
                    ct.wait(y)
                    cst.wait(y)
                    cg.wait(y)
                cl.inc(y.dma_start(out=idxt[:], in_=idx_in[b * 128:(b + 1) * 128, :]), 16)
                cl.inc(y.dma_start(out=smt[:], in_=sm_in[b * 128:(b + 1) * 128, :]), 16)
                cl.inc(y.dma_start(out=xrow[:], in_=x_in[b * 128:(b + 1) * 128, :]), 16)
                cl.inc(y.dma_start(out=srw[:], in_=s_in[b * 128:(b + 1) * 128, :]), 16)
            p.Y(yb)

            def gx(g, b=b):
                if b > 0:
                    cv.wait(g)
                cg.inc(g.dma_start(out=xnf_bf[:], in_=xnf_in[b * 128:(b + 1) * 128, :]), 16)
            p.G(gx)

            def vi(v, b=b):
                cl.wait(v)
                cv.inc(v.tensor_copy(idxf[:], idxt[:]))
            p.V(vi)

            def ti(t, b=b):
                cv.wait(t)
                cs.wait(t)
                ct.inc(t.transpose(ps_tr[:], idxf[:], id_sb[:]))
            p.Tn(ti)

            def si(s, b=b):
                ct.wait(s)
                cs.inc(s.copy(out=idxT[:].bitcast(F32) if False else idxT[:], in_=ps_tr[:]))
            p.S(si)

            base_g = cg.n
            for s_ in range(128):
                def gd(g, b=b, s_=s_, base_g=base_g):
                    cl.wait(g)
                    if s_ >= NGB:
                        g.wait_ge(cv.sem, VTRACK[0] - (NGB - 1))
                    cg.inc(g.indirect_dma_start(gdb[s_ % NGB][:], None, tdn_all,
                                                IndirectOffsetOnAxis(idxt[:, s_:s_ + 1], 0)), 16)
                p.G(gd)

                def vd_(v, b=b, s_=s_, base_g=base_g):
                    v.wait_ge(cg.sem, base_g + (s_ + 1) * 16)
                    if s_ == 0:
                        cg.wait(v)
                    cv.inc(v.scalar_tensor_tensor(junk[:], gdb[s_ % NGB][:], 1.0, xnf_bf[:],
                                                  ALU.mult, ALU.mult,
                                                  accum_out=htile[:, s_:s_ + 1]))
                    VTRACK[0] = cv.n
                p.V(vd_)

            def sg(s, b=b):
                cv.wait(s)
                cs.inc(s.activation(out=wg[:], in_=htile[:], func=AF.Gelu))
            p.S(sg)

            def vw_(v, b=b):
                cs.wait(v)
                cv.inc(v.tensor_tensor(wg[:], wg[:], smt[:], ALU.mult))
            p.V(vw_)

            def tw_(t, b=b):
                cv.wait(t)
                cs.wait(t)
                ct.inc(t.transpose(ps_tr[:], wg[:], id_sb[:]))
            p.Tn(tw_)

            def sw_(s, b=b):
                ct.wait(s)
                cs.inc(s.copy(out=wgT[:], in_=ps_tr[:]))
            p.S(sw_)

            base_u = cg.n
            for j in range(128):
                def gu_(g, b=b, j=j, base_u=base_u):
                    cs.wait(g) if j == 0 else None
                    if j >= NGB:
                        g.wait_ge(ct.sem, TTRACK[0] - 2 * (NGB - 1))
                    cg.inc(g.indirect_dma_start(gub[j % NGB][:], None, tup_all,
                                                IndirectOffsetOnAxis(idxT[:, j:j + 1], 0)), 16)
                p.G(gu_)

                def vzc(v, b=b, j=j):
                    if j == 0:
                        cs.wait(v)
                    if j >= 64:
                        v.wait_ge(ct.sem, TTRACK[0])
                    cv.inc(v.tensor_copy(zb[j % 64][:, (j % 64):(j % 64) + 1],
                                         wgT[:, j:j + 1]))
                p.V(vzc)

                def tu(t, b=b, j=j, base_u=base_u):
                    t.wait_ge(cg.sem, base_u + (j + 1) * 16)
                    cv.wait(t)
                    if j % 64 == 0:
                        cs.wait(t)
                    half_idx = j % 64
                    ct.inc(t.matmul(ps_u0[:], zb[j % 64][:], gub[j % NGB][:, 0:512],
                                    start=(half_idx == 0), stop=(half_idx == 63)))
                    ct.inc(t.matmul(ps_u1[:], zb[j % 64][:], gub[j % NGB][:, 512:1024],
                                    start=(half_idx == 0), stop=(half_idx == 63)))
                    TTRACK[0] = ct.n
                p.Tn(tu)

                if j % 64 == 63:
                    grp = j // 64
                    def su(s, b=b, grp=grp):
                        ct.wait(s)
                        if grp == 0:
                            cv.wait(s)
                        r0 = grp * 64
                        s.activation(out=gout[r0:r0 + 64, 0:512], in_=ps_u0[:], func=AF.Silu)
                        cs.inc(s.activation(out=gout[r0:r0 + 64, 512:1024], in_=ps_u1[:], func=AF.Silu))
                    p.S(su)

            def vx(v, b=b):
                cs.wait(v)
                cv.chain(v, v.tensor_tensor(x1t[:], srw[:], gout[:], ALU.mult))
                cv.inc(v.tensor_tensor(x1t[:], x1t[:], xrow[:], ALU.add))
            p.V(vx)

            def yx(y, b=b):
                cv.wait(y)
                cst.inc(y.dma_start(out=x1_own[b * 128:(b + 1) * 128, :], in_=x1t[:]), 16)
            p.Y(yx)

        def yfin(y):
            cst.wait(y)
            cl.wait(y)
            cg.wait(y)
        p.Y(yfin)

    return p


VTRACK = [0]
TTRACK = [0]


def run_l0_ffn_gather(inputs, idx, sm, xnf, s0, x):
    p = build_l0_ffn_gather()
    ident = np.eye(128, dtype=np.float32)
    in_maps = []
    for ci, (b, hh) in enumerate(_core_slices()):
        in_maps.append({
            "idx_in": np.ascontiguousarray(idx[b, hh * TH:(hh + 1) * TH]),
            "sm_in": np.ascontiguousarray(sm[b, hh * TH:(hh + 1) * TH]),
            "xnf_in": np.ascontiguousarray(xnf[b, hh * TH:(hh + 1) * TH]),
            "s_in": np.ascontiguousarray(s0[b, hh * TH:(hh + 1) * TH]),
            "x_in": np.ascontiguousarray(x[b, hh * TH:(hh + 1) * TH]),
            "tdn_sl": np.ascontiguousarray(inputs["L0_ffn_table_down"][ci * 8192:(ci + 1) * 8192]),
            "tup_sl": np.ascontiguousarray(inputs["L0_ffn_table_up"][ci * 8192:(ci + 1) * 8192]),
            "ident": ident,
        })
    res = run_bass_kernel_spmd(p.nc, in_maps, core_ids=list(range(8)))
    x1 = np.zeros((4, T, D), np.float32)
    for ci, (b, hh) in enumerate(_core_slices()):
        x1[b, hh * TH:(hh + 1) * TH] = res.results[ci]["x1_own"]
    return x1




def build_rwkv(parent=None, io=None, tag="E_"):
    p = Prog(parent, io, tag)
    x1_in = p.inp("x1_in", (TH, D), F32)
    x1p_in = p.inp("x1p_in", (TH, D), F32)
    WkTc = p.inp("WkTc", (D, D), F32)
    WvTc = p.inp("WvTc", (D, D), F32)
    WrTc = p.inp("WrTc", (D, D), F32)
    WoTc = p.inp("WoTc", (D, D), F32)
    mixk_b = p.inp("mixk_b", (128, D), F32)
    mixv_b = p.inp("mixv_b", (128, D), F32)
    mixr_b = p.inp("mixr_b", (128, D), F32)
    td_b = p.inp("td_b", (128, D), F32)
    tf_b = p.inp("tf_b", (128, D), F32)
    jcol = p.inp("jcol", (128, 1), F32)
    tprefix = p.inp("tprefix", (128, 128), F32)
    rolem = p.inp("rolem", (1, D), F32)
    ident = p.inp("ident", (128, 128), F32)
    s1_own = p.out("s1_own", (TH, D), F32)

    sp_n = p.dram("sp_n", (TH, D), F32)
    sp_d = p.dram("sp_d", (TH, D), F32)
    sp_a = p.dram("sp_a", (TH, D), F32)
    sp_b = p.dram("sp_b", (TH, D), F32)
    ag_in = p.dram("ag_in", (2, D), F32)
    ag_out = p.dram("ag_out", (4, D), F32)

    with p.es:
        id_sb = p.sb("id_sb", (128, 128), F32)
        tpre = p.sb("tpre", (128, 128), F32)
        xa = p.sb("xa", (128, D), F32)
        xb_ = p.sb("xb_", (128, D), F32)
        tmp = p.sb("tmp", (128, D), F32)
        tmp2 = p.sb("tmp2", (128, D), F32)
        mc = p.sb("mc", (128, 1), F32)
        rc = p.sb("rc", (128, 1), F32)
        mixk = p.sb("mixk", (128, D), F32)
        mixv = p.sb("mixv", (128, D), F32)
        mixr = p.sb("mixr", (128, D), F32)
        wtile = p.sb("wtile", (128, D), F32)
        utile = p.sb("utile", (128, D), F32)
        lj = p.sb("lj", (128, D), F32)
        ljm = p.sb("ljm", (128, D), F32)
        l128 = p.sb("l128", (1, D), F32)
        l127 = p.sb("l127", (1, D), F32)
        jc = p.sb("jc", (128, 1), F32)
        wk_sb = p.sb("wk_sb", (128, 8, D), BF16)
        wv_sb = p.sb("wv_sb", (128, 8, D), BF16)
        wr_sb = p.sb("wr_sb", (128, 8, D), BF16)
        wo_sb = p.sb("wo_sb", (128, 8, D), BF16)
        kinT = p.sb("kinT", (128, 8, 128), BF16)
        krow = p.sb("krow", (128, D), F32)
        vrow_ = p.sb("vrow_", (128, D), F32)
        rbuf = p.sb("rbuf", (128, NTB, D), BF16)
        alpha = p.sb("alpha", (128, D), F32)
        beta = p.sb("beta", (128, D), F32)
        sn = p.sb("sn", (128, D), F32)
        sd_ = p.sb("sd_", (128, D), F32)
        rrowst = p.sb("rrowst", (1, D), F32)
        rrowst2 = p.sb("rrowst2", (1, D), F32)
        initr = p.sb("initr", (1, D), F32)
        initr2 = p.sb("initr2", (1, D), F32)
        rolesb = p.sb("rolesb", (1, D), F32)
        ones_cst = p.sb("ones_cst", (1, 128), F32)
        rw = p.sb("rw", (128, D), F32)
        rwT = p.sb("rwT", (128, 8, 128), BF16)
        srow_ = p.sb("srow_", (128, D), F32)

        ps_tr = p.ps("ps_tr", (128, 128))
        ps_kv = p.ps("ps_kv", (128, 1024))
        ps_p1 = p.ps("ps_p1", (128, 1024))
        ps_p2 = p.ps("ps_p2", (128, 1024))

        cl = p.ctr("cl")
        cg = p.ctr("cg")
        cv = p.ctr("cv")
        cs = p.ctr("cs")
        ct = p.ctr("ct")
        cst = p.ctr("cst")
        ccg = p.ctr("ccg")
        p.start()

        def y0(y):
            cl.inc(y.dma_start(out=id_sb[:], in_=ident), 16)
            cl.inc(y.dma_start(out=tpre[:], in_=tprefix), 16)
            cl.inc(y.dma_start(out=mixk[:], in_=mixk_b), 16)
            cl.inc(y.dma_start(out=mixv[:], in_=mixv_b), 16)
            cl.inc(y.dma_start(out=mixr[:], in_=mixr_b), 16)
            cl.inc(y.dma_start(out=wtile[:], in_=td_b), 16)
            cl.inc(y.dma_start(out=utile[:], in_=tf_b), 16)
            cl.inc(y.dma_start(out=jc[:], in_=jcol), 16)
            cl.inc(y.dma_start(out=rolesb[:], in_=rolem), 16)
        p.Y(y0)

        def g0(g):
            for c in range(8):
                cg.inc(g.dma_start(out=wk_sb[:, c, :], in_=WkTc[c * 128:(c + 1) * 128, :]), 16)
                cg.inc(g.dma_start(out=wv_sb[:, c, :], in_=WvTc[c * 128:(c + 1) * 128, :]), 16)
                cg.inc(g.dma_start(out=wr_sb[:, c, :], in_=WrTc[c * 128:(c + 1) * 128, :]), 16)
                cg.inc(g.dma_start(out=wo_sb[:, c, :], in_=WoTc[c * 128:(c + 1) * 128, :]), 16)
        p.G(g0)

        def s0_(s):
            cl.wait(s)
            cs.inc(s.activation(out=wtile[:], in_=wtile[:], func=AF.Exp))
        p.S(s0_)

        def v0(v):
            cs.wait(v)
            cv.chain(v, v.tensor_scalar(wtile[:], wtile[:], -1.0, None, ALU.mult))
            cv.chain(v, v.tensor_scalar(lj[:], wtile[:], jc[:], None, ALU.mult))
            cv.chain(v, v.tensor_scalar(tmp[:, 0:1], jc[:], -1.0, 1.0, ALU.mult, ALU.add))
            cv.chain(v, v.tensor_scalar(tmp[:, 0:1], jc[:], 1.0, -1.0, ALU.mult, ALU.add))
            cv.inc(v.tensor_scalar(ljm[:], wtile[:], tmp[:, 0:1], None, ALU.mult))
        p.V(v0)

        def s1_(s):
            cv.wait(s)
            s.activation(out=lj[:], in_=lj[:], func=AF.Exp)
            cs.chain(s, s.activation(out=ljm[:], in_=ljm[:], func=AF.Exp))
            cs.chain(s, s.activation(out=l128[:], in_=wtile[0:1, :], func=AF.Exp, scale=128.0))
            cs.chain(s, s.activation(out=l127[:], in_=wtile[0:1, :], func=AF.Exp, scale=127.0))
            cs.inc(s.activation(out=utile[:], in_=utile[:], func=AF.Exp))
        p.S(s1_)

        def v1(v):
            cs.wait(v)
            v.memset(rrowst[:], 0.0)
            v.memset(rrowst2[:], 0.0)
            cv.inc(v.memset(ones_cst[:], 1.0))
        p.V(v1)

        for c in range(NTB):
            def yb(y, c=c):
                if c > 0:
                    cv.wait(y)
                    ct.wait(y)
                cl.inc(y.dma_start(out=xa[:], in_=x1_in[c * 128:(c + 1) * 128, :]), 16)
                cl.inc(y.dma_start(out=xb_[:], in_=x1p_in[c * 128:(c + 1) * 128, :]), 16)
            p.Y(yb)

            def vw_(v, c=c):
                cl.wait(v)
                cv.inc(v.nop())
            p.V(vw_)
            ln_emit(p, xa[:], xa[:], tmp[:], mc[:], rc[:], cv, cs)
            ln_emit(p, xb_[:], xb_[:], tmp[:], mc[:], rc[:], cv, cs)

            for which, mx, wsb in (("r", mixr, wr_sb), ("k", mixk, wk_sb), ("v", mixv, wv_sb)):
                def vmix(v, which=which, mx=mx, c=c):
                    cs.wait(v)
                    cv.chain(v, v.tensor_tensor(tmp[:], xa[:], xb_[:], ALU.subtract))
                    cv.chain(v, v.tensor_tensor(tmp[:], tmp[:], mx[:], ALU.mult))
                    cv.inc(v.tensor_tensor(tmp[:], tmp[:], xb_[:], ALU.add))
                p.V(vmix)

                for cc in range(8):
                    def ttr(t, cc=cc, which=which):
                        cv.wait(t)
                        cs.wait(t)
                        ct.inc(t.transpose(ps_tr[:], tmp[:, cc * 128:(cc + 1) * 128], id_sb[:]))
                    p.Tn(ttr)

                    def str_(s, cc=cc):
                        ct.wait(s)
                        cs.inc(s.copy(out=kinT[:, cc, :], in_=ps_tr[:]))
                    p.S(str_)

                def tmm(t, which=which, c=c):
                    cs.wait(t)
                    cg.wait(t)
                    wsb_ = {"r": wr_sb, "k": wk_sb, "v": wv_sb}[which]
                    for cc in range(8):
                        for j in range(2):
                            ct.inc(t.matmul(ps_kv[:, j * 512:(j + 1) * 512],
                                            kinT[:, cc, :], wsb_[:, cc, j * 512:(j + 1) * 512],
                                            start=(cc == 0), stop=(cc == 7)))
                p.Tn(tmm)

                if which == "r":
                    def sre(s, c=c):
                        ct.wait(s)
                        cs.inc(s.copy(out=rbuf[:, c, :], in_=ps_kv[:, 0:1024]))
                    p.S(sre)
                elif which == "k":
                    def ske(s, c=c):
                        ct.wait(s)
                        cs.inc(s.activation(out=krow[:], in_=ps_kv[:, 0:1024], func=AF.Exp))
                    p.S(ske)
                else:
                    def sve(s, c=c):
                        ct.wait(s)
                        cs.inc(s.copy(out=vrow_[:], in_=ps_kv[:, 0:1024]))
                    p.S(sve)

            def va(v, c=c):
                cs.wait(v)
                cv.chain(v, v.reciprocal(tmp2[:], lj[:]))
                cv.chain(v, v.tensor_tensor(beta[:], krow[:], tmp2[:], ALU.mult))
                cv.chain(v, v.tensor_tensor(alpha[:], beta[:], vrow_[:], ALU.mult))
                cv.inc(v.nop())
            p.V(va)

            def gsp(g, c=c):
                cv.wait(g)
                cg.inc(g.dma_start(out=sp_b[c * 128:(c + 1) * 128, :], in_=krow[:]), 16)
                cg.inc(g.dma_start(out=sp_a[c * 128:(c + 1) * 128, :], in_=tmp[:]) if False else
                       g.dma_start(out=sp_a[c * 128:(c + 1) * 128, :], in_=vrow_[:]), 16)
            p.G(gsp)

            def tpm(t, c=c):
                cv.wait(t)
                cs.wait(t)
                for j in range(2):
                    ct.inc(t.matmul(ps_p1[:, j * 512:(j + 1) * 512], tpre[:],
                                    alpha[:, j * 512:(j + 1) * 512], start=True, stop=True))
                    ct.inc(t.matmul(ps_p2[:, j * 512:(j + 1) * 512], tpre[:],
                                    beta[:, j * 512:(j + 1) * 512], start=True, stop=True))
            p.Tn(tpm)

            def vsl(v, c=c):
                ct.wait(v)
                cv.chain(v, v.tensor_tensor(sn[:], ps_p1[:, 0:1024], ljm[:], ALU.mult))
                cv.chain(v, v.tensor_tensor(sd_[:], ps_p2[:, 0:1024], ljm[:], ALU.mult))
                cv.chain(v, v.memset(sn[0:1, :], 0.0))
                cv.chain(v, v.memset(sd_[0:1, :], 0.0))
                cv.inc(v.nop())
            p.V(vsl)

            def tbc(t, c=c):
                cv.wait(t)
                for j in range(2):
                    ct.inc(t.matmul(ps_kv[:, j * 512:(j + 1) * 512], ones_cst[0:1, :],
                                    rrowst[0:1, j * 512:(j + 1) * 512], start=True, stop=True))
            p.Tn(tbc)

            def vsl2(v, c=c):
                ct.wait(v)
                cv.chain(v, v.tensor_tensor(tmp2[:], ps_kv[:, 0:1024], lj[:], ALU.mult))
                cv.chain(v, v.tensor_tensor(sn[:], sn[:], tmp2[:], ALU.add))
                cv.inc(v.nop())
            p.V(vsl2)

            def tbc2(t, c=c):
                cv.wait(t)
                for j in range(2):
                    ct.inc(t.matmul(ps_kv[:, j * 512:(j + 1) * 512], ones_cst[0:1, :],
                                    rrowst2[0:1, j * 512:(j + 1) * 512], start=True, stop=True))
            p.Tn(tbc2)

            def vsl3(v, c=c):
                ct.wait(v)
                cv.chain(v, v.tensor_tensor(tmp2[:], ps_kv[:, 0:1024], lj[:], ALU.mult))
                cv.chain(v, v.tensor_tensor(sd_[:], sd_[:], tmp2[:], ALU.add))
                cv.inc(v.nop())
            p.V(vsl3)

            def gsp2(g, c=c):
                cv.wait(g)
                cg.inc(g.dma_start(out=sp_n[c * 128:(c + 1) * 128, :], in_=sn[:]), 16)
                cg.inc(g.dma_start(out=sp_d[c * 128:(c + 1) * 128, :], in_=sd_[:]), 16)
            p.G(gsp2)

            def vru(v, c=c):
                ct.wait(v)
                cv.chain(v, v.tensor_tensor(rrowst[0:1, :], rrowst[0:1, :], l128[:], ALU.mult))
                cv.chain(v, v.tensor_tensor(tmp[0:1, :], ps_p1[0:1, 0:1024], l127[:], ALU.mult))
                cv.chain(v, v.tensor_tensor(rrowst[0:1, :], rrowst[0:1, :], tmp[0:1, :], ALU.add))
                cv.chain(v, v.tensor_tensor(rrowst2[0:1, :], rrowst2[0:1, :], l128[:], ALU.mult))
                cv.chain(v, v.tensor_tensor(tmp[0:1, :], ps_p2[0:1, 0:1024], l127[:], ALU.mult))
                cv.inc(v.tensor_tensor(rrowst2[0:1, :], rrowst2[0:1, :], tmp[0:1, :], ALU.add))
            p.V(vru)

        def gag(g):
            cv.wait(g)
            cg.inc(g.dma_start(out=ag_in[0:1, :], in_=rrowst[0:1, :]), 16)
            cg.inc(g.dma_start(out=ag_in[1:2, :], in_=rrowst2[0:1, :]), 16)
            cg.wait(g)
            g.collective_compute("AllGather", ALU.bypass,
                                 replica_groups=[[0, 1], [2, 3], [4, 5], [6, 7]],
                                 ins=[ag_in], outs=[ag_out]).then_inc(ccg.sem, 1)
            g.wait_ge(ccg.sem, 1)
            cg.inc(g.dma_start(out=initr[:], in_=ag_out[0:1, :]), 16)
            cg.inc(g.dma_start(out=initr2[:], in_=ag_out[1:2, :]), 16)
        p.G(gag)

        def vinit(v):
            cg.wait(v)
            cl.wait(v)
            cv.chain(v, v.tensor_tensor(initr[0:1, :], initr[0:1, :], rolesb[:], ALU.mult))
            cv.inc(v.tensor_tensor(initr2[0:1, :], initr2[0:1, :], rolesb[:], ALU.mult))
        p.V(vinit)

        for c in range(NTB):
            def yb2(y, c=c):
                cv.wait(y)
                ct.wait(y)
                cl.inc(y.dma_start(out=sn[:], in_=sp_n[c * 128:(c + 1) * 128, :]), 16)
                cl.inc(y.dma_start(out=sd_[:], in_=sp_d[c * 128:(c + 1) * 128, :]), 16)
                cl.inc(y.dma_start(out=krow[:], in_=sp_b[c * 128:(c + 1) * 128, :]), 16)
                cl.inc(y.dma_start(out=vrow_[:], in_=sp_a[c * 128:(c + 1) * 128, :]), 16)
            p.Y(yb2)

            if c > 0:
                def vic(v, c=c):
                    cv.wait(v)
                    cv.chain(v, v.tensor_tensor(initr[0:1, :], initr[0:1, :], l128[:], ALU.mult))
                    cv.inc(v.tensor_tensor(initr2[0:1, :], initr2[0:1, :], l128[:], ALU.mult))
                p.V(vic)

            def tbc3(t, c=c):
                cv.wait(t)
                cs.wait(t)
                for j in range(2):
                    ct.inc(t.matmul(ps_p1[:, j * 512:(j + 1) * 512], ones_cst[0:1, :],
                                    initr[0:1, j * 512:(j + 1) * 512], start=True, stop=True))
                    ct.inc(t.matmul(ps_p2[:, j * 512:(j + 1) * 512], ones_cst[0:1, :],
                                    initr2[0:1, j * 512:(j + 1) * 512], start=True, stop=True))
            p.Tn(tbc3)

            def vwk(v, c=c):
                cl.wait(v)
                ct.wait(v)
                cv.chain(v, v.tensor_tensor(tmp[:], ps_p1[:, 0:1024], lj[:], ALU.mult))
                cv.chain(v, v.tensor_tensor(sn[:], sn[:], tmp[:], ALU.add))
                cv.chain(v, v.tensor_tensor(tmp[:], ps_p2[:, 0:1024], lj[:], ALU.mult))
                cv.chain(v, v.tensor_tensor(sd_[:], sd_[:], tmp[:], ALU.add))
                cv.chain(v, v.tensor_tensor(tmp[:], krow[:], utile[:], ALU.mult))
                cv.chain(v, v.tensor_tensor(sd_[:], sd_[:], tmp[:], ALU.add))
                cv.chain(v, v.tensor_tensor(tmp[:], tmp[:], vrow_[:], ALU.mult))
                cv.chain(v, v.tensor_tensor(sn[:], sn[:], tmp[:], ALU.add))
                cv.chain(v, v.reciprocal(tmp2[:], sd_[:]))
                cv.inc(v.tensor_tensor(rw[:], sn[:], tmp2[:], ALU.mult))
            p.V(vwk)

            def ssg(s, c=c):
                cv.wait(s)
                cs.inc(s.activation(out=tmp[:], in_=rbuf[:, c, :], func=AF.Sigmoid))
            p.S(ssg)

            def vrw(v, c=c):
                cs.wait(v)
                cv.inc(v.tensor_tensor(rw[:], rw[:], tmp[:], ALU.mult))
            p.V(vrw)

            for cc in range(8):
                def ttr2(t, cc=cc, c=c):
                    cv.wait(t)
                    cs.wait(t)
                    ct.inc(t.transpose(ps_tr[:], rw[:, cc * 128:(cc + 1) * 128], id_sb[:]))
                p.Tn(ttr2)

                def str2(s, cc=cc):
                    ct.wait(s)
                    cs.inc(s.copy(out=rwT[:, cc, :], in_=ps_tr[:]))
                p.S(str2)

            def two(t, c=c):
                cs.wait(t)
                for cc in range(8):
                    for j in range(2):
                        ct.inc(t.matmul(ps_kv[:, j * 512:(j + 1) * 512],
                                        rwT[:, cc, :], wo_sb[:, cc, j * 512:(j + 1) * 512],
                                        start=(cc == 0), stop=(cc == 7)))
            p.Tn(two)

            def sso(s, c=c):
                ct.wait(s)
                cs.inc(s.copy(out=srow_[:], in_=ps_kv[:, 0:1024]))
            p.S(sso)

            def yso(y, c=c):
                cs.wait(y)
                cst.inc(y.dma_start(out=s1_own[c * 128:(c + 1) * 128, :], in_=srow_[:]), 16)
            p.Y(yso)

        def yfin(y):
            cst.wait(y)
            cl.wait(y)
            cg.wait(y)
        p.Y(yfin)

    return p


def run_rwkv(inputs, x1):
    p = build_rwkv()
    ident = np.eye(128, dtype=np.float32)
    jcol = np.arange(128, dtype=np.float32).reshape(128, 1)
    tpre = np.zeros((128, 128), np.float32)
    tpre[:, 0] = 1.0
    for j in range(1, 128):
        tpre[:j, j] = 1.0
    tile = lambda vv: np.tile(np.asarray(vv, np.float32).reshape(1, -1), (128, 1))
    in_maps = []
    for (b, hh) in _core_slices():
        own = x1[b, hh * TH:(hh + 1) * TH]
        if hh == 0:
            prev = np.concatenate([np.zeros((1, D), np.float32), own[:-1]], axis=0)
        else:
            prev = x1[b, hh * TH - 1:(hh + 1) * TH - 1]
        role = np.full((1, D), 1.0 if hh == 1 else 0.0, np.float32)
        in_maps.append({
            "x1_in": np.ascontiguousarray(own),
            "x1p_in": np.ascontiguousarray(prev),
            "WkTc": np.ascontiguousarray(inputs["L1_attn_Wk"].T),
            "WvTc": np.ascontiguousarray(inputs["L1_attn_Wv"].T),
            "WrTc": np.ascontiguousarray(inputs["L1_attn_Wr"].T),
            "WoTc": np.ascontiguousarray(inputs["L1_attn_Wo"].T),
            "mixk_b": tile(inputs["L1_attn_mix_k"]),
            "mixv_b": tile(inputs["L1_attn_mix_v"]),
            "mixr_b": tile(inputs["L1_attn_mix_r"]),
            "td_b": tile(inputs["L1_attn_time_decay"]),
            "tf_b": tile(inputs["L1_attn_time_first"]),
            "jcol": jcol,
            "tprefix": tpre,
            "rolem": role,
            "ident": ident,
        })
    res = run_bass_kernel_spmd(p.nc, in_maps, core_ids=list(range(8)))
    s1 = np.zeros((4, T, D), np.float32)
    for ci, (b, hh) in enumerate(_core_slices()):
        s1[b, hh * TH:(hh + 1) * TH] = res.results[ci]["s1_own"]
    return s1




NKB = 18
NPAD = 2432

def build_l1_attn(parent=None, io=None, tag="Fa_"):
    p = Prog(parent, io, tag)
    st_in = p.inp("st_in", (NPAD, D), F32)
    Wqkv_in = p.inp("Wqkv_in", (D, 3072), F32)
    Wo_in = p.inp("Wo_in", (D, D), F32)
    qs_t = p.inp("qs_t", (128, D), F32)
    ks_t = p.inp("ks_t", (128, D), F32)
    cq_t = p.inp("cq_t", (NKB * 128, DH), F32)
    sq_t = p.inp("sq_t", (NKB * 128, DH), F32)
    ck_t = p.inp("ck_t", (NKB * 128, DH), F32)
    sk_t = p.inp("sk_t", (NKB * 128, DH), F32)
    m_first = p.inp("m_first", (128, 2, 512), F32)
    m_rest = p.inp("m_rest", (128, 2, 512), F32)
    ident = p.inp("ident", (128, 128), F32)
    st_out = p.out("st_out", (TH, D), F32)

    with p.es:
        id_sb = p.sb("id_sb", (128, 128), F32)
        xs = p.sb("xs", (128, D), F32)
        xsT = p.sb("xsT", (128, 8, 128), BF16)
        qk = p.sb("qk", (128, 2048), F32)
        ss2 = p.sb("ss2", (128, 32), F32)
        rn2 = p.sb("rn2", (128, 32), F32)
        qs_sb = p.sb("qs_sb", (128, D), F32)
        ks_sb = p.sb("ks_sb", (128, D), F32)
        rotb = p.sb("rotb", (128, 2, 4, DH), F32)
        tmpa = p.sb("tmpa", (128, D), F32)
        tmpb = p.sb("tmpb", (128, D), F32)
        qT2 = p.sb("qT2", (128, 8, TH), BF16)
        kT2 = p.sb("kT2", (128, 8, NKB * 128), BF16)
        vr = p.sb("vr", (128, NKB, 1024), BF16)
        wq_sb = p.sb("wq_sb", (128, 8, 2048), BF16)
        wo_sb = p.sb("wo_sb", (128, 8, D), BF16)
        mf_sb = p.sb("mf_sb", (128, 2, 512), F32)
        mr_sb = p.sb("mr_sb", (128, 2, 512), F32)
        prow2 = p.sb("prow2", (128, 2, 512), F32)
        ptb2 = p.sb("ptb2", (128, 2, 4, 128), BF16)
        nmx2 = p.sb("nmx2", (128, 2), F32)
        ssum2 = p.sb("ssum2", (128, 2), F32)
        avf2 = p.sb("avf2", (128, 2, DH), F32)
        avT = p.sb("avT", (128, 8, 256), BF16)
        orow = tmpa
        strow = tmpb

        ps_b = p.ps("ps_b", (128, 2048))
        ps_tr4 = p.ps("ps_tr4", (128, 1024))
        ps_av = p.ps("ps_av", (128, 512))

        cl = p.ctr("cl")
        cg = p.ctr("cg")
        cv = p.ctr("cv")
        cs = p.ctr("cs")
        ct = p.ctr("ct")
        cst = p.ctr("cst")
        p.start()

        def y0(y):
            cl.inc(y.dma_start(out=id_sb[:], in_=ident), 16)
            cl.inc(y.dma_start(out=qs_sb[:], in_=qs_t), 16)
            cl.inc(y.dma_start(out=ks_sb[:], in_=ks_t), 16)
            cl.inc(y.dma_start(out=mf_sb[:], in_=m_first), 16)
            cl.inc(y.dma_start(out=mr_sb[:], in_=m_rest), 16)
        p.Y(y0)

        def g0(g):
            for c in range(8):
                cg.inc(g.dma_start(out=wo_sb[:, c, :], in_=Wo_in[c * 128:(c + 1) * 128, :]), 16)
        p.G(g0)

        for which in ("qk", "v"):
            def gw(g, which=which):
                ct.wait(g)
                for c in range(8):
                    if which == "qk":
                        cg.inc(g.dma_start(out=wq_sb[:, c, 0:2048],
                                           in_=Wqkv_in[c * 128:(c + 1) * 128, 0:2048]), 16)
                    else:
                        cg.inc(g.dma_start(out=wq_sb[:, c, 0:1024],
                                           in_=Wqkv_in[c * 128:(c + 1) * 128, 2048:3072]), 16)
            p.G(gw)

            QMK = {}
            for mb in range(NKB):
                def yx(y, mb=mb, which=which):
                    if mb > 0:
                        y.wait_ge(ct.sem, QMK[(mb - 1, "ttr")])
                    else:
                        ct.wait(y)
                    cl.inc(y.dma_start(out=xs[:, 0:512],
                                       in_=st_in[mb * 128 + 1:mb * 128 + 129, 0:512]), 16)
                    cl.inc(y.dma_start(out=xs[:, 512:1024],
                                       in_=st_in[mb * 128:mb * 128 + 128, 512:1024]), 16)
                    if which == "qk":
                        pr = mb % 2
                        cl.inc(y.dma_start(out=rotb[:, pr, 0, :], in_=cq_t[mb * 128:(mb + 1) * 128, :]), 16)
                        cl.inc(y.dma_start(out=rotb[:, pr, 1, :], in_=sq_t[mb * 128:(mb + 1) * 128, :]), 16)
                        cl.inc(y.dma_start(out=rotb[:, pr, 2, :], in_=ck_t[mb * 128:(mb + 1) * 128, :]), 16)
                        cl.inc(y.dma_start(out=rotb[:, pr, 3, :], in_=sk_t[mb * 128:(mb + 1) * 128, :]), 16)
                p.Y(yx)

                for g4 in range(2):
                    def ttr(t, g4=g4, mb=mb, which=which):
                        cl.wait(t)
                        if g4 == 0:
                            if mb == 0:
                                cs.wait(t)
                            else:
                                key = (mb - 1, "sq4") if which == "qk" else (mb - 1, "sv")
                                t.wait_ge(cs.sem, QMK.get(key) or QMK[(mb - 1, "sev1")])
                        else:
                            t.wait_ge(cs.sem, QMK[(mb, "sev0")])
                        for c4 in range(4):
                            ct.inc(t.transpose(ps_tr4[:, c4 * 128:(c4 + 1) * 128],
                                               xs[:, (g4 * 4 + c4) * 128:(g4 * 4 + c4 + 1) * 128], id_sb[:]))
                        QMK[(mb, "ttr")] = ct.n
                    p.Tn(ttr)

                    def sev(s, g4=g4, mb=mb):
                        s.wait_ge(ct.sem, QMK[(mb, "ttr")])
                        cs.inc(s.copy(out=xsT[:, g4 * 4:(g4 + 1) * 4, :].rearrange("p a c -> p (a c)"),
                                      in_=ps_tr4[:, 0:512]))
                        QMK[(mb, "sev" + str(g4))] = cs.n
                    p.S(sev)

                def tmm(t, which=which, mb=mb):
                    t.wait_ge(cs.sem, QMK[(mb, "sev1")])
                    prevk = (mb - 1, "sqk" if which == "qk" else "sv")
                    if prevk in QMK:
                        t.wait_ge(cs.sem, QMK[prevk])
                    if mb == 0:
                        cg.wait(t)
                    nj = 4 if which == "qk" else 2
                    for c in range(8):
                        for j in range(nj):
                            ct.inc(t.matmul(ps_b[:, j * 512:(j + 1) * 512],
                                            xsT[:, c, :], wq_sb[:, c, j * 512:(j + 1) * 512],
                                            start=(c == 0), stop=(c == 7)))
                    QMK[(mb, "tmm")] = ct.n
                p.Tn(tmm)

                if which == "v":
                    def sv(s, mb=mb):
                        s.wait_ge(ct.sem, QMK[(mb, "tmm")])
                        cs.inc(s.copy(out=vr[:, mb, :], in_=ps_b[:, 0:1024]))
                        QMK[(mb, "sv")] = cs.n
                    p.S(sv)
                    continue

                def sqk(s, mb=mb):
                    s.wait_ge(ct.sem, QMK[(mb, "tmm")])
                    cv.wait(s)
                    s.copy(out=qk[:, 0:1024], in_=ps_b[:, 0:1024])
                    cs.inc(s.copy(out=qk[:, 1024:2048], in_=ps_b[:, 1024:2048]))
                    QMK[(mb, "sqk")] = cs.n
                p.S(sqk)

                def sl2(s):
                    cs.wait(s)
                    s.activation(out=tmpa[:], in_=qk[:, 0:1024], func=AF.Square)
                    cs.inc(s.activation(out=tmpb[:], in_=qk[:, 1024:2048], func=AF.Square))
                p.S(sl2)

                def vl2(v):
                    cs.wait(v)
                    cv.chain(v, v.reduce_sum(ss2[:, 0:16], tmpa[:].rearrange("p (h e) -> p h e", e=64), axis=AX))
                    cv.chain(v, v.reduce_sum(ss2[:, 16:32], tmpb[:].rearrange("p (h e) -> p h e", e=64), axis=AX))
                    cv.inc(v.tensor_scalar_max(ss2[:], ss2[:], 1e-24))
                p.V(vl2)

                def sl3(s):
                    cv.wait(s)
                    s.activation(out=ss2[:], in_=ss2[:], func=AF.Ln)
                    cs.inc(s.activation(out=rn2[:], in_=ss2[:], func=AF.Exp, scale=-0.5))
                p.S(sl3)

                def vrot(v, mb=mb):
                    cs.wait(v)
                    rotp = rotb[:, mb % 2]
                    cv.chain(v, v.tensor_tensor(
                        qk[:, 0:1024].rearrange("p (h e) -> p h e", e=64),
                        qk[:, 0:1024].rearrange("p (h e) -> p h e", e=64),
                        rn2[:, 0:16].unsqueeze(2).to_broadcast([128, 16, 64]), ALU.mult))
                    cv.chain(v, v.tensor_tensor(qk[:, 0:1024], qk[:, 0:1024], qs_sb[:], ALU.mult))
                    cv.chain(v, v.tensor_tensor(
                        qk[:, 1024:2048].rearrange("p (h e) -> p h e", e=64),
                        qk[:, 1024:2048].rearrange("p (h e) -> p h e", e=64),
                        rn2[:, 16:32].unsqueeze(2).to_broadcast([128, 16, 64]), ALU.mult))
                    cv.chain(v, v.tensor_tensor(qk[:, 1024:2048], qk[:, 1024:2048], ks_sb[:], ALU.mult))
                    for base, ci, si_ in ((0, 0, 1), (1024, 2, 3)):
                        qv = qk[:, base:base + 1024].rearrange("p (h e) -> p h e", e=64)
                        ta = tmpa[:].rearrange("p (h e) -> p h e", e=64)
                        tb_ = tmpb[:].rearrange("p (h e) -> p h e", e=64)
                        cv.chain(v, v.tensor_tensor(
                            ta, qv, rotp[:, ci, :].unsqueeze(1).to_broadcast([128, 16, 64]), ALU.mult))
                        cv.chain(v, v.tensor_tensor(
                            tb_[:, :, 0:32], qv[:, :, 32:64],
                            rotp[:, si_, 0:32].unsqueeze(1).to_broadcast([128, 16, 32]), ALU.mult))
                        cv.chain(v, v.tensor_tensor(
                            tb_[:, :, 32:64], qv[:, :, 0:32],
                            rotp[:, si_, 32:64].unsqueeze(1).to_broadcast([128, 16, 32]), ALU.mult))
                        cv.chain(v, v.tensor_tensor(ta[:, :, 0:32], ta[:, :, 0:32], tb_[:, :, 0:32], ALU.subtract))
                        cv.chain(v, v.tensor_tensor(ta[:, :, 32:64], ta[:, :, 32:64], tb_[:, :, 32:64], ALU.add))
                        cv.chain(v, v.tensor_copy(qk[:, base:base + 1024], tmpa[:]))
                p.V(vrot)

                for g4 in range(4):
                    def tq_(t, g4=g4):
                        cv.wait(t)
                        cs.wait(t)
                        for i in range(4):
                            c = g4 * 4 + i
                            ct.inc(t.transpose(ps_tr4[:, i * 128:(i + 1) * 128],
                                               qk[:, c * 128:(c + 1) * 128], id_sb[:]))
                    p.Tn(tq_)

                    def sq_(s, mb=mb, g4=g4):
                        ct.wait(s)
                        if g4 < 2:
                            if mb >= 2:
                                cs.inc(s.copy(
                                    out=qT2[:, g4 * 4:(g4 + 1) * 4, (mb - 2) * 128:(mb - 1) * 128],
                                    in_=ps_tr4[:, 0:512].rearrange("p (a c) -> p a c", c=128)))
                            else:
                                cs.inc(s.nop())
                        else:
                            cs.inc(s.copy(
                                out=kT2[:, (g4 - 2) * 4:(g4 - 1) * 4, mb * 128:(mb + 1) * 128],
                                in_=ps_tr4[:, 0:512].rearrange("p (a c) -> p a c", c=128)))
                        if g4 == 3:
                            QMK[(mb, "sq4")] = cs.n
                    p.S(sq_)

        MK = {}
        it = 0
        for w in range(8):
            for h in range(16):
                hp = (h % 2) * 64
                hc = h // 2
                for qt in range(2):
                    qq0 = w * 256 + qt * 128
                    pb_ = it % 2
                    def ts_(t, w=w, hp=hp, hc=hc, qq0=qq0, pb_=pb_, it=it):
                        if it == 0:
                            cs.wait(t)
                            cv.wait(t)
                        if it - 2 in MK:
                            t.wait_ge(cv.sem, MK[it - 2]["vms"])
                        ct.inc(t.matmul(ps_b[:, pb_ * 512:(pb_ + 1) * 512],
                                        qT2[hp:hp + 64, hc, qq0:qq0 + 128],
                                        kT2[hp:hp + 64, hc, w * 256:w * 256 + 512],
                                        start=True, stop=True))
                        MK.setdefault(it, {})["ts"] = ct.n
                    p.Tn(ts_)

                    def vms(v, w=w, qt=qt, pb_=pb_, it=it):
                        v.wait_ge(ct.sem, MK[it]["ts"])
                        cv.chain(v, v.tensor_scalar(prow2[:, pb_, :], ps_b[:, pb_ * 512:(pb_ + 1) * 512],
                                                    8.0, None, ALU.mult))
                        msk = mf_sb if w == 0 else mr_sb
                        cv.chain(v, v.tensor_tensor(prow2[:, pb_, :], prow2[:, pb_, :], msk[:, qt, :], ALU.add))
                        cv.inc(v.reduce_max(nmx2[:, pb_:pb_ + 1], prow2[:, pb_, :], axis=AX, negate=True))
                        MK[it]["vms"] = cv.n
                    p.V(vms)

                    def sxp(s, pb_=pb_, it=it):
                        s.wait_ge(cv.sem, MK[it]["vms"])
                        cs.inc(s.activation(out=prow2[:, pb_, :], in_=prow2[:, pb_, :], func=AF.Exp,
                                            bias=nmx2[:, pb_:pb_ + 1], accum_out=ssum2[:, pb_:pb_ + 1]))
                        MK[it]["sxp"] = cs.n
                    p.S(sxp)

                    def vnp(v, pb_=pb_, it=it):
                        v.wait_ge(cs.sem, MK[it]["sxp"])
                        cv.chain(v, v.reciprocal(ssum2[:, pb_:pb_ + 1], ssum2[:, pb_:pb_ + 1]))
                        cv.inc(v.tensor_scalar(prow2[:, pb_, :], prow2[:, pb_, :],
                                               ssum2[:, pb_:pb_ + 1], None, ALU.mult))
                        MK[it]["vnp"] = cv.n
                    p.V(vnp)

                    def tpt(t, pb_=pb_, it=it):
                        t.wait_ge(cv.sem, MK[it]["vnp"])
                        if it - 2 in MK:
                            t.wait_ge(cs.sem, MK[it - 2]["spt"])
                        for kb in range(4):
                            ct.inc(t.transpose(ps_tr4[:, pb_ * 512 + kb * 128:pb_ * 512 + (kb + 1) * 128],
                                               prow2[:, pb_, kb * 128:(kb + 1) * 128], id_sb[:]))
                        MK[it]["tpt"] = ct.n
                    p.Tn(tpt)

                    def spt(s, pb_=pb_, it=it):
                        s.wait_ge(ct.sem, MK[it]["tpt"])
                        cs.inc(s.copy(out=ptb2[:, pb_, :, :].rearrange("p a c -> p (a c)"),
                                      in_=ps_tr4[:, pb_ * 512:(pb_ + 1) * 512]))
                        MK[it]["spt"] = cs.n
                    p.S(spt)

                    def tav(t, w=w, h=h, pb_=pb_, it=it):
                        t.wait_ge(cs.sem, MK[it]["spt"])
                        for kb in range(4):
                            ct.inc(t.matmul(ps_av[:, pb_ * 64:(pb_ + 1) * 64], ptb2[:, pb_, kb, :],
                                            vr[:, w * 2 + kb, h * 64:(h + 1) * 64],
                                            start=(kb == 0), stop=(kb == 3)))
                        MK[it]["tav"] = ct.n
                    p.Tn(tav)

                    def sav(s, pb_=pb_, it=it):
                        s.wait_ge(ct.sem, MK[it]["tav"])
                        cs.inc(s.copy(out=avf2[:, pb_, :], in_=ps_av[:, pb_ * 64:(pb_ + 1) * 64]))
                        MK[it]["sav"] = cs.n
                    p.S(sav)

                    def tavT(t, pb_=pb_, it=it):
                        t.wait_ge(cs.sem, MK[it]["sav"])
                        ct.inc(t.transpose(ps_av[0:64, 256 + pb_ * 128:256 + (pb_ + 1) * 128],
                                           avf2[:, pb_, :], id_sb[:]))
                        MK[it]["tavT"] = ct.n
                    p.Tn(tavT)

                    def savT(s, h=h, hp=hp, hc=hc, qt=qt, pb_=pb_, it=it):
                        s.wait_ge(ct.sem, MK[it]["tavT"])
                        cs.inc(s.copy(out=avT[hp:hp + 64, hc, qt * 128:(qt + 1) * 128],
                                      in_=ps_av[0:64, 256 + pb_ * 128:256 + (pb_ + 1) * 128]))
                        MK[it]["savT"] = cs.n
                    p.S(savT)
                    it += 1

            for qt in range(2):
                def yst(y, w=w, qt=qt):
                    cv.wait(y)
                    cst.wait(y)
                    r0 = 257 + w * 256 + qt * 128
                    cl.inc(y.dma_start(out=strow[:], in_=st_in[r0:r0 + 128, :]), 16)
                p.Y(yst)

                def two(t, w=w, qt=qt):
                    cs.wait(t)
                    cg.wait(t)
                    for c in range(8):
                        for j in range(2):
                            ct.inc(t.matmul(ps_b[:, 1024 + j * 512:1024 + (j + 1) * 512],
                                            avT[:, c, qt * 128:(qt + 1) * 128],
                                            wo_sb[:, c, j * 512:(j + 1) * 512],
                                            start=(c == 0), stop=(c == 7)))
                p.Tn(two)

                def sor(s):
                    ct.wait(s)
                    cs.inc(s.copy(out=orow[:], in_=ps_b[:, 1024:2048]))
                p.S(sor)

                def vor(v):
                    cs.wait(v)
                    cl.wait(v)
                    cv.inc(v.tensor_tensor(orow[:], orow[:], strow[:], ALU.add))
                p.V(vor)

                def yo2(y, w=w, qt=qt):
                    cv.wait(y)
                    cst.inc(y.dma_start(out=st_out[(w * 256 + qt * 128):(w * 256 + qt * 128 + 128), :],
                                        in_=orow[:]), 16)
                p.Y(yo2)

        def yfin(y):
            cst.wait(y)
            cl.wait(y)
            cg.wait(y)
        p.Y(yfin)
    return p


def build_l1_ffn(parent=None, io=None, tag="Ff_"):
    p = Prog(parent, io, tag)
    st_in = p.inp("st_in", (TH + 1, D), F32)
    W1_in = p.inp("W1_in", (D, 5460), F32)
    W2_in = p.inp("W2_in", (2816, D), F32)
    b1_t = p.inp("b1_t", (128, 5460), F32)
    b2_t = p.inp("b2_t", (128, D), F32)
    g_t = p.inp("g_t", (128, D), F32)
    bl_t = p.inp("bl_t", (128, D), F32)
    ident = p.inp("ident", (128, 128), F32)
    st_out = p.out("st_out", (TH, D), F32)

    with p.es:
        id_sb = p.sb("id_sb", (128, 128), F32)
        xs = p.sb("xs", (128, D), F32)
        xr = p.sb("xr", (128, D), F32)
        tmp = p.sb("tmp", (128, D), F32)
        mc = p.sb("mc", (128, 1), F32)
        rc = p.sb("rc", (128, 1), F32)
        xnT = p.sb("xnT", (128, 8, 128), BF16)
        w1a = p.sb("w1a", (128, 8, 2730), BF16)
        w1g = p.sb("w1g", (128, 8, 2730), BF16)
        w2_sb = p.sb("w2_sb", (128, 22, D), BF16)
        b1_sb = p.sb("b1_sb", (128, 5460), BF16)
        b2_sb = p.sb("b2_sb", (128, D), F32)
        g_sb = p.sb("g_sb", (128, D), F32)
        bl_sb = p.sb("bl_sb", (128, D), F32)
        h2 = p.sb("h2", (128, 2816), F32)
        apad = p.sb("apad", (128, 1365), F32)
        gat = p.sb("gat", (128, 1365), F32)
        h2T = p.sb("h2T", (128, 22, 128), BF16)
        orow = p.sb("orow", (128, D), F32)

        ps_a = p.ps("ps_a", (128, 1408))
        ps_g = p.ps("ps_g", (128, 1408))
        ps_tr = p.ps("ps_tr", (128, 512))

        cl = p.ctr("cl")
        cg = p.ctr("cg")
        cv = p.ctr("cv")
        cs = p.ctr("cs")
        ct = p.ctr("ct")
        cst = p.ctr("cst")
        p.start()

        def y0(y):
            cl.inc(y.dma_start(out=id_sb[:], in_=ident), 16)
            cl.inc(y.dma_start(out=b2_sb[:], in_=b2_t), 16)
            cl.inc(y.dma_start(out=g_sb[:], in_=g_t), 16)
            cl.inc(y.dma_start(out=bl_sb[:], in_=bl_t), 16)
        p.Y(y0)

        def g0(g):
            cg.inc(g.dma_start(out=b1_sb[:], in_=b1_t), 16)
            for c in range(22):
                cg.inc(g.dma_start(out=w2_sb[:, c, :], in_=W2_in[c * 128:(c + 1) * 128, :]), 16)
            for c in range(8):
                cg.inc(g.dma_start(out=w1a[:, c, :], in_=W1_in[c * 128:(c + 1) * 128, 0:2730]), 16)
                cg.inc(g.dma_start(out=w1g[:, c, :], in_=W1_in[c * 128:(c + 1) * 128, 2730:5460]), 16)
        p.G(g0)

        FMK = {}
        for mb in range(NTB):
            def yx(y, mb=mb):
                cv.wait(y)
                ct.wait(y)
                cst.wait(y)
                cl.inc(y.dma_start(out=xs[:, 0:512],
                                   in_=st_in[mb * 128 + 1:mb * 128 + 129, 0:512]), 16)
                cl.inc(y.dma_start(out=xs[:, 512:1024],
                                   in_=st_in[mb * 128:mb * 128 + 128, 512:1024]), 16)
                cl.inc(y.dma_start(out=xr[:], in_=st_in[mb * 128 + 1:mb * 128 + 129, :]), 16)
            p.Y(yx)

            def vw0(v):
                cl.wait(v)
                cv.inc(v.nop())
            p.V(vw0)
            ln_emit(p, xs[:], xs[:], tmp[:], mc[:], rc[:], cv, cs)

            def vgb(v):
                cs.wait(v)
                cv.chain(v, v.tensor_tensor(xs[:], xs[:], g_sb[:], ALU.mult))
                cv.inc(v.tensor_tensor(xs[:], xs[:], bl_sb[:], ALU.add))
            p.V(vgb)

            for g4 in range(2):
                def ttr(t, g4=g4, mb=mb):
                    cv.wait(t)
                    cs.wait(t)
                    for c4 in range(4):
                        ct.inc(t.transpose(ps_tr[:, c4 * 128:(c4 + 1) * 128],
                                           xs[:, (g4 * 4 + c4) * 128:(g4 * 4 + c4 + 1) * 128], id_sb[:]))
                    FMK[(mb, "ttr")] = ct.n
                p.Tn(ttr)

                def sev(s, g4=g4, mb=mb):
                    s.wait_ge(ct.sem, FMK[(mb, "ttr")])
                    cs.inc(s.copy(out=xnT[:, g4 * 4:(g4 + 1) * 4, :].rearrange("p a c -> p (a c)"),
                                  in_=ps_tr[:]))
                    FMK[(mb, "sev")] = cs.n
                p.S(sev)

            for half in range(2):
                def tw1(t, half=half, mb=mb):
                    t.wait_ge(cs.sem, FMK[(mb, "sev")])
                    if half == 0 and (mb - 1, "so2") in FMK:
                        t.wait_ge(cs.sem, FMK[(mb - 1, "so2")])
                    if mb == 0 and half == 0:
                        cg.wait(t)
                    vkey = (mb, 0, "vba") if half == 1 else (mb - 1, 1, "vba")
                    if vkey in FMK:
                        t.wait_ge(cv.sem, FMK[vkey])
                    for c in range(8):
                        for j in range(3):
                            n0, n1 = j * 512, min((j + 1) * 512, 1365)
                            ct.inc(t.matmul(ps_a[:, n0:n1], xnT[:, c, :],
                                            w1a[:, c, half * 1365 + n0:half * 1365 + n1],
                                            start=(c == 0), stop=(c == 7)))
                            ct.inc(t.matmul(ps_g[:, n0:n1], xnT[:, c, :],
                                            w1g[:, c, half * 1365 + n0:half * 1365 + n1],
                                            start=(c == 0), stop=(c == 7)))
                p.Tn(tw1)

                def vba(v, half=half, mb=mb):
                    ct.wait(v)
                    cs.wait(v)
                    cg.wait(v)
                    FMK[(mb, half, "vba_pre")] = 1
                    cv.chain(v, v.tensor_tensor(apad[:], ps_a[:, 0:1365],
                                                b1_sb[:, half * 1365:(half + 1) * 1365], ALU.add))
                    cv.inc(v.tensor_tensor(gat[:], ps_g[:, 0:1365],
                                           b1_sb[:, 2730 + half * 1365:2730 + (half + 1) * 1365], ALU.add))
                    FMK[(mb, half, "vba")] = cv.n
                p.V(vba)

                def sge(s, half=half):
                    cv.wait(s)
                    cs.inc(s.activation(out=gat[:], in_=gat[:], func=AF.Gelu))
                p.S(sge)

                def vh2(v, half=half):
                    cs.wait(v)
                    cv.inc(v.tensor_tensor(h2[:, half * 1365:(half + 1) * 1365],
                                           apad[:], gat[:], ALU.mult))
                p.V(vh2)

            def vpad(v):
                cv.inc(v.memset(h2[:, 2730:2816], 0.0))
            p.V(vpad)

            for g4 in range(6):
                nt = 4 if g4 < 5 else 2
                def tht(t, g4=g4, nt=nt):
                    cv.wait(t)
                    cs.wait(t)
                    for c4 in range(nt):
                        ct.inc(t.transpose(ps_tr[:, c4 * 128:(c4 + 1) * 128],
                                           h2[:, (g4 * 4 + c4) * 128:(g4 * 4 + c4 + 1) * 128], id_sb[:]))
                p.Tn(tht)

                def sht(s, g4=g4, nt=nt):
                    ct.wait(s)
                    cs.inc(s.copy(out=h2T[:, g4 * 4:g4 * 4 + nt, :].rearrange("p a c -> p (a c)"),
                                  in_=ps_tr[:, 0:nt * 128]))
                p.S(sht)

            def tw2(t):
                cs.wait(t)
                for c in range(22):
                    for j in range(2):
                        ct.inc(t.matmul(ps_a[:, j * 512:(j + 1) * 512],
                                        h2T[:, c, :], w2_sb[:, c, j * 512:(j + 1) * 512],
                                        start=(c == 0), stop=(c == 21)))
            p.Tn(tw2)

            def so2(s, mb=mb):
                ct.wait(s)
                cs.inc(s.copy(out=orow[:], in_=ps_a[:, 0:1024]))
                FMK[(mb, "so2")] = cs.n
            p.S(so2)

            def vo2(v):
                cs.wait(v)
                cv.chain(v, v.tensor_tensor(orow[:], orow[:], b2_sb[:], ALU.add))
                cv.inc(v.tensor_tensor(orow[:], orow[:], xr[:], ALU.add))
            p.V(vo2)

            def yo2(y, mb=mb):
                cv.wait(y)
                cst.inc(y.dma_start(out=st_out[mb * 128:(mb + 1) * 128, :], in_=orow[:]), 16)
            p.Y(yo2)

        def yfin(y):
            cst.wait(y)
            cl.wait(y)
            cg.wait(y)
        p.Y(yfin)
    return p


def _rotary_tables():
    half = np.arange(0, DH, 2, dtype=np.float64)
    inv_freq = 1.0 / 10000.0 ** (half / DH)
    t = np.arange(T, dtype=np.float64)
    freqs = t[:, None] * inv_freq[None, :]
    freqs = np.concatenate([freqs, freqs], -1)
    base_scale = (half + 0.4 * DH) / (1.4 * DH)
    power = (t - T // 2) / 128.0
    scale = base_scale[None, :] ** power[:, None]
    scale = np.concatenate([scale, scale], -1)
    cos, sin = np.cos(freqs), np.sin(freqs)
    cq = (cos * scale).astype(np.float32)
    sq = (sin * scale).astype(np.float32)
    ck = (cos / scale).astype(np.float32)
    sk = (sin / scale).astype(np.float32)
    return cq, sq, ck, sk


def _attn_masks(first_window_blocked):
    m = np.zeros((128, 2, 512), np.float32)
    for qt in range(2):
        for pi in range(128):
            i = qt * 128 + pi
            for j in range(512):
                if not (i <= j <= i + 256):
                    m[pi, qt, j] = -3e38
    mf = m.copy()
    if first_window_blocked:
        mf[:, :, 0:256] = -3e38
    return mf, m


_MASK_CACHE = {}


def run_l1_attn(inputs, st, layer):
    p = build_l1_attn()
    ident = np.eye(128, dtype=np.float32)
    cq, sq, ck, sk = _rotary_tables()
    if "m" not in _MASK_CACHE:
        _MASK_CACHE["m0"] = _attn_masks(True)
        _MASK_CACHE["m1"] = _attn_masks(False)
        _MASK_CACHE["m"] = 1
    qs = np.tile(inputs["L1_ffn_q_scale"][layer], 16)
    ks = np.tile(inputs["L1_ffn_k_scale"][layer], 16)
    in_maps = []
    for (b, hh) in _core_slices():
        own0 = hh * TH
        sti = np.zeros((NPAD, D), np.float32)
        lo = own0 - 257
        src_lo = max(lo, 0)
        sti[src_lo - lo:257 + TH] = st[b, src_lo:own0 + TH]
        pos = np.arange(own0 - 256, own0 + 2048)
        posc = np.clip(pos, 0, T - 1)
        mf, mr = _MASK_CACHE["m0" if hh == 0 else "m1"]
        in_maps.append({
            "st_in": sti,
            "Wqkv_in": np.ascontiguousarray(inputs["L1_ffn_Wqkv"][layer]),
            "Wo_in": np.ascontiguousarray(inputs["L1_ffn_Wo"][layer]),
            "qs_t": np.tile(qs[None, :], (128, 1)).astype(np.float32),
            "ks_t": np.tile(ks[None, :], (128, 1)).astype(np.float32),
            "cq_t": np.ascontiguousarray(cq[posc]),
            "sq_t": np.ascontiguousarray(sq[posc]),
            "ck_t": np.ascontiguousarray(ck[posc]),
            "sk_t": np.ascontiguousarray(sk[posc]),
            "m_first": mf, "m_rest": mr,
            "ident": ident,
        })
    res = run_bass_kernel_spmd(p.nc, in_maps, core_ids=list(range(8)))
    out = np.zeros((4, T, D), np.float32)
    for ci, (b, hh) in enumerate(_core_slices()):
        out[b, hh * TH:(hh + 1) * TH] = res.results[ci]["st_out"]
    return out


def run_l1_ffn(inputs, st, layer):
    p = build_l1_ffn()
    ident = np.eye(128, dtype=np.float32)
    W2p = np.zeros((2816, D), np.float32)
    W2p[0:2730] = inputs["L1_ffn_W2"][layer]
    tile = lambda vv: np.tile(np.asarray(vv, np.float32)[None, :], (128, 1))
    in_maps = []
    for (b, hh) in _core_slices():
        own0 = hh * TH
        sti = np.zeros((TH + 1, D), np.float32)
        lo = own0 - 1
        src_lo = max(lo, 0)
        sti[src_lo - lo:] = st[b, src_lo:own0 + TH]
        in_maps.append({
            "st_in": sti,
            "W1_in": np.ascontiguousarray(inputs["L1_ffn_W1"][layer]),
            "W2_in": W2p,
            "b1_t": tile(inputs["L1_ffn_b1"][layer]),
            "b2_t": tile(inputs["L1_ffn_b2"][layer]),
            "g_t": tile(inputs["L1_ffn_ln_g"][layer]),
            "bl_t": tile(inputs["L1_ffn_ln_b"][layer]),
            "ident": ident,
        })
    res = run_bass_kernel_spmd(p.nc, in_maps, core_ids=list(range(8)))
    out = np.zeros((4, T, D), np.float32)
    for ci, (b, hh) in enumerate(_core_slices()):
        out[b, hh * TH:(hh + 1) * TH] = res.results[ci]["st_out"]
    return out




def build_combine(parent=None, io=None, tag="G_"):
    p = Prog(parent, io, tag)
    x1_in = p.inp("x1_in", (TH, D), F32)
    s1_in = p.inp("s1_in", (TH, D), F32)
    g1_in = p.inp("g1_in", (TH, D), F32)
    x2_out = p.out("x2_out", (TH, D), F32)
    with p.es:
        a = p.sb("a", (128, D), F32)
        b_ = p.sb("b_", (128, D), F32)
        c_ = p.sb("c_", (128, D), F32)
        cl = p.ctr("cl")
        cv = p.ctr("cv")
        cs = p.ctr("cs")
        cst = p.ctr("cst")
        p.start()
        for mb in range(NTB):
            def yb(y, mb=mb):
                if mb > 0:
                    cst.wait(y)
                    cv.wait(y)
                    cs.wait(y)
                cl.inc(y.dma_start(out=a[:], in_=g1_in[mb * 128:(mb + 1) * 128, :]), 16)
                cl.inc(y.dma_start(out=b_[:], in_=s1_in[mb * 128:(mb + 1) * 128, :]), 16)
                cl.inc(y.dma_start(out=c_[:], in_=x1_in[mb * 128:(mb + 1) * 128, :]), 16)
            p.Y(yb)

            def sg(s, mb=mb):
                cl.wait(s)
                cv.wait(s)
                cs.inc(s.activation(out=a[:], in_=a[:], func=AF.Silu))
            p.S(sg)

            def vc(v, mb=mb):
                cs.wait(v)
                cv.chain(v, v.tensor_tensor(a[:], a[:], b_[:], ALU.mult))
                cv.inc(v.tensor_tensor(a[:], a[:], c_[:], ALU.add))
            p.V(vc)

            def yo(y, mb=mb):
                cv.wait(y)
                cst.inc(y.dma_start(out=x2_out[mb * 128:(mb + 1) * 128, :], in_=a[:]), 16)
            p.Y(yo)
        def yfin(y):
            cst.wait(y)
        p.Y(yfin)
    return p


def run_combine(x1, s1, g1):
    p = build_combine()
    in_maps = []
    for (b, hh) in _core_slices():
        in_maps.append({
            "x1_in": np.ascontiguousarray(x1[b, hh * TH:(hh + 1) * TH]),
            "s1_in": np.ascontiguousarray(s1[b, hh * TH:(hh + 1) * TH]),
            "g1_in": np.ascontiguousarray(g1[b, hh * TH:(hh + 1) * TH]),
        })
    res = run_bass_kernel_spmd(p.nc, in_maps, core_ids=list(range(8)))
    x2 = np.zeros((4, T, D), np.float32)
    for ci, (b, hh) in enumerate(_core_slices()):
        x2[b, hh * TH:(hh + 1) * TH] = res.results[ci]["x2_out"]
    return x2


def build_ln_only(parent=None, io=None, tag="L_"):
    p = Prog(parent, io, tag)
    x_in = p.inp("x_in", (TH, D), F32)
    xn_out = p.out("xn_out", (TH, D), F32)
    with p.es:
        xt = p.sb("xt", (128, D), F32)
        xo = p.sb("xo", (128, D), F32)
        tmp = p.sb("tmp", (128, D), F32)
        mc = p.sb("mc", (128, 1), F32)
        rc = p.sb("rc", (128, 1), F32)
        cl = p.ctr("cl")
        cv = p.ctr("cv")
        cs = p.ctr("cs")
        cst = p.ctr("cst")
        p.start()
        for mb in range(NTB):
            def yb(y, mb=mb):
                if mb > 0:
                    cst.wait(y)
                    cv.wait(y)
                    cs.wait(y)
                cl.inc(y.dma_start(out=xt[:], in_=x_in[mb * 128:(mb + 1) * 128, :]), 16)
            p.Y(yb)
            def vv(v):
                cl.wait(v)
                cv.inc(v.nop())
            p.V(vv)
            ln_emit(p, xt[:], xo[:], tmp[:], mc[:], rc[:], cv, cs)
            def yo(y, mb=mb):
                cv.wait(y)
                cst.inc(y.dma_start(out=xn_out[mb * 128:(mb + 1) * 128, :], in_=xo[:]), 16)
            p.Y(yo)
        def yfin(y):
            cst.wait(y)
        p.Y(yfin)
    return p


def run_ln(x1):
    p = build_ln_only()
    in_maps = []
    for (b, hh) in _core_slices():
        in_maps.append({"x_in": np.ascontiguousarray(x1[b, hh * TH:(hh + 1) * TH])})
    res = run_bass_kernel_spmd(p.nc, in_maps, core_ids=list(range(8)))
    out = np.zeros((4, T, D), np.float32)
    for ci, (b, hh) in enumerate(_core_slices()):
        out[b, hh * TH:(hh + 1) * TH] = res.results[ci]["xn_out"]
    return out



def emit_x1prev(master, x1_d, x1p_d, role_col, tag="XP_"):
    p = Prog(master, {}, tag)
    agi = p.dram("agi", (1, D), F32)
    ago = p.dram("ago", (2, D), F32)
    with p.es:
        row = p.sb("row", (1, D), F32)
        rcol = p.sb("rcol", (1, 1), F32)
        cg = p.ctr("cg")
        cl = p.ctr("cl")
        cv = p.ctr("cv")
        ccx = Counter(p, tag + "cc")

        def g(g_):
            cg.inc(g_.dma_start(out=x1p_d[1:TH, :], in_=x1_d[0:TH - 1, :]), 16)
            cg.inc(g_.dma_start(out=agi, in_=x1_d[TH - 1:TH, :]), 16)
            cg.wait(g_)
            g_.collective_compute("AllGather", ALU.bypass,
                                  replica_groups=[[0, 1], [2, 3], [4, 5], [6, 7]],
                                  ins=[agi], outs=[ago]).then_inc(ccx.sem, 1)
            g_.wait_ge(ccx.sem, 1)
            cg.inc(g_.dma_start(out=row[:], in_=ago[0:1, :]), 16)
        p.G(g)

        def y(y_):
            cl.inc(y_.dma_start(out=rcol[:], in_=role_col[0:1, 0:1]), 16)
        p.Y(y)

        def v(v_):
            cg.wait(v_)
            cl.wait(v_)
            cv.inc(v_.tensor_scalar(row[:], row[:], rcol[:], None, ALU.mult))
        p.V(v)

        def g2(g_):
            cv.wait(g_)
            cg.inc(g_.dma_start(out=x1p_d[0:1, :], in_=row[:]), 16)
        p.G(g2)


def emit_halo(master, st_d, role_col, tag):
    p = Prog(master, {}, tag)
    agi = p.dram("agi", (257, D), F32)
    ago = p.dram("ago", (514, D), F32)
    with p.es:
        tile = p.sb("tile", (128, D), F32)
        rcol = p.sb("rcol", (128, 1), F32)
        cg = p.ctr("cg")
        cl = p.ctr("cl")
        cv = p.ctr("cv")
        cs = p.ctr("cs")
        ccx = Counter(p, tag + "cc")

        def y(y_):
            cl.inc(y_.dma_start(out=rcol[:], in_=role_col), 16)
        p.Y(y)

        def g(g_):
            cg.inc(g_.dma_start(out=agi, in_=st_d[2048:2305, :]), 16)
            cg.wait(g_)
            g_.collective_compute("AllGather", ALU.bypass,
                                  replica_groups=[[0, 1], [2, 3], [4, 5], [6, 7]],
                                  ins=[agi], outs=[ago]).then_inc(ccx.sem, 1)
            g_.wait_ge(ccx.sem, 1)
            cg.inc(g_.nop())
        p.G(g)

        for i, (r0, r1) in enumerate(((0, 128), (128, 256), (256, 257))):
            n = r1 - r0
            def gl(g_, r0=r0, n=n):
                cg.wait(g_)
                cv.wait(g_)
                cg.inc(g_.dma_start(out=tile[0:n, :], in_=ago[r0:r0 + n, :]), 16)
            p.G(gl)

            def v(v_, n=n):
                cg.wait(v_)
                cl.wait(v_)
                cv.inc(v_.tensor_scalar(tile[0:n, :], tile[0:n, :], rcol[0:n, :], None, ALU.mult))
            p.V(v)

            def g2(g_, r0=r0, n=n):
                cv.wait(g_)
                cg.inc(g_.dma_start(out=st_d[r0:r0 + n, :], in_=tile[0:n, :]), 16)
            p.G(g2)


def kernel(**inputs):
    x = np.asarray(inputs["L0_attn_x"])
    master = Prog()
    nc = master.nc

    def mi(name, shape):
        return master.inp(name, shape, F32)
    xb = mi("xb", (T, D))
    WqT = mi("WqT", (128, DH)); WkT = mi("WkT", (128, DH)); WvT = mi("WvT", (128, DH))
    WoT = mi("WoT", (D, D))
    qposb = mi("qposb", (128, TH)); kposc = mi("kposc", (128, NB))
    ident = mi("ident", (128, 128)); onesc = mi("onesc", (1, 128))
    wqT = mi("wqT", (D, 8192)); gcol = mi("gcol", (128, 8)); gb_in = mi("gb_in", (128, D))
    keysT = mi("keysT", (2, 8, 4, 128, 256))
    iota256 = master.inp("iota256", (128, 256), I32)
    iota16f = mi("iota16f", (128, 16))
    tdn_sl = mi("tdn_sl", (8192, D)); tup_sl = mi("tup_sl", (8192, D))
    WkTc = mi("WkTc", (D, D)); WvTc = mi("WvTc", (D, D)); WrTc = mi("WrTc", (D, D)); WoTc = mi("WoTc", (D, D))
    mixk_b = mi("mixk_b", (128, D)); mixv_b = mi("mixv_b", (128, D)); mixr_b = mi("mixr_b", (128, D))
    td_b = mi("td_b", (128, D)); tf_b = mi("tf_b", (128, D))
    jcol = mi("jcol", (128, 1)); tprefix = mi("tprefix", (128, 128)); rolem = mi("rolem", (1, D))
    role_col = mi("role_col", (128, 1))
    qs_t = [mi(f"qs_t{i}", (128, D)) for i in range(6)]
    ks_t = [mi(f"ks_t{i}", (128, D)) for i in range(6)]
    cq_t = mi("cq_t", (NKB * 128, DH)); sq_t = mi("sq_t", (NKB * 128, DH))
    ck_t = mi("ck_t", (NKB * 128, DH)); sk_t = mi("sk_t", (NKB * 128, DH))
    m_first = mi("m_first", (128, 2, 512)); m_rest = mi("m_rest", (128, 2, 512))
    Wqkv_i = [mi(f"Wqkv_{i}", (D, 3072)) for i in range(6)]
    Wo_i = [mi(f"Wol_{i}", (D, D)) for i in range(6)]
    W1_i = [mi(f"W1_{i}", (D, 5460)) for i in range(6)]
    W2_i = [mi(f"W2_{i}", (2816, D)) for i in range(6)]
    b1_i = [mi(f"b1_{i}", (128, 5460)) for i in range(6)]
    b2_i = [mi(f"b2_{i}", (128, D)) for i in range(6)]
    g_i = [mi(f"g_{i}", (128, D)) for i in range(6)]
    bl_i = [mi(f"bl_{i}", (128, D)) for i in range(6)]
    x2_out = master.out("x2_out", (TH, D), F32)

    s0_d = master.dram("s0_d", (TH, D), F32)
    xn_d = master.dram("xn_d", (TH, D), F32)
    idx_d = nc.dram_tensor("idx_d", [TH, 128], I32, kind="Internal").ap()
    sm_d = master.dram("sm_d", (TH, 128), F32)
    xnf_d = master.dram("xnf_d", (TH, D), F32)
    x1_d = master.dram("x1_d", (TH, D), F32)
    x1p_d = master.dram("x1p_d", (TH, D), F32)
    s1_d = master.dram("s1_d", (TH, D), F32)
    stA = master.dram("stA", (NPAD, D), F32)
    stB = master.dram("stB", (NPAD, D), F32)

    import os
    PH = os.environ.get("KPHASES", "BCDXELHFG")
    with master.es:
        master.start()
        if "B" in PH: build_l0_attn(master, {
            "xb": xb, "WqT": WqT, "WkT": WkT, "WvT": WvT, "WoT": WoT,
            "qposb": qposb, "kposc": kposc, "ident": ident, "onesc": onesc,
            "s_own": s0_d, "xn_own": xn_d})
        master.barrier()
        if "C" in PH: build_l0_ffn_sim(master, {
            "xn_own": xn_d, "wqT": wqT, "gcol": gcol, "gb_in": gb_in, "keysT": keysT,
            "iota256": iota256, "iota16f": iota16f, "ident": ident,
            "idx_all": idx_d, "sm_all": sm_d, "xnf_own": xnf_d})
        master.barrier()
        if "D" in PH: build_l0_ffn_gather(master, {
            "idx_in": idx_d, "sm_in": sm_d, "xnf_in": xnf_d, "s_in": s0_d,
            "x_in": xb[0:TH, :], "tdn_sl": tdn_sl, "tup_sl": tup_sl, "ident": ident,
            "x1_own": x1_d})
        master.barrier()
        if "X" in PH: emit_x1prev(master, x1_d, x1p_d, role_col)
        master.barrier()
        if "E" in PH: build_rwkv(master, {
            "x1_in": x1_d, "x1p_in": x1p_d, "WkTc": WkTc, "WvTc": WvTc, "WrTc": WrTc,
            "WoTc": WoTc, "mixk_b": mixk_b, "mixv_b": mixv_b, "mixr_b": mixr_b,
            "td_b": td_b, "tf_b": tf_b, "jcol": jcol, "tprefix": tprefix,
            "rolem": rolem, "ident": ident, "s1_own": s1_d})
        master.barrier()
        if "L" in PH: build_ln_only(master, {"x_in": x1_d, "xn_out": stA[257:2305, :]})
        master.barrier()
        if "F" in PH: emit_halo(master, stA, role_col, "H0_")
        master.barrier()
        cur, nxt = stA, stB
        for layer in (range(6) if "F" in PH else []):
            build_l1_attn(master, {
                "st_in": cur, "Wqkv_in": Wqkv_i[layer], "Wo_in": Wo_i[layer],
                "qs_t": qs_t[layer], "ks_t": ks_t[layer],
                "cq_t": cq_t, "sq_t": sq_t, "ck_t": ck_t, "sk_t": sk_t,
                "m_first": m_first, "m_rest": m_rest, "ident": ident,
                "st_out": nxt[257:2305, :]}, tag=f"F{layer}a_")
            master.barrier()
            emit_halo(master, nxt, role_col, f"Ha{layer}_")
            master.barrier()
            cur, nxt = nxt, cur
            build_l1_ffn(master, {
                "st_in": cur[256:2305, :], "W1_in": W1_i[layer], "W2_in": W2_i[layer],
                "b1_t": b1_i[layer], "b2_t": b2_i[layer], "g_t": g_i[layer],
                "bl_t": bl_i[layer], "ident": ident,
                "st_out": nxt[257:2305, :]}, tag=f"F{layer}f_")
            master.barrier()
            if layer < 5:
                emit_halo(master, nxt, role_col, f"Hf{layer}_")
                master.barrier()
            cur, nxt = nxt, cur
        if "G" in PH: build_combine(master, {
            "x1_in": x1_d, "s1_in": s1_d, "g1_in": cur[257:2305, :], "x2_out": x2_out})

    identH = np.eye(128, dtype=np.float32)
    onescH = np.ones((1, 128), dtype=np.float32)
    keys = inputs["L0_ffn_keys"]
    keysTH = np.zeros((2, 8, 4, 128, 256), np.float32)
    for half in range(2):
        for h in range(8):
            for i in range(4):
                keysTH[half, h, i] = keys[h, :, half, i * 128:(i + 1) * 128].T
    gamma = inputs["L0_ffn_gamma"]
    cq, sq, ck, sk = _rotary_tables()
    if "m" not in _MASK_CACHE:
        _MASK_CACHE["m0"] = _attn_masks(True)
        _MASK_CACHE["m1"] = _attn_masks(False)
        _MASK_CACHE["m"] = 1
    jcolH = np.arange(128, dtype=np.float32).reshape(128, 1)
    tpreH = np.zeros((128, 128), np.float32)
    tpreH[:, 0] = 1.0
    for j in range(1, 128):
        tpreH[:j, j] = 1.0
    tile = lambda vv: np.tile(np.asarray(vv, np.float32).reshape(1, -1), (128, 1))
    W2pH = [None] * 6
    for i in range(6):
        W2pH[i] = np.zeros((2816, D), np.float32)
        W2pH[i][0:2730] = inputs["L1_ffn_W2"][i]

    in_maps = []
    for ci, (b, hh) in enumerate(_core_slices()):
        own = x[b, hh * TH:(hh + 1) * TH, :]
        other = x[b, (1 - hh) * TH:(1 - hh) * TH + TH, :]
        xbH = np.ascontiguousarray(np.concatenate([own, other], axis=0))
        qpos = np.arange(hh * TH, (hh + 1) * TH, dtype=np.float32)
        rot = np.concatenate([np.arange(hh * TH, (hh + 1) * TH),
                              np.arange((1 - hh) * TH, (1 - hh) * TH + TH)]).astype(np.float32)
        pos = np.arange(hh * TH - 256, hh * TH + 2048)
        posc = np.clip(pos, 0, T - 1)
        mf, mr = _MASK_CACHE["m0" if hh == 0 else "m1"]
        im = {
            "xb": xbH,
            "WqT": np.ascontiguousarray(np.vstack([inputs["L0_attn_Wq"].T] * 2)),
            "WkT": np.ascontiguousarray(np.vstack([inputs["L0_attn_Wk"].T] * 2)),
            "WvT": np.ascontiguousarray(np.vstack([inputs["L0_attn_Wv"].T] * 2)),
            "WoT": np.ascontiguousarray(inputs["L0_attn_Wo"].T),
            "qposb": np.tile(qpos[None, :], (128, 1)),
            "kposc": np.ascontiguousarray(rot.reshape(NB, 128).T),
            "ident": identH, "onesc": onescH,
            "wqT": np.ascontiguousarray(inputs["L0_ffn_wq"].T),
            "gcol": np.ascontiguousarray(gamma.reshape(8, 128).T),
            "gb_in": tile(gamma),
            "keysT": keysTH,
            "iota256": np.tile(np.arange(256, dtype=np.int32)[None, :], (128, 1)),
            "iota16f": np.tile(np.arange(16, dtype=np.float32)[None, :], (128, 1)),
            "tdn_sl": np.ascontiguousarray(inputs["L0_ffn_table_down"][ci * 8192:(ci + 1) * 8192]),
            "tup_sl": np.ascontiguousarray(inputs["L0_ffn_table_up"][ci * 8192:(ci + 1) * 8192]),
            "WkTc": np.ascontiguousarray(inputs["L1_attn_Wk"].T),
            "WvTc": np.ascontiguousarray(inputs["L1_attn_Wv"].T),
            "WrTc": np.ascontiguousarray(inputs["L1_attn_Wr"].T),
            "WoTc": np.ascontiguousarray(inputs["L1_attn_Wo"].T),
            "mixk_b": tile(inputs["L1_attn_mix_k"]),
            "mixv_b": tile(inputs["L1_attn_mix_v"]),
            "mixr_b": tile(inputs["L1_attn_mix_r"]),
            "td_b": tile(inputs["L1_attn_time_decay"]),
            "tf_b": tile(inputs["L1_attn_time_first"]),
            "jcol": jcolH, "tprefix": tpreH,
            "rolem": np.full((1, D), 1.0 if hh == 1 else 0.0, np.float32),
            "role_col": np.full((128, 1), 1.0 if hh == 1 else 0.0, np.float32),
            "cq_t": np.ascontiguousarray(cq[posc]),
            "sq_t": np.ascontiguousarray(sq[posc]),
            "ck_t": np.ascontiguousarray(ck[posc]),
            "sk_t": np.ascontiguousarray(sk[posc]),
            "m_first": mf, "m_rest": mr,
        }
        for i in range(6):
            im[f"qs_t{i}"] = tile(np.tile(inputs["L1_ffn_q_scale"][i], 16))
            im[f"ks_t{i}"] = tile(np.tile(inputs["L1_ffn_k_scale"][i], 16))
            im[f"Wqkv_{i}"] = np.ascontiguousarray(inputs["L1_ffn_Wqkv"][i])
            im[f"Wol_{i}"] = np.ascontiguousarray(inputs["L1_ffn_Wo"][i])
            im[f"W1_{i}"] = np.ascontiguousarray(inputs["L1_ffn_W1"][i])
            im[f"W2_{i}"] = W2pH[i]
            im[f"b1_{i}"] = tile(inputs["L1_ffn_b1"][i])
            im[f"b2_{i}"] = tile(inputs["L1_ffn_b2"][i])
            im[f"g_{i}"] = tile(inputs["L1_ffn_ln_g"][i])
            im[f"bl_{i}"] = tile(inputs["L1_ffn_ln_b"][i])
        in_maps.append(im)

    res = run_bass_kernel_spmd(master.nc, in_maps, core_ids=list(range(8)))
    x2 = np.zeros((4, T, D), np.float32)
    for ci, (b, hh) in enumerate(_core_slices()):
        x2[b, hh * TH:(hh + 1) * TH] = res.results[ci]["x2_out"]
    return x2
